# version 01
# baseline (speedup 1.0000x reference)
import numpy as np
from contextlib import ExitStack
import concourse.bass as bass
import concourse.mybir as mybir

F32 = mybir.dt.float32
BF16 = mybir.dt.bfloat16
U32 = mybir.dt.uint32
I32 = mybir.dt.int32
AF = mybir.ActivationFunctionType
ALU = mybir.AluOpType
AX = mybir.AxisListType
IOA = bass.IndirectOffsetOnAxis

ENGINES = ("sync", "tensor", "vector", "scalar", "gpsimd")


class _Sem:
    def __init__(self, h):
        self.h = h
        self.n = 0


class Prog:
    def __init__(self, nc, ctx):
        self.nc = nc
        self.ctx = ctx
        self.script = []

    def sem(self, name):
        return _Sem(self.ctx.enter_context(self.nc.semaphore(name)))

    def emit(self, eng, fn, log=""):
        self.script.append((eng, fn, log))

    def wait(self, eng, sem, n=None):
        v = sem.n if n is None else n
        if v <= 0:
            return
        self.emit(eng, lambda e, _s=sem.h, _v=v: e.wait_ge(_s, _v),
                  log=f"wait_ge({sem.h.name if hasattr(sem.h,'name') else '?'},{v})")

    def op(self, eng, opname, *args, inc=None, inck=1, **kwargs):
        if inc is not None:
            inc.n += inck
        import traceback as _tb
        _site = "".join(_tb.format_stack(limit=4)[:-1])
        def f(e, _a=args, _k=kwargs, _inc=inc, _inck=inck, _op=opname, _site=_site):
            try:
                ins = getattr(e, _op)(*_a, **_k)
            except Exception as ex:
                raise RuntimeError(f"op {_op} failed: {ex}\nemitted at:\n{_site}")
            if _inc is not None:
                ins.then_inc(_inc.h, _inck)
        lg = opname
        if inc is not None:
            lg += f" ->inc({inc.h.name if hasattr(inc.h,'name') else '?'},{inck})={inc.n}"
        self.emit(eng, f, log=lg)

    def finalize(self):
        nc = self.nc
        script = self.script
        with nc.Block() as block:
            for eng in ENGINES:
                items = [f for (e, f, _l) in script if e == eng]
                if not items:
                    continue
                def section(engobj, _items=items):
                    for f in _items:
                        f(engobj)
                getattr(block, eng)(section)


def make_sems(p):
    return {k: p.sem(k) for k in ("ld", "st", "gd", "pe", "ve", "se", "gp", "ig")}



class LNKit:
    def __init__(self, sb, tag, n=1024):
        self.s1 = sb(f"ln_s1_{tag}", [128, 1])
        self.s2 = sb(f"ln_s2_{tag}", [128, 1])
        self.mu = sb(f"ln_mu_{tag}", [128, 1])
        self.var = sb(f"ln_var_{tag}", [128, 1])
        self.sd = sb(f"ln_sd_{tag}", [128, 1])
        self.rstd = sb(f"ln_rstd_{tag}", [128, 1])
        self.scr = sb(f"ln_scr_{tag}", [128, n])
        self.tiny = sb(f"ln_tiny_{tag}", [128, 1])
        self.xc = sb(f"ln_xc_{tag}", [128, n])


def emit_ln(p, kit, xpo, out_ap, se, ve, eps=1e-5, n=1024.0):
    p.wait("vector", ve)
    p.op("vector", "tensor_scalar", kit.mu[:], kit.s1[:], 1.0 / n, None, ALU.mult,
         inc=ve)
    p.wait("vector", ve)
    p.op("vector", "tensor_scalar", kit.xc[:], xpo, kit.mu[:], None, ALU.subtract,
         inc=ve)
    p.wait("scalar", ve)
    p.op("scalar", "activation", kit.scr[:], kit.xc[:], AF.Square, accum_out=kit.s2[:])
    p.op("scalar", "activation", kit.tiny[:], kit.s2[:], AF.Copy, bias=0.0, scale=1.0,
         inc=se)
    p.wait("vector", se)
    p.op("vector", "tensor_scalar", kit.var[:], kit.s2[:], 1.0 / n, eps, ALU.mult,
         ALU.add, inc=ve)
    p.wait("scalar", ve)
    p.op("scalar", "activation", kit.sd[:], kit.var[:], AF.Sqrt)
    p.op("scalar", "activation", kit.tiny[:], kit.sd[:], AF.Copy, bias=0.0, scale=1.0,
         inc=se)
    p.wait("vector", se)
    p.op("vector", "reciprocal", kit.rstd[:], kit.sd[:], inc=ve)
    p.wait("vector", ve)
    p.op("vector", "tensor_scalar", out_ap, kit.xc[:], kit.rstd[:], None, ALU.mult,
         inc=ve)



def build_stageA(nc, ctx, p, d, s, phases=("pre", "a0", "a0b", "a1", "a2")):
    import os as _os
    sb = lambda n_, s_, t_=F32: ctx.enter_context(nc.sbuf_tensor(n_, s_, t_))
    ld, st, gd = s["ld"], s["st"], s["gd"]
    pe, ve, se, gp = s["pe"], s["ve"], s["se"], s["gp"]

    NT, HC, TC = 64, 8, 16

    bqc = sb("A_bq", [128, 8])
    identf = sb("A_identf", [128, 128])
    identb = sb("A_identb", [128, 128], BF16)
    pmf = sb("A_pmf", [128, 128])
    dpart = sb("A_dpart", [128, 16])
    dbuf = sb("A_dbuf", [128, 64])
    dinv = sb("A_dinv", [128, 64])
    kit = LNKit(sb, "A")

    p.op("gpsimd", "iota", pmf[:], pattern=[[-1, 128]], base=0, channel_multiplier=1,
         allow_small_or_imprecise_dtypes=True, inc=gp)
    p.wait("vector", gp)
    p.op("vector", "tensor_scalar", identf[:], pmf[:], 0.0, None, ALU.is_equal, inc=ve)
    p.wait("vector", ve)
    p.op("vector", "tensor_copy", identb[:], identf[:], inc=ve)
    ident_ready = ve.n
    p.op("sync", "dma_start", out=bqc[:], in_=d["bq"].rearrange("(m p) -> p m", p=128),
         allow_slow_non_contiguous=True, inc=ld, inck=16)
    bq_ready = ld.n

    if "pre" not in phases:
        return st.n
    with ExitStack() as pctx:
        xin = [pctx.enter_context(nc.sbuf_tensor(f"A_xin{i}", [128, 1024], F32))
               for i in range(2)]
        xbsb = [pctx.enter_context(nc.sbuf_tensor(f"A_xbsb{i}", [128, 1024], BF16))
                for i in range(2)]
        pre_ve0, pre_st0 = ve.n, st.n
        for i in range(NT):
            if i >= 2:
                p.wait("sync", ve, pre_ve0 + i - 1)
            p.op("sync", "dma_start", out=xin[i % 2][:],
                 in_=d["x0"][i * 128:(i + 1) * 128, :], inc=ld, inck=16)
            p.wait("vector", ld)
            if i >= 2:
                p.wait("vector", st, pre_st0 + (i - 1) * 16)
            p.op("vector", "tensor_copy", xbsb[i % 2][:], xin[i % 2][:], inc=ve)
            p.wait("scalar", ve)
            p.op("scalar", "dma_start", out=d["xbf"][i * 128:(i + 1) * 128, :],
                 in_=xbsb[i % 2][:], inc=st, inck=16)
        xbf_done = st.n

    with ExitStack() as actx:
        asb = lambda n_, s_, t_=F32: actx.enter_context(nc.sbuf_tensor(n_, s_, t_))
        aps = lambda n_, s_, t_=F32: actx.enter_context(nc.psum_tensor(n_, s_, t_))
        qt = asb("A_qt", [128, HC * 8192], BF16)
        wq = asb("A_wq", [128, HC * 1024], BF16)
        xts = [asb(f"A_xts{g}", [128, HC * 512], BF16) for g in range(2)]
        pctx0 = ExitStack()
        pq = [pctx0.enter_context(nc.psum_tensor(f"A_pq{m}", [128, 512], F32))
              for m in range(8)]

        p.wait("gpsimd", st, xbf_done)
        for hc in range(HC):
            p.op("gpsimd", "dma_start", out=wq[:, hc * 1024:(hc + 1) * 1024],
                 in_=d["Wq"][hc * 128:(hc + 1) * 128, :], inc=gd, inck=16)
        wq_ready = gd.n

        if "a0" not in phases:
            return st.n
        p.wait("sync", st, xbf_done)
        a0_pe0, a0_ve0 = pe.n, ve.n
        for tc in range(TC):
            g = tc % 2
            if tc >= 2:
                p.wait("sync", pe, a0_pe0 + tc - 1)
            for hc in range(HC):
                p.op("sync", "dma_start",
                     out=xts[g][:, hc * 512:(hc + 1) * 512],
                     in_=d["xbf"][tc * 512:(tc + 1) * 512, hc * 128:(hc + 1) * 128],
                     transpose=True, inc=ld, inck=16)
            p.wait("tensor", ld)
            if tc == 0:
                p.wait("tensor", gd, wq_ready)
            if tc >= 1:
                p.wait("tensor", ve, a0_ve0 + tc * 8)
            for hc in range(HC):
                for m in range(8):
                    p.op("tensor", "matmul", pq[m][:],
                         wq[:, hc * 1024 + m * 128: hc * 1024 + (m + 1) * 128],
                         xts[g][:, hc * 512:(hc + 1) * 512],
                         start=(hc == 0), stop=(hc == HC - 1),
                         inc=(pe if (hc == HC - 1 and m == 7) else None))
            p.wait("vector", pe)
            if tc == 0:
                p.wait("vector", ld, bq_ready)
            for m in range(8):
                p.op("vector", "tensor_scalar",
                     qt[:, m * 8192 + tc * 512: m * 8192 + (tc + 1) * 512],
                     pq[m][:], bqc[:, m:m + 1], None, ALU.add, inc=ve)

        pctx0.close()
        if "a0b" in phases:
            vst = [asb(f"A_vst{i}", [128, 1024], BF16) for i in range(2)]
            pctx1 = ExitStack()
            ptb = [pctx1.enter_context(nc.psum_tensor(f"A_ptb{k}", [128, 128], BF16))
                   for k in range(2)]
            b_ve0, b_st0 = ve.n, st.n
            p.wait("tensor", ve, ident_ready)
            p.wait("tensor", ve, b_ve0)
            nt_tr = 0
            for i in range(NT):
                for dc in range(8):
                    k = nt_tr % 2
                    if nt_tr >= 2:
                        p.wait("tensor", ve, b_ve0 + nt_tr - 1)
                    p.op("tensor", "transpose", ptb[k][:],
                         qt[:, dc * 8192 + i * 128: dc * 8192 + (i + 1) * 128],
                         identb[:], inc=pe)
                    p.wait("vector", pe)
                    if dc == 0 and i >= 2:
                        p.wait("vector", st, b_st0 + (i - 1) * 16)
                    p.op("vector", "tensor_copy",
                         vst[i % 2][:, dc * 128:(dc + 1) * 128], ptb[k][:], inc=ve)
                    nt_tr += 1
                p.wait("scalar", ve)
                p.op("scalar", "dma_start", out=d["vtok"][i * 128:(i + 1) * 128, :],
                     in_=vst[i % 2][:], inc=st, inck=16)
            pctx1.close()
        vtok_done = st.n

        if "a1" not in phases:
            return st.n
        esb = [asb(f"A_esb{i}", [128, 8192], BF16) for i in range(2)]
        a1tiny = asb("A_a1tiny", [128, 1], BF16)
        a1tiny2 = asb("A_a1tiny2", [128, 1])
        import os as _os2
        if _os2.environ.get("A1_SERIAL"):
            _p = aps("A_ps1s", [128, 512], F32)
            ps1 = [_p, _p]
        elif _os2.environ.get("A1_SPACED"):
            ps1 = []
            _dummies = []
            for k in range(2):
                ps1.append(aps(f"A_ps1{k}", [128, 512], F32))
                _dummies.append(aps(f"A_ps1d{k}", [128, 512], F32))
        else:
            ps1 = [aps(f"A_ps1{k}", [128, 512], F32) for k in range(2)]
        if _os.environ.get("A1_QTDRAM"):
            for c in range(HC):
                p.op("sync", "dma_start", out=qt[:, c * 8192:(c + 1) * 8192],
                     in_=d["qtdbg"][:, c * 8192:(c + 1) * 8192], inc=ld, inck=16)
            p.wait("tensor", ld)
        a1_se0, a1_ve0, a1_st0 = se.n, ve.n, st.n
        _a1n = int(_os.environ.get("A1N", NT))
        nsc = 0
        _a1sc = int(_os.environ.get("A1SC", TC))
        _a1hc = int(_os.environ.get("A1HC", HC))
        a1_ve_guard = ve.n
        se_after_group = {}
        for j in range(_a1n):
            for sc in range(_a1sc):
                k = nsc % 2
                if nsc < 2:
                    p.wait("tensor", ve, a1_ve_guard)
                if nsc >= 2:
                    p.wait("tensor", se, se_after_group[nsc - 2])
                for hc in range(_a1hc):
                    p.op("tensor", "matmul", ps1[k][:],
                         qt[:, hc * 8192 + j * 128: hc * 8192 + (j + 1) * 128],
                         qt[:, hc * 8192 + sc * 512: hc * 8192 + (sc + 1) * 512],
                         start=(hc == 0), stop=(hc == _a1hc - 1),
                         inc=(pe if hc == _a1hc - 1 else None))
                p.wait("scalar", pe)
                if sc == 0 and j >= 2 and not _os.environ.get("A1_NODMA"):
                    p.wait("scalar", st, a1_st0 + (j - 1) * 16)
                if sc == 0 and j >= 1:
                    p.wait("scalar", ve, a1_ve0 + j)
                p.op("scalar", "activation",
                     esb[j % 2][:, sc * 512:(sc + 1) * 512], ps1[k][:], AF.Exp,
                     scale=1.0 / 32.0, accum_out=dpart[:, sc:sc + 1], inc=se)
                se_after_group[nsc] = se.n
                nsc += 1
            p.op("scalar", "activation", a1tiny[:], esb[j % 2][:, 8191:8192], AF.Copy,
                 bias=0.0, scale=1.0)
            p.op("scalar", "activation", a1tiny2[:], dpart[:, _a1sc - 1:_a1sc], AF.Copy,
                 bias=0.0, scale=1.0, inc=se)
            p.wait("vector", se)
            p.op("vector", "tensor_reduce", dbuf[:, j:j + 1], dpart[:], AX.X, ALU.add,
                 inc=ve)
            if not _os.environ.get("A1_NODMA"):
                p.op("scalar", "dma_start", out=d["ebuf"][j, :, :], in_=esb[j % 2][:],
                     inc=st, inck=16)
        p.wait("vector", ve)
        p.op("vector", "reciprocal", dinv[:], dbuf[:], inc=ve)
        dinv_ready = ve.n
        a1_store_done = st.n
        a1_pe_done = pe.n

    if "a2" not in phases:
        return st.n
    with ExitStack() as actx:
        asb = lambda n_, s_, t_=F32: actx.enter_context(nc.sbuf_tensor(n_, s_, t_))
        aps = lambda n_, s_, t_=F32: actx.enter_context(nc.psum_tensor(n_, s_, t_))
        vsb = asb("A_vsb", [128, NT * 1024], BF16)
        es2 = [asb(f"A_es2{i}", [128, 512], BF16) for i in range(4)]
        po = [aps(f"A_po{m}", [128, 512]) for m in range(8)]
        oT = asb("A_oT", [128, 8 * 512])
        xres = [asb(f"A_xres{i}", [128, 1024]) for i in range(2)]
        xpo = asb("A_xpo", [128, 1024])
        xpo2 = asb("A_xpo2", [128, 1024])
        xout = [asb(f"A_xout{i}", [128, 1024]) for i in range(2)]

        p.wait("sync", st, vtok_done)
        p.wait("sync", pe, a1_pe_done)
        for j in range(NT):
            p.op("sync", "dma_start", out=vsb[:, j * 1024:(j + 1) * 1024],
                 in_=d["vtok"][j * 128:(j + 1) * 128, :], inc=ld, inck=16)
        v_loaded = ld.n
        p.wait("sync", st, a1_store_done)
        p.wait("vector", ve, dinv_ready)

        a2_st0 = st.n
        ne = 0
        pe_after_consume = {}
        epi_ve = 0
        xres_ve = {}
        sq_se = 0
        for qc in range(TC):
            for j in range(NT):
                sl = ne % 4
                if ne >= 4:
                    p.wait("sync", pe, pe_after_consume[ne - 4])
                p.op("sync", "dma_start", out=es2[sl][:],
                     in_=d["ebuf"][j, :, qc * 512:(qc + 1) * 512], inc=ld, inck=16)
                p.wait("tensor", ld)
                if j == 0:
                    p.wait("tensor", ld, v_loaded)
                    if qc > 0:
                        p.wait("tensor", ve, epi_ve)
                for dc in range(8):
                    p.op("tensor", "matmul", po[dc][:],
                         vsb[:, j * 1024 + dc * 128: j * 1024 + (dc + 1) * 128],
                         es2[sl][:],
                         start=(j == 0), stop=(j == NT - 1),
                         inc=(pe if dc == 7 else None))
                pe_after_consume[ne] = pe.n
                ne += 1
            p.wait("vector", pe)
            for dc in range(8):
                p.op("vector", "tensor_copy", oT[:, dc * 512:(dc + 1) * 512],
                     po[dc][:], inc=ve)
            evac_done = ve.n
            for tsub in range(4):
                tq = qc * 4 + tsub
                if tq >= 2:
                    p.wait("sync", ve, xres_ve[tq - 2])
                p.op("sync", "dma_start", out=xres[tq % 2][:],
                     in_=d["x0"][tq * 128:(tq + 1) * 128, :], inc=ld, inck=16)
                xres_ld = ld.n
                tr_ve0 = ve.n
                for dc in range(8):
                    bank, slot = dc % 2, (dc // 2) % 2
                    p.wait("tensor", ve, max(evac_done, tr_ve0 + dc - 1))
                    p.op("tensor", "transpose",
                         po[bank][:, slot * 128:(slot + 1) * 128],
                         oT[:, dc * 512 + tsub * 128: dc * 512 + (tsub + 1) * 128],
                         identf[:], inc=pe)
                    p.wait("vector", pe)
                    if dc == 0:
                        p.wait("vector", se, sq_se)
                    p.op("vector", "tensor_scalar", xpo[:, dc * 128:(dc + 1) * 128],
                         po[bank][:, slot * 128:(slot + 1) * 128],
                         dinv[:, tq:tq + 1], None, ALU.mult, inc=ve)
                p.wait("vector", ld, xres_ld)
                p.wait("vector", ve)
                p.op("vector", "scalar_tensor_tensor", xpo2[:], xres[tq % 2][:], 1.0,
                     xpo[:], ALU.mult, ALU.add, accum_out=kit.s1[:], inc=ve)
                xres_ve[tq] = ve.n
                if tq >= 2:
                    p.wait("vector", st, a2_st0 + (tq - 1) * 16)
                p.wait("scalar", ve)
                if _os.environ.get("A2_DBG") == "xpo2":
                    p.op("vector", "tensor_copy", xout[tq % 2][:], xpo2[:], inc=ve)
                    sq_se = se.n
                elif _os.environ.get("A2_DBG") == "attn":
                    p.op("vector", "tensor_copy", xout[tq % 2][:], xpo[:], inc=ve)
                    sq_se = se.n
                else:
                    emit_ln(p, kit, xpo2[:], xout[tq % 2][:], se, ve)
                    sq_se = se.n - 1
                epi_ve = ve.n
                p.wait("scalar", ve)
                p.op("scalar", "dma_start", out=d["xs1"][tq * 128:(tq + 1) * 128, :],
                     in_=xout[tq % 2][:], inc=st, inck=16)
        stageA_done = st.n
    return stageA_done


def emit_cast_pass(p, nc, ctx, d_src, d_dst, s, tag, nt=64, width=1024):
    ld, st, ve = s["ld"], s["st"], s["ve"]
    with ExitStack() as pctx:
        xin = [pctx.enter_context(nc.sbuf_tensor(f"{tag}_ci{i}", [128, width], F32))
               for i in range(2)]
        xbs = [pctx.enter_context(nc.sbuf_tensor(f"{tag}_cb{i}", [128, width], BF16))
               for i in range(2)]
        ve0, st0 = ve.n, st.n
        for i in range(nt):
            if i >= 2:
                p.wait("sync", ve, ve0 + i - 1)
            p.op("sync", "dma_start", out=xin[i % 2][:],
                 in_=d_src[i * 128:(i + 1) * 128, :], inc=ld, inck=16)
            p.wait("vector", ld)
            if i >= 2:
                p.wait("vector", st, st0 + (i - 1) * 16)
            p.op("vector", "tensor_copy", xbs[i % 2][:], xin[i % 2][:], inc=ve)
            p.wait("scalar", ve)
            p.op("scalar", "dma_start", out=d_dst[i * 128:(i + 1) * 128, :],
                 in_=xbs[i % 2][:], inc=st, inck=16)
    return st.n


def emit_wT_load(p, nc, w_dram, wT_sb, wtmp, ptw, identb, s, nchunk_r=8, nchunk_c=8,
                 ident_ready=0):
    gd, pe, ve = s["gd"], s["pe"], s["ve"]
    R = nchunk_r * 128
    C = nchunk_c * 128
    ntr = 0
    p.wait("tensor", ve, ident_ready)
    for rc in range(nchunk_r):
        p.op("gpsimd", "dma_start", out=wtmp[:],
             in_=w_dram[rc * 128:(rc + 1) * 128, :], inc=gd, inck=16)
        p.wait("tensor", gd)
        for cc in range(nchunk_c):
            k = ntr % 2
            if ntr >= 2:
                p.wait("tensor", ve)
            p.op("tensor", "transpose", ptw[k][:],
                 wtmp[:, cc * 128:(cc + 1) * 128], identb[:], inc=pe)
            p.wait("vector", pe)
            p.op("vector", "tensor_copy",
                 wT_sb[:, cc * R + rc * 128: cc * R + (rc + 1) * 128], ptw[k][:],
                 inc=ve)
            ntr += 1
        p.wait("gpsimd", ve)
    return



def build_stageC(nc, ctx, p, d, s):
    sb = lambda n_, s_, t_=F32: ctx.enter_context(nc.sbuf_tensor(n_, s_, t_))
    ld, st, gd = s["ld"], s["st"], s["gd"]
    pe, ve, se, gp = s["pe"], s["ve"], s["se"], s["gp"]
    NT, HC, TC = 64, 8, 16

    identf = sb("C_identf", [128, 128])
    identb = sb("C_identb", [128, 128], BF16)
    pmf = sb("C_pmf", [128, 128])
    ltm = sb("C_ltm", [128, 128])
    mskC = [sb(f"C_msk{r}", [128, 512]) for r in range(4)]
    scbuf = sb("C_scbuf", [128, 64])
    scmb = sb("C_scmb", [128, 64])
    biasDall = sb("C_biasDall", [128, 64])
    dpc = sb("C_dpc", [128, 18])
    dbufC = sb("C_dbufC", [128, 64])
    dinvC = sb("C_dinvC", [128, 64])
    smaxc = sb("C_smaxc", [128, 1])
    nsmaxc = sb("C_nsmaxc", [128, 1])
    ones1r = sb("C_ones1r", [1, 128])
    sm1sb = sb("C_sm1sb", [1, 1])
    scT = sb("C_scT", [64, 128])
    scrow = sb("C_scrow", [1, 8192])
    screp = sb("C_screp", [128, 8192])
    woT = sb("C_woT", [128, HC * 1024], BF16)
    kit = LNKit(sb, "C")
    ctiny = sb("C_ctiny", [128, 1], BF16)

    p.op("gpsimd", "iota", pmf[:], pattern=[[-1, 128]], base=0, channel_multiplier=1,
         allow_small_or_imprecise_dtypes=True)
    for r in range(4):
        p.op("gpsimd", "iota", mskC[r][:], pattern=[[1, 512]], base=-128 * r,
             channel_multiplier=-1, allow_small_or_imprecise_dtypes=True)
    p.op("gpsimd", "memset", ones1r[:], 1.0, inc=gp)
    p.wait("vector", gp)
    p.op("vector", "tensor_scalar", identf[:], pmf[:], 0.0, None, ALU.is_equal, inc=ve)
    p.wait("vector", ve)
    p.op("vector", "tensor_copy", identb[:], identf[:])
    p.op("vector", "tensor_scalar", ltm[:], pmf[:], 0.0, None, ALU.is_ge)
    for r in range(4):
        p.op("vector", "tensor_scalar", mskC[r][:], mskC[r][:], 0.0, None, ALU.is_ge)
    p.op("vector", "nop", inc=ve)
    ident_ready = ve.n

    x2bf_done = emit_cast_pass(p, nc, ctx, d["x2"], d["x2bf"], s, "C")

    with ExitStack() as cctx:
        csb = lambda n_, s_, t_=F32: cctx.enter_context(nc.sbuf_tensor(n_, s_, t_))
        cps = lambda n_, s_, t_=F32: cctx.enter_context(nc.psum_tensor(n_, s_, t_))
        wjT = csb("C_wjT", [128, HC * 1024], BF16)
        wjvT = csb("C_wjvT", [128, HC * 1024], BF16)
        wtmp = csb("C_wtmp", [128, 1024], BF16)
        xts = [csb(f"C_xts{g}", [128, HC * 512], BF16) for g in range(2)]
        vstg = [csb(f"C_vstg{i}", [128, 1024], BF16) for i in range(2)]
        scp = csb("C_scp", [128, 2])
        wctx = ExitStack()
        ptw = [wctx.enter_context(nc.psum_tensor(f"C_ptw{k}", [128, 128], BF16))
               for k in range(2)]
        p.wait("gpsimd", st, x2bf_done)
        emit_wT_load(p, nc, d["wj"], wjT, wtmp, ptw, identb, s, ident_ready=ident_ready)
        emit_wT_load(p, nc, d["wjv"], wjvT, wtmp, ptw, identb, s, ident_ready=ident_ready)
        emit_wT_load(p, nc, d["wo"], woT, wtmp, ptw, identb, s, ident_ready=ident_ready)
        wT_ready_ve = ve.n
        wctx.close()

        py = [cps(f"C_py{m}", [128, 512]) for m in range(8)]
        p.wait("sync", st, x2bf_done)
        st_i = {}
        pe_tc = {}
        se_tile = {}
        ve_tile = {}
        for tc in range(TC):
            g = tc % 2
            if tc >= 2:
                p.wait("sync", pe, pe_tc[tc - 2])
            for hc in range(HC):
                p.op("sync", "dma_start",
                     out=xts[g][:, hc * 512:(hc + 1) * 512],
                     in_=d["x2bf"][tc * 512:(tc + 1) * 512, hc * 128:(hc + 1) * 128],
                     transpose=True, inc=ld, inck=16)
            p.wait("tensor", ld)
            if tc == 0:
                p.wait("tensor", ve, wT_ready_ve)
            for i4 in range(4):
                i = tc * 4 + i4
                bg = 4 * (i % 2)
                if i >= 2:
                    p.wait("tensor", se, se_tile[i - 2])
                    p.wait("tensor", ve, ve_tile[i - 2])
                for hc in range(HC):
                    lhs = xts[g][:, hc * 512 + i4 * 128: hc * 512 + (i4 + 1) * 128]
                    for half in range(2):
                        p.op("tensor", "matmul", py[bg + half][:],
                             lhs, wjT[:, hc * 1024 + half * 512: hc * 1024 + (half + 1) * 512],
                             start=(hc == 0), stop=(hc == HC - 1))
                        p.op("tensor", "matmul", py[bg + 2 + half][:],
                             lhs, wjvT[:, hc * 1024 + half * 512: hc * 1024 + (half + 1) * 512],
                             start=(hc == 0), stop=(hc == HC - 1),
                             inc=(pe if (hc == HC - 1 and half == 1) else None))
                pe_tc[tc] = pe.n
                p.wait("scalar", pe)
                p.op("scalar", "activation", kit.scr[:, 0:512], py[bg][:], AF.Square,
                     accum_out=scp[:, 0:1])
                p.op("scalar", "activation", kit.scr[:, 0:512], py[bg + 1][:], AF.Square,
                     accum_out=scp[:, 1:2])
                p.op("scalar", "activation", ctiny[:], scp[:, 1:2], AF.Copy,
                     bias=0.0, scale=1.0, inc=se)
                se_tile[i] = se.n
                p.wait("vector", se)
                if i >= 2:
                    p.wait("vector", st, st_i[i - 2])
                p.op("vector", "tensor_scalar", scbuf[:, i:i + 1], scp[:, 0:1],
                     scp[:, 1:2], 1.0 / 32.0, ALU.add, ALU.mult)
                p.op("vector", "tensor_copy", vstg[i % 2][:, 0:512], py[bg + 2][:])
                p.op("vector", "tensor_copy", vstg[i % 2][:, 512:1024], py[bg + 3][:],
                     inc=ve)
                ve_tile[i] = ve.n
                p.wait("scalar", ve)
                p.op("scalar", "dma_start", out=d["vtokC"][i * 128:(i + 1) * 128, :],
                     in_=vstg[i % 2][:], inc=st, inck=16)
                st_i[i] = st.n
        vtokC_done = st.n

    with ExitStack() as cctx:
        csb = lambda n_, s_, t_=F32: cctx.enter_context(nc.sbuf_tensor(n_, s_, t_))
        cps = lambda n_, s_, t_=F32: cctx.enter_context(nc.psum_tensor(n_, s_, t_))
        pctx = ExitStack()
        ptf = pctx.enter_context(nc.psum_tensor("C_ptf", [128, 512], F32))
        p.wait("vector", ve)
        p.op("vector", "tensor_reduce", smaxc[:], scbuf[:], AX.X, ALU.max, inc=ve)
        p.wait("tensor", ve)
        p.op("tensor", "transpose", ptf[0:1, 0:128], smaxc[:], identf[:], inc=pe)
        p.wait("vector", pe)
        p.op("vector", "tensor_reduce", sm1sb[:], ptf[0:1, 0:128], AX.X, ALU.max,
             inc=ve)
        p.wait("tensor", ve)
        p.op("tensor", "transpose", ptf[0:64, 0:128], scbuf[:, 0:64], identf[:], inc=pe)
        p.wait("vector", pe)
        p.op("vector", "tensor_copy", scT[:], ptf[0:64, 0:128], inc=ve)
        p.wait("scalar", ve)
        p.op("scalar", "dma_start", out=d["scd"][:, :], in_=scT[:], inc=st, inck=16)
        p.wait("sync", st)
        p.op("sync", "dma_start", out=scrow[:],
             in_=d["scd"].rearrange("(o a) b -> o (a b)", o=1), inc=ld, inck=16)
        p.wait("tensor", ld)
        p.wait("tensor", ve)
        for c in range(16):
            p.op("tensor", "matmul", ptf[:],
                 ones1r[:], scrow[0:1, c * 512:(c + 1) * 512],
                 start=True, stop=True, inc=pe)
            p.wait("vector", pe)
            p.op("vector", "tensor_copy", screp[:, c * 512:(c + 1) * 512], ptf[:],
                 inc=ve)
            p.wait("tensor", ve)
        p.op("tensor", "matmul", ptf[:, 0:1], ones1r[:], sm1sb[:],
             start=True, stop=True, inc=pe)
        p.wait("vector", pe)
        p.op("vector", "tensor_copy", smaxc[:], ptf[:, 0:1], inc=ve)
        p.wait("vector", ve)
        p.op("vector", "tensor_scalar", nsmaxc[:], smaxc[:], -1.0, None, ALU.mult)
        p.op("vector", "tensor_scalar", scmb[:], scbuf[:], smaxc[:, 0:1], None,
             ALU.subtract)
        p.op("vector", "tensor_scalar", biasDall[:], scbuf[:], nsmaxc[:, 0:1], None,
             ALU.mult, inc=ve)
        prep_ve = ve.n

        edia = csb("C_edia", [128, 128])
        dscr = kit.scr
        dve_i = 0
        p.wait("scalar", ve, prep_ve)
        for i in range(NT):
            fc = (i * 128) // 512
            rm = i * 128 - fc * 512
            if i >= 1:
                p.wait("scalar", ve)
            p.op("vector", "memset", dpc[:], 0.0, inc=ve)
            p.wait("scalar", ve)
            ncol = 0
            for c in range(fc):
                p.op("scalar", "activation", dscr[:, 0:512],
                     screp[:, c * 512:(c + 1) * 512], AF.Exp,
                     scale=scbuf[:, i:i + 1], bias=biasDall[:, i:i + 1],
                     accum_out=dpc[:, ncol:ncol + 1])
                ncol += 1
            if rm > 0:
                p.op("scalar", "activation", dscr[:, 0:rm],
                     screp[:, fc * 512:fc * 512 + rm], AF.Exp,
                     scale=scbuf[:, i:i + 1], bias=biasDall[:, i:i + 1],
                     accum_out=dpc[:, ncol:ncol + 1])
                ncol += 1
            p.op("scalar", "activation", edia[:],
                 screp[:, i * 128:(i + 1) * 128], AF.Exp,
                 scale=scbuf[:, i:i + 1], bias=biasDall[:, i:i + 1])
            p.op("scalar", "activation", ctiny[:], edia[:, 127:128], AF.Copy,
                 bias=0.0, scale=1.0, inc=se)
            p.wait("vector", se)
            p.op("vector", "tensor_tensor", edia[:], edia[:], ltm[:], ALU.mult, inc=ve)
            p.wait("vector", ve)
            p.op("vector", "tensor_reduce", kit.s1[:], edia[:], AX.X, ALU.add, inc=ve)
            p.wait("vector", ve)
            p.op("vector", "tensor_reduce", kit.s2[:], dpc[:], AX.X, ALU.add, inc=ve)
            p.wait("vector", ve)
            p.op("vector", "tensor_tensor", dbufC[:, i:i + 1], kit.s1[:], kit.s2[:],
                 ALU.add, inc=ve)
        p.wait("vector", ve)
        p.op("vector", "reciprocal", dinvC[:], dbufC[:], inc=ve)
        dinv_ready = ve.n

        pctx.close()
        esbC = [csb(f"C_esb{k}", [128, 512], BF16) for k in range(2)]
        vstr = [csb(f"C_vstr{k}", [128, 1024], BF16) for k in range(4)]
        oTC = csb("C_oTC", [128, 8 * 512], BF16)
        xpoC = csb("C_xpoC", [128, 1024])
        xpo2C = csb("C_xpo2C", [128, 1024])
        xresC = [csb(f"C_xresC{k}", [128, 1024]) for k in range(2)]
        xoutC = [csb(f"C_xoutC{k}", [128, 1024]) for k in range(2)]
        poC = [cps(f"C_poC{m}", [128, 512]) for m in range(8)]
        p.wait("sync", st, vtokC_done)
        nv = 0
        pe_v = {}
        se_et = {}
        c2_st0 = st.n
        epi_ve = 0
        xres_ve = {}
        for qc in range(TC):
            jmax = qc * 4 + 4
            for j in range(jmax):
                slv = nv % 4
                if nv >= 4:
                    p.wait("sync", pe, pe_v[nv - 4])
                p.op("sync", "dma_start", out=vstr[slv][:],
                     in_=d["vtokC"][j * 128:(j + 1) * 128, :], inc=ld, inck=16)
                ke = j % 2
                if j >= 2 or qc > 0:
                    p.wait("scalar", pe)
                p.wait("scalar", ve, dinv_ready)
                p.op("scalar", "activation", esbC[ke][:],
                     screp[:, qc * 512:(qc + 1) * 512], AF.Exp,
                     scale=scmb[:, j:j + 1])
                p.op("scalar", "activation", ctiny[:], esbC[ke][:, 511:512], AF.Copy,
                     bias=0.0, scale=1.0, inc=se)
                se_et[(qc, j)] = se.n
                if j >= qc * 4:
                    r = j - qc * 4
                    p.wait("vector", se)
                    p.op("vector", "tensor_tensor", esbC[ke][:], esbC[ke][:],
                         mskC[r][:], ALU.mult, inc=ve)
                    p.wait("tensor", ve)
                else:
                    p.wait("tensor", se)
                p.wait("tensor", ld)
                if qc == 0 and j == 0:
                    pass
                if j == 0 and qc > 0:
                    p.wait("tensor", ve, epi_ve)
                for dc in range(8):
                    p.op("tensor", "matmul", poC[dc][:],
                         vstr[slv][:, dc * 128:(dc + 1) * 128],
                         esbC[ke][:],
                         start=(j == 0), stop=(j == jmax - 1),
                         inc=(pe if dc == 7 else None))
                pe_v[nv] = pe.n
                nv += 1
            p.wait("vector", pe)
            for dc in range(8):
                p.op("vector", "tensor_copy", oTC[:, dc * 512:(dc + 1) * 512],
                     poC[dc][:], inc=ve)
            evac_done = ve.n
            for tsub in range(4):
                tq = qc * 4 + tsub
                if tq >= 2:
                    p.wait("sync", ve, xres_ve[tq - 2])
                p.op("sync", "dma_start", out=xresC[tq % 2][:],
                     in_=d["x2"][tq * 128:(tq + 1) * 128, :], inc=ld, inck=16)
                xres_ld = ld.n
                p.wait("tensor", ve, evac_done if tsub == 0 else xres_ve[tq - 1])
                for hc2 in range(HC):
                    for half in range(2):
                        p.op("tensor", "matmul", poC[half][:],
                             oTC[:, hc2 * 512 + tsub * 128: hc2 * 512 + (tsub + 1) * 128],
                             woT[:, hc2 * 1024 + half * 512: hc2 * 1024 + (half + 1) * 512],
                             start=(hc2 == 0), stop=(hc2 == HC - 1),
                             inc=(pe if (hc2 == HC - 1 and half == 1) else None))
                p.wait("vector", pe)
                for half in range(2):
                    p.op("vector", "tensor_scalar",
                         xpoC[:, half * 512:(half + 1) * 512], poC[half][:],
                         dinvC[:, tq:tq + 1], None, ALU.mult,
                         inc=(ve if half == 1 else None))
                p.wait("vector", ld, xres_ld)
                p.wait("vector", ve)
                p.op("vector", "scalar_tensor_tensor", xpo2C[:], xresC[tq % 2][:], 1.0,
                     xpoC[:], ALU.mult, ALU.add, accum_out=kit.s1[:], inc=ve)
                xres_ve[tq] = ve.n
                if tq >= 2:
                    p.wait("vector", st, c2_st0 + (tq - 1) * 16)
                emit_ln(p, kit, xpo2C[:], xoutC[tq % 2][:], se, ve)
                epi_ve = ve.n
                p.wait("scalar", ve)
                p.op("scalar", "dma_start", out=d["xs3"][tq * 128:(tq + 1) * 128, :],
                     in_=xoutC[tq % 2][:], inc=st, inck=16)
        stageC_done = st.n
    return stageC_done


def emit_ln_affine(p, kit, xpo, out_ap, se, ve, lgrep=None, lbrep=None,
                   eps=1e-5, n=1024.0):
    p.wait("vector", ve)
    p.op("vector", "tensor_scalar", kit.mu[:], kit.s1[:], 1.0 / n, None, ALU.mult,
         inc=ve)
    p.wait("vector", ve)
    p.op("vector", "tensor_scalar", kit.xc[:], xpo, kit.mu[:], None, ALU.subtract,
         inc=ve)
    p.wait("scalar", ve)
    p.op("scalar", "activation", kit.scr[:], kit.xc[:], AF.Square, accum_out=kit.s2[:])
    p.op("scalar", "activation", kit.tiny[:], kit.s2[:], AF.Copy, bias=0.0, scale=1.0,
         inc=se)
    p.wait("vector", se)
    p.op("vector", "tensor_scalar", kit.var[:], kit.s2[:], 1.0 / n, eps, ALU.mult,
         ALU.add, inc=ve)
    p.wait("scalar", ve)
    p.op("scalar", "activation", kit.sd[:], kit.var[:], AF.Sqrt)
    p.op("scalar", "activation", kit.tiny[:], kit.sd[:], AF.Copy, bias=0.0, scale=1.0,
         inc=se)
    p.wait("vector", se)
    p.op("vector", "reciprocal", kit.rstd[:], kit.sd[:], inc=ve)
    p.wait("vector", ve)
    if lgrep is None:
        p.op("vector", "tensor_scalar", out_ap, kit.xc[:], kit.rstd[:], None, ALU.mult,
             inc=ve)
    else:
        p.op("vector", "scalar_tensor_tensor", kit.scr[:], kit.xc[:], kit.rstd[:],
             lgrep, ALU.mult, ALU.mult, inc=ve)
        p.wait("vector", ve)
        p.op("vector", "tensor_tensor", out_ap, kit.scr[:], lbrep, ALU.add, inc=ve)


def emit_replicate_row(p, nc, row_ap, out_sb, ptf, ones1r, s, width):
    pe, ve = s["pe"], s["ve"]
    p.wait("tensor", ve)
    p.op("tensor", "matmul", ptf[:, 0:width], ones1r[:], row_ap, start=True, stop=True,
         inc=pe)
    p.wait("vector", pe)
    p.op("vector", "tensor_copy", out_sb, ptf[:, 0:width], inc=ve)


def build_stageD(nc, ctx, p, d, s):
    sb = lambda n_, s_, t_=F32: ctx.enter_context(nc.sbuf_tensor(n_, s_, t_))
    ld, st, gd = s["ld"], s["st"], s["gd"]
    pe, ve, se, gp = s["pe"], s["ve"], s["se"], s["gp"]
    NT, HC, TC = 64, 8, 16

    identf = sb("D_identf", [128, 128])
    identb = sb("D_identb", [128, 128], BF16)
    pmf = sb("D_pmf", [128, 128])
    mskD = [sb(f"D_msk{r}", [128, 512]) for r in range(8)]
    ones1r = sb("D_ones1r", [1, 128])
    onescol = sb("D_onescol", [128, 1], BF16)
    kit = LNKit(sb, "D")
    dtiny = sb("D_tiny", [128, 1], BF16)

    p.op("gpsimd", "iota", pmf[:], pattern=[[-1, 128]], base=0, channel_multiplier=1,
         allow_small_or_imprecise_dtypes=True)
    p.op("gpsimd", "memset", ones1r[:], 1.0)
    p.op("gpsimd", "memset", onescol[:], 1.0, inc=gp)
    p.wait("vector", gp)
    p.op("vector", "tensor_scalar", identf[:], pmf[:], 0.0, None, ALU.is_equal, inc=ve)
    p.wait("vector", ve)
    p.op("vector", "tensor_copy", identb[:], identf[:])
    for r in range(8):
        p.op("gpsimd", "iota", kit.scr[:, 0:512], pattern=[[1, 512]],
             base=512 - 128 * r, channel_multiplier=-1,
             allow_small_or_imprecise_dtypes=True, inc=gp)
        p.wait("vector", gp)
        p.op("vector", "tensor_scalar", kit.xc[:, 0:512], kit.scr[:, 0:512], 0.0, None,
             ALU.is_ge, inc=ve)
        p.wait("vector", ve)
        p.op("vector", "tensor_scalar", kit.scr[:, 0:512], kit.scr[:, 0:512], 512.0,
             None, ALU.is_le, inc=ve)
        p.wait("vector", ve)
        p.op("vector", "tensor_tensor", mskD[r][:], kit.xc[:, 0:512],
             kit.scr[:, 0:512], ALU.mult, inc=ve)
        p.wait("gpsimd", ve)
    ident_ready = ve.n

    import os as _os
    _dph = set(_os.environ.get("DPH", "d12,d3a,d3b,d3c,fin").split(","))
    _dnl = int(_os.environ.get("DNL", 2))
    zin = d["x3"]
    for l in range(_dnl):
        p.wait("sync", st)
        zbf_done = emit_cast_pass(p, nc, ctx, zin, d["zbf"], s, f"D{l}")
        if "d12" not in _dph:
            return st.n
        with ExitStack() as dctx:
            dsb = lambda n_, s_, t_=F32: dctx.enter_context(nc.sbuf_tensor(f"{n_}_{l}", s_, t_))
            dps = lambda n_, s_, t_=F32: dctx.enter_context(nc.psum_tensor(f"{n_}_{l}", s_, t_))
            wqkvsb = dsb("D_wqkv", [128, HC * 1536], BF16)
            wosb = dsb("D_wosb", [128, 4 * 1024], BF16)
            qsrow = dsb("D_qsrow", [1, 512])
            ksrow = dsb("D_ksrow", [1, 512])
            qsrep = dsb("D_qsrep", [128, 512])
            ksrep = dsb("D_ksrep", [128, 512])
            ush = [[dsb(f"D_ush{g}_{hc}", [128, 512], BF16) for hc in range(HC)]
                   for g in range(2)]
            shs = [[dsb(f"D_shs{g}_{hc}", [128, 512], BF16) for hc in range(4)]
                   for g in range(2)]
            qwork = dsb("D_qwork", [128, 512])
            kwork = dsb("D_kwork", [128, 512])
            sqs = dsb("D_sqs", [128, 8])
            sqs2 = dsb("D_sqs2", [128, 8])
            rwork = dsb("D_rwork", [128, 512])
            t1w = dsb("D_t1w", [128, 512])
            t2w = dsb("D_t2w", [128, 512])
            qfb = dsb("D_qfb", [128, 512], BF16)
            kfb = dsb("D_kfb", [128, 512], BF16)
            tblq = [dsb(f"D_tblq{i}", [128, 64 * 4]) for i in range(2)]
            vring = [dsb(f"D_vring{i}", [128, 512], BF16) for i in range(8)]
            kTr = dsb("D_kTr", [64, 8 * 1024], BF16)
            qTw = dsb("D_qTw", [64, 8 * 512], BF16)
            esbD = [dsb(f"D_esbD{i}", [128, 512], BF16) for i in range(2)]
            invDsb = dsb("D_invDsb", [1, 512])
            rinvsb = dsb("D_rinvsb", [64, 512])
            aT = dsb("D_aT", [128, 4 * 512], BF16)
            xw = dsb("D_xw", [128, 1024])
            zres = [dsb(f"D_zres{i}", [128, 1024]) for i in range(2)]
            z2t = [dsb(f"D_z2t{i}", [128, 1024]) for i in range(2)]
            pq = [dps(f"D_pq{m}", [128, 512]) for m in range(6)]
            ptb = [dps(f"D_ptb{k}", [128, 128], BF16) for k in range(2)]

            p.wait("gpsimd", st, zbf_done)
            p.wait("sync", st, zbf_done)
            p.wait("vector", st, zbf_done)
            for hc in range(HC):
                p.op("gpsimd", "dma_start", out=wqkvsb[:, hc * 1536:(hc + 1) * 1536],
                     in_=d["wqkv"][l, hc * 128:(hc + 1) * 128, :], inc=gd, inck=16)
            for b4 in range(4):
                p.op("gpsimd", "dma_start", out=wosb[:, b4 * 1024:(b4 + 1) * 1024],
                     in_=d["wo_l"][l, b4 * 128:(b4 + 1) * 128, :], inc=gd, inck=16)
            w_ready = gd.n
            p.op("sync", "dma_start", out=qsrow[:], in_=d["qs_t"][l:l + 1, :],
                 inc=ld, inck=16)
            p.op("sync", "dma_start", out=ksrow[:], in_=d["ks_t"][l:l + 1, :],
                 inc=ld, inck=16)
            p.wait("tensor", ld)
            emit_replicate_row(p, nc, qsrow[:], qsrep[:], pq[0], ones1r, s, 512)
            emit_replicate_row(p, nc, ksrow[:], ksrep[:], pq[0], ones1r, s, 512)
            rep_ready = ve.n

            p.wait("sync", st, zbf_done)
            ve_slab = {}
            ve_rot = {}
            pe_tile = {}
            ve_evac = {}
            se_l2 = {}
            ve_attn_done = 0
            st_z2 = {}
            _d12tc = int(_os.environ.get("D12TC", TC))
            for tc in range(_d12tc):
                g = tc % 2
                if tc >= 2:
                    p.wait("sync", pe, pe_tile[(tc - 2, 3)])
                for hc in range(HC):
                    p.op("sync", "dma_start", out=ush[g][hc][:],
                         in_=d["zbf"][tc * 512:(tc + 1) * 512,
                                      hc * 128:(hc + 1) * 128],
                         transpose=True, inc=ld, inck=16)
                slab_ld = ld.n
                p.wait("vector", ld, slab_ld)
                if tc >= 2:
                    p.wait("vector", pe, pe_tile[(tc - 2, 3)])
                for hc in range(4):
                    if tc == 0:
                        p.op("vector", "memset", shs[g][hc][:, 0:1], 0.0)
                    else:
                        p.op("vector", "tensor_copy", shs[g][hc][:, 0:1],
                             ush[g ^ 1][hc + 4][:, 511:512])
                    p.op("vector", "tensor_copy", shs[g][hc][:, 1:512],
                         ush[g][hc + 4][:, 0:511],
                         inc=(ve if hc == 3 else None))
                ve_slab[tc] = ve.n

                for i4 in range(4):
                    i = tc * 4 + i4
                    bq3 = 3 * (i % 2)
                    if i >= 2:
                        p.wait("sync", ve, ve_rot.get(i - 2, 0))
                    p.op("sync", "dma_start", out=tblq[i % 2][:, 0:64],
                         in_=d["cq"][i * 128:(i + 1) * 128, :], inc=ld, inck=16)
                    p.op("sync", "dma_start", out=tblq[i % 2][:, 64:128],
                         in_=d["sq"][i * 128:(i + 1) * 128, :], inc=ld, inck=16)
                    p.op("sync", "dma_start", out=tblq[i % 2][:, 128:192],
                         in_=d["ck"][i * 128:(i + 1) * 128, :], inc=ld, inck=16)
                    p.op("sync", "dma_start", out=tblq[i % 2][:, 192:256],
                         in_=d["sk"][i * 128:(i + 1) * 128, :], inc=ld, inck=16)
                    tbl_ld = ld.n
                    p.wait("tensor", ve, ve_slab[tc])
                    if i == 0:
                        p.wait("tensor", gd, w_ready)
                        p.wait("tensor", ve, max(ident_ready, rep_ready))
                    if i >= 2:
                        p.wait("tensor", ve, ve_evac.get(i - 2, 0))
                        p.wait("tensor", se, se_l2.get(i - 2, 0))
                    if i4 == 0 and ve_attn_done:
                        p.wait("tensor", ve, ve_attn_done)
                        p.wait("tensor", se)
                    for hc in range(HC):
                        lhs = (ush[g][hc] if hc < 4 else shs[g][hc - 4])
                        lhs = lhs[:, i4 * 128:(i4 + 1) * 128]
                        for part in range(3):
                            p.op("tensor", "matmul", pq[bq3 + part][:],
                                 lhs,
                                 wqkvsb[:, hc * 1536 + part * 512:
                                        hc * 1536 + (part + 1) * 512],
                                 start=(hc == 0), stop=(hc == HC - 1),
                                 inc=(pe if (hc == HC - 1 and part == 2) else None))
                    pe_tile[(tc, i4)] = pe.n
                    p.wait("vector", pe)
                    p.op("vector", "tensor_copy", vring[i % 8][:], pq[bq3 + 2][:])
                    p.op("vector", "tensor_copy", qwork[:], pq[bq3][:])
                    p.op("vector", "tensor_copy", kwork[:], pq[bq3 + 1][:], inc=ve)
                    ve_evac[i] = ve.n
                    for which, work, srep, ct_off, st_off, outb in (
                            ("q", qwork, qsrep, 0, 64, qfb),
                            ("k", kwork, ksrep, 128, 192, kfb)):
                        w3 = work[:].rearrange("p (h e) -> p h e", e=64)
                        p.wait("vector", ve)
                        p.op("vector", "tensor_tensor",
                             rwork[:].rearrange("p (h e) -> p h e", e=64), w3, w3,
                             ALU.mult, inc=ve)
                        p.wait("vector", ve)
                        p.op("vector", "tensor_reduce", sqs[:],
                             rwork[:].rearrange("p (h e) -> p h e", e=64), AX.X,
                             ALU.add, inc=ve)
                        p.wait("scalar", ve)
                        p.op("scalar", "activation", sqs2[:], sqs[:], AF.Sqrt)
                        p.op("scalar", "activation", dtiny[:], sqs2[:, 7:8], AF.Copy,
                             bias=0.0, scale=1.0, inc=se)
                        p.wait("vector", se)
                        p.op("vector", "tensor_scalar", sqs2[:], sqs2[:], 1e-12, None,
                             ALU.max, inc=ve)
                        p.wait("vector", ve)
                        p.op("vector", "reciprocal", sqs2[:], sqs2[:], inc=ve)
                        p.wait("vector", ve)
                        sqb = sqs2[:].rearrange("p (h one) -> p h one", one=1) \
                            .broadcast_to([128, 8, 64])
                        p.op("vector", "tensor_tensor",
                             work[:].rearrange("p (h e) -> p h e", e=64), w3, sqb,
                             ALU.mult, inc=ve)
                        p.wait("vector", ve)
                        p.op("vector", "tensor_tensor", work[:], work[:], srep[:],
                             ALU.mult, inc=ve)
                        p.wait("vector", ve)
                        w4 = work[:].rearrange("p (h two e) -> p h two e", two=2, e=32)
                        r4 = rwork[:].rearrange("p (h two e) -> p h two e", two=2, e=32)
                        p.op("vector", "tensor_scalar", r4[:, :, 0:1, :],
                             w4[:, :, 1:2, :], -1.0, None, ALU.mult)
                        p.op("vector", "tensor_scalar", r4[:, :, 1:2, :],
                             w4[:, :, 0:1, :], 1.0, None, ALU.mult, inc=ve)
                        p.wait("vector", ld, tbl_ld)
                        p.wait("vector", ve)
                        ctb = tblq[i % 2][:, ct_off:ct_off + 64] \
                            .rearrange("p (one e) -> p one e", one=1) \
                            .broadcast_to([128, 8, 64])
                        stb = tblq[i % 2][:, st_off:st_off + 64] \
                            .rearrange("p (one e) -> p one e", one=1) \
                            .broadcast_to([128, 8, 64])
                        p.op("vector", "tensor_tensor",
                             t1w[:].rearrange("p (h e) -> p h e", e=64),
                             work[:].rearrange("p (h e) -> p h e", e=64), ctb,
                             ALU.mult)
                        p.op("vector", "tensor_tensor",
                             t2w[:].rearrange("p (h e) -> p h e", e=64),
                             rwork[:].rearrange("p (h e) -> p h e", e=64), stb,
                             ALU.mult, inc=ve)
                        p.wait("vector", ve)
                        p.op("vector", "tensor_tensor", outb[:], t1w[:], t2w[:],
                             ALU.add, inc=ve)
                    se_l2[i] = se.n
                    ve_rot[i] = ve.n
                    if "dbgD" in d and i == int(_os.environ.get("DDUMPI", -1)):
                        p.wait("scalar", ve)
                        p.op("scalar", "dma_start", out=d["dbgD"][0:128, 0:512],
                             in_=ush[g][0][:], inc=st, inck=16)
                        p.op("scalar", "dma_start", out=d["dbgD"][0:128, 512:1024],
                             in_=shs[g][0][:], inc=st, inck=16)
                        p.op("scalar", "dma_start", out=d["dbgD"][0:128, 1024:1536],
                             in_=wqkvsb[:, 0:512], inc=st, inck=16)
                        p.op("scalar", "dma_start", out=d["dbgD"][0:128, 1536:2048],
                             in_=vring[i % 8][:], inc=st, inck=16)
                    ntr0 = 0
                    for which, src, dstbase in (("q", qfb, None), ("k", kfb, None)):
                        for hh in range(8):
                            kk = ntr0 % 2
                            p.wait("tensor", ve)
                            p.op("tensor", "transpose", ptb[kk][0:64, 0:128],
                                 src[:, hh * 64:(hh + 1) * 64], identb[:], inc=pe)
                            p.wait("vector", pe)
                            if which == "q":
                                dst = qTw[0:64, hh * 512 + i4 * 128:
                                          hh * 512 + (i4 + 1) * 128]
                            else:
                                dst = kTr[0:64, hh * 1024 + (i % 8) * 128:
                                          hh * 1024 + ((i % 8) + 1) * 128]
                            p.op("vector", "tensor_copy", dst, ptb[kk][0:64, 0:128],
                                 inc=ve)
                            ntr0 += 1
                if _os.environ.get("D12SUB") == "qkv":
                    continue
                r0 = 4 if tc == 0 else 0
                qkv_done_ve = ve.n
                _skipattn = _os.environ.get("D12SUB") == "skipattn"
                psT = [pq[0], pq[1]]
                poW = pq[2]
                pD = pq[3]
                pR = pq[4]
                for hh in range(0 if _skipattn else 8):
                    if hh == 0:
                        p.wait("tensor", ve, qkv_done_ve)
                    first = True
                    for r in range(r0, 8):
                        gt = (tc - 1) * 4 + r
                        rp = gt % 8
                        kk = r % 2
                        p.wait("tensor", se)
                        p.op("tensor", "matmul", psT[kk][:],
                             kTr[0:64, hh * 1024 + rp * 128:hh * 1024 + (rp + 1) * 128],
                             qTw[0:64, hh * 512:(hh + 1) * 512],
                             start=True, stop=True, inc=pe)
                        p.wait("scalar", pe)
                        p.op("scalar", "activation", esbD[kk][:], psT[kk][:], AF.Exp,
                             scale=8.0)
                        p.op("scalar", "activation", dtiny[:], esbD[kk][:, 511:512],
                             AF.Copy, bias=0.0, scale=1.0, inc=se)
                        p.wait("vector", se)
                        p.op("vector", "tensor_tensor", esbD[kk][:], esbD[kk][:],
                             mskD[r][:], ALU.mult, inc=ve)
                        p.wait("tensor", ve)
                        p.op("tensor", "matmul", poW[0:64, :],
                             vring[rp][:, hh * 64:(hh + 1) * 64], esbD[kk][:],
                             start=first, stop=(r == 7))
                        p.op("tensor", "matmul", pD[0:1, :],
                             onescol[:], esbD[kk][:],
                             start=first, stop=(r == 7),
                             inc=(pe if r == 7 else None))
                        first = False
                    p.wait("vector", pe)
                    p.op("vector", "reciprocal", invDsb[:], pD[0:1, :], inc=ve)
                    p.wait("tensor", ve)
                    p.op("tensor", "matmul", pR[0:64, :], ones1r[0:1, 0:64],
                         invDsb[:], start=True, stop=True, inc=pe)
                    p.wait("vector", pe)
                    p.op("vector", "tensor_copy", rinvsb[:], pR[0:64, :], inc=ve)
                    p.wait("vector", ve)
                    p.op("vector", "tensor_tensor",
                         aT[(hh % 2) * 64:(hh % 2) * 64 + 64,
                            (hh // 2) * 512:(hh // 2 + 1) * 512],
                         poW[0:64, :], rinvsb[:], ALU.mult, inc=ve)
                ve_attn = ve.n if not _skipattn else qkv_done_ve
                if _os.environ.get("D12SUB") == "attn":
                    continue
                pwo = [pq[4], pq[5]]
                _dwo = _os.environ.get("D12WO", "")
                for i4 in range(4):
                    i = tc * 4 + i4
                    if i >= 2:
                        p.wait("sync", ve)
                    p.op("sync", "dma_start", out=zres[i % 2][:],
                         in_=zin[i * 128:(i + 1) * 128, :], inc=ld, inck=16)
                    zld = ld.n
                    p.wait("tensor", ve, max(ve_attn, ve_attn_done))
                    if _dwo != "nomm":
                        for hp in range(4):
                            for half in range(2):
                                p.op("tensor", "matmul", pwo[half][:],
                                     aT[:, hp * 512 + i4 * 128:hp * 512 + (i4 + 1) * 128],
                                     wosb[:, hp * 1024 + half * 512:
                                          hp * 1024 + (half + 1) * 512],
                                     start=(hp == 0), stop=(hp == 3),
                                     inc=(pe if (hp == 3 and half == 1) else None))
                        p.wait("vector", pe)
                    p.wait("vector", ld, zld)
                    if i >= 2 and _dwo != "nostore":
                        p.wait("vector", st, st_z2[i - 2])
                    for half in range(2):
                        if _dwo == "nostt":
                            p.op("vector", "tensor_copy",
                                 z2t[i % 2][:, half * 512:(half + 1) * 512],
                                 zres[i % 2][:, half * 512:(half + 1) * 512],
                                 inc=(ve if half == 1 else None))
                        else:
                            p.op("vector", "scalar_tensor_tensor",
                                 z2t[i % 2][:, half * 512:(half + 1) * 512],
                                 zres[i % 2][:, half * 512:(half + 1) * 512], 1.0,
                                 pwo[half][:], ALU.mult, ALU.add,
                                 inc=(ve if half == 1 else None))
                    ve_attn_done = ve.n
                    p.wait("scalar", ve)
                    if _dwo != "nostore":
                        p.op("scalar", "dma_start",
                             out=d["z2"][i * 128:(i + 1) * 128, :], in_=z2t[i % 2][:],
                             inc=st, inck=16)
                        st_z2[i] = st.n
            d12_done = st.n

        if "d3a" not in _dph:
            return st.n
        with ExitStack() as dctx:
            dsb = lambda n_, s_, t_=F32: dctx.enter_context(nc.sbuf_tensor(f"{n_}_{l}", s_, t_))
            dps = lambda n_, s_, t_=F32: dctx.enter_context(nc.psum_tensor(f"{n_}_{l}", s_, t_))
            lgrow = dsb("D_lgrow", [1, 1024])
            lbrow = dsb("D_lbrow", [1, 1024])
            lgrep = dsb("D_lgrep", [128, 1024])
            lbrep = dsb("D_lbrep", [128, 1024])
            htile = [dsb(f"D_ht{i}", [128, 1024]) for i in range(2)]
            hnstage = [dsb(f"D_hn{i}", [128, 1152], BF16) for i in range(2)]
            ptf2 = dps("D_ptf2", [128, 512])
            p.wait("sync", st, d12_done)
            p.wait("vector", st, d12_done)
            p.op("sync", "dma_start", out=lgrow[:], in_=d["lng"][l:l + 1, :],
                 inc=ld, inck=16)
            p.op("sync", "dma_start", out=lbrow[:], in_=d["lnb"][l:l + 1, :],
                 inc=ld, inck=16)
            p.wait("tensor", ld)
            for half in range(2):
                emit_replicate_row(p, nc, lgrow[0:1, half * 512:(half + 1) * 512],
                                   lgrep[:, half * 512:(half + 1) * 512], ptf2,
                                   ones1r, s, 512)
                emit_replicate_row(p, nc, lbrow[0:1, half * 512:(half + 1) * 512],
                                   lbrep[:, half * 512:(half + 1) * 512], ptf2,
                                   ones1r, s, 512)
            p.op("vector", "memset", hnstage[0][:, 1024:1152], 0.0)
            p.op("vector", "memset", hnstage[1][:, 1024:1152], 0.0)
            p.op("vector", "memset", hnstage[0][:, 1024:1025], 1.0)
            p.op("vector", "memset", hnstage[1][:, 1024:1025], 1.0, inc=ve)
            p.wait("sync", st, d12_done)
            st_hn = {}
            ve_h = {}
            for i in range(NT):
                if i >= 2:
                    p.wait("sync", ve, ve_h[i - 2])
                p.op("sync", "dma_start", out=htile[i % 2][:, 0:512],
                     in_=d["z2"][i * 128:(i + 1) * 128, 0:512], inc=ld, inck=16)
                if i == 0:
                    p.op("sync", "dma_start", out=htile[0][1:128, 512:1024],
                         in_=d["z2"][0:127, 512:1024], inc=ld, inck=16)
                else:
                    p.op("sync", "dma_start", out=htile[i % 2][:, 512:1024],
                         in_=d["z2"][i * 128 - 1:(i + 1) * 128 - 1, 512:1024],
                         inc=ld, inck=16)
                p.wait("vector", ld)
                if i == 0:
                    p.op("vector", "memset", htile[0][0:1, 512:1024], 0.0)
                p.op("vector", "tensor_reduce", kit.s1[:], htile[i % 2][:], AX.X,
                     ALU.add, inc=ve)
                ve_h[i] = ve.n
                if i >= 2:
                    p.wait("vector", st, st_hn[i - 2])
                emit_ln_affine(p, kit, htile[i % 2][:], hnstage[i % 2][:, 0:1024],
                               se, ve, lgrep=lgrep[:], lbrep=lbrep[:])
                ve_h[i] = ve.n
                p.wait("scalar", ve)
                p.op("scalar", "dma_start", out=d["hnbf"][i * 128:(i + 1) * 128, :],
                     in_=hnstage[i % 2][:], inc=st, inck=16)
                st_hn[i] = st.n
            d3a_done = st.n

        if "d3b" not in _dph:
            return st.n
        with ExitStack() as dctx:
            dsb = lambda n_, s_, t_=F32: dctx.enter_context(nc.sbuf_tensor(f"{n_}_{l}", s_, t_))
            dps = lambda n_, s_, t_=F32: dctx.enter_context(nc.psum_tensor(f"{n_}_{l}", s_, t_))
            w1sb = dsb("D_w1sb", [128, 9 * 5632], BF16)
            hnT = [[dsb(f"D_hnT{g}_{c}", [128, 512], BF16) for c in range(9)]
                   for g in range(2)]
            gsc = dsb("D_gsc", [128, 256])
            h2st = [dsb(f"D_h2st{i}", [128, 2944], BF16) for i in range(2)]
            psW = [dps(f"D_psW{k}", [128, 512]) for k in range(2)]
            p.wait("gpsimd", st, d3a_done)
            p.wait("vector", st, d3a_done)
            for c in range(9):
                p.op("gpsimd", "dma_start", out=w1sb[:, c * 5632:(c + 1) * 5632],
                     in_=d["w1e"][l, c * 128:(c + 1) * 128, :], inc=gd, inck=16)
            w1_ready = gd.n
            p.op("vector", "memset", h2st[0][:, 2816:2944], 0.0)
            p.op("vector", "memset", h2st[1][:, 2816:2944], 0.0)
            p.op("vector", "memset", h2st[0][:, 2816:2817], 1.0)
            p.op("vector", "memset", h2st[1][:, 2816:2817], 1.0, inc=ve)
            h2const_ve = ve.n
            p.wait("sync", st, d3a_done)
            pe_g = {}
            st_h2 = {}
            ve_h2 = {}
            se_h2 = {}
            ngrp = 0
            for tc in range(TC):
                g = tc % 2
                if tc >= 2:
                    p.wait("sync", pe, pe_g[(tc - 2, 3, 10)])
                for c in range(9):
                    p.op("sync", "dma_start", out=hnT[g][c][:],
                         in_=d["hnbf"][tc * 512:(tc + 1) * 512,
                                       c * 128:(c + 1) * 128],
                         transpose=True, inc=ld, inck=16)
                p.wait("tensor", ld)
                if tc == 0:
                    p.wait("tensor", gd, w1_ready)
                    p.wait("tensor", ve, h2const_ve)
                for i4 in range(4):
                    i = tc * 4 + i4
                    for grp in range(11):
                        kk = ngrp % 2
                        if ngrp >= 2:
                            p.wait("tensor", se, se_h2.get(ngrp - 2, 0))
                            p.wait("tensor", ve, ve_h2.get(ngrp - 2, 0))
                        for c in range(9):
                            p.op("tensor", "matmul", psW[kk][:],
                                 hnT[g][c][:, i4 * 128:(i4 + 1) * 128],
                                 w1sb[:, c * 5632 + grp * 512:
                                      c * 5632 + (grp + 1) * 512],
                                 start=(c == 0), stop=(c == 8),
                                 inc=(pe if c == 8 else None))
                        pe_g[(tc, i4, grp)] = pe.n
                        p.wait("scalar", pe)
                        p.wait("scalar", ve, ve_h2.get(ngrp - 1, 0))
                        pg = psW[kk][:].rearrange("p (a b) -> p a b", b=256)
                        p.op("scalar", "activation",
                             gsc[:].rearrange("p (a b) -> p a b", a=2),
                             pg[:, :, 128:256], AF.Gelu)
                        p.op("scalar", "activation", dtiny[:], gsc[:, 255:256],
                             AF.Copy, bias=0.0, scale=1.0, inc=se)
                        se_h2[ngrp] = se.n
                        p.wait("vector", se)
                        if i >= 2 and grp == 0:
                            p.wait("vector", st, st_h2[i - 2])
                        p.op("vector", "tensor_tensor",
                             h2st[i % 2][:, grp * 256:(grp + 1) * 256]
                             .rearrange("p (a b) -> p a b", a=2),
                             pg[:, :, 0:128],
                             gsc[:].rearrange("p (a b) -> p a b", a=2), ALU.mult,
                             inc=ve)
                        ve_h2[ngrp] = ve.n
                        ngrp += 1
                    p.wait("scalar", ve)
                    p.op("scalar", "dma_start",
                         out=d["h2bf"][i * 128:(i + 1) * 128, :], in_=h2st[i % 2][:],
                         inc=st, inck=16)
                    st_h2[i] = st.n
                pe_g[(tc, 3, 10)] = pe.n
            d3b_done = st.n

        if "d3c" not in _dph:
            return st.n
        zout = d["z3a"] if l == 0 else d["z3b"]
        with ExitStack() as dctx:
            dsb = lambda n_, s_, t_=F32: dctx.enter_context(nc.sbuf_tensor(f"{n_}_{l}", s_, t_))
            dps = lambda n_, s_, t_=F32: dctx.enter_context(nc.psum_tensor(f"{n_}_{l}", s_, t_))
            w2sb = dsb("D_w2sb", [128, 23 * 1024], BF16)
            h2T = [[dsb(f"D_h2T{g}_{c}", [128, 512], BF16) for c in range(23)]
                   for g in range(2)]
            z2res = [dsb(f"D_z2res{i}", [128, 1024]) for i in range(2)]
            z3t = [dsb(f"D_z3t{i}", [128, 1024]) for i in range(2)]
            psW2 = [dps(f"D_psW2{k}", [128, 512]) for k in range(2)]
            p.wait("gpsimd", st, d3b_done)
            for c in range(23):
                p.op("gpsimd", "dma_start", out=w2sb[:, c * 1024:(c + 1) * 1024],
                     in_=d["w2e"][l, c * 128:(c + 1) * 128, :], inc=gd, inck=16)
            w2_ready = gd.n
            p.wait("sync", st, d3b_done)
            pe_t = {}
            st_z3 = {}
            ve_z3 = {}
            for tc in range(TC):
                g = tc % 2
                if tc >= 2:
                    p.wait("sync", pe, pe_t[(tc - 2, 3)])
                for c in range(23):
                    p.op("sync", "dma_start", out=h2T[g][c][:],
                         in_=d["h2bf"][tc * 512:(tc + 1) * 512,
                                       c * 128:(c + 1) * 128],
                         transpose=True, inc=ld, inck=16)
                p.wait("tensor", ld)
                if tc == 0:
                    p.wait("tensor", gd, w2_ready)
                for i4 in range(4):
                    i = tc * 4 + i4
                    if i >= 2:
                        p.wait("sync", ve, ve_z3.get(i - 2, 0))
                    p.op("sync", "dma_start", out=z2res[i % 2][:],
                         in_=d["z2"][i * 128:(i + 1) * 128, :], inc=ld, inck=16)
                    zld = ld.n
                    if i >= 1:
                        p.wait("tensor", ve, ve_z3.get(i - 1, 0))
                    for half in range(2):
                        for c in range(23):
                            p.op("tensor", "matmul", psW2[half][:],
                                 h2T[g][c][:, i4 * 128:(i4 + 1) * 128],
                                 w2sb[:, c * 1024 + half * 512:
                                      c * 1024 + (half + 1) * 512],
                                 start=(c == 0), stop=(c == 22),
                                 inc=(pe if (c == 22 and half == 1) else None))
                    pe_t[(tc, i4)] = pe.n
                    p.wait("vector", pe)
                    p.wait("vector", ld, zld)
                    if i >= 2:
                        p.wait("vector", st, st_z3[i - 2])
                    for half in range(2):
                        p.op("vector", "scalar_tensor_tensor",
                             z3t[i % 2][:, half * 512:(half + 1) * 512],
                             z2res[i % 2][:, half * 512:(half + 1) * 512], 1.0,
                             psW2[half][:], ALU.mult, ALU.add,
                             inc=(ve if half == 1 else None))
                    ve_z3[i] = ve.n
                    p.wait("scalar", ve)
                    p.op("scalar", "dma_start",
                         out=zout[i * 128:(i + 1) * 128, :], in_=z3t[i % 2][:],
                         inc=st, inck=16)
                    st_z3[i] = st.n
            d3c_done = st.n
        zin = zout

    if "fin" not in _dph:
        return st.n
    with ExitStack() as dctx:
        dsb = lambda n_, s_, t_=F32: dctx.enter_context(nc.sbuf_tensor(f"{n_}_{l}", s_, t_))
        xa = [dsb(f"D_fxa{i}", [128, 1024]) for i in range(2)]
        xb = [dsb(f"D_fxb{i}", [128, 1024]) for i in range(2)]
        xo = [dsb(f"D_fxo{i}", [128, 1024]) for i in range(2)]
        xsum = dsb("D_fxsum", [128, 1024])
        p.wait("sync", st, d3c_done)
        st_f = {}
        ve_f = {}
        for i in range(NT):
            if i >= 2:
                p.wait("sync", ve, ve_f[i - 2])
            p.op("sync", "dma_start", out=xa[i % 2][:],
                 in_=d["x3"][i * 128:(i + 1) * 128, :], inc=ld, inck=16)
            p.op("sync", "dma_start", out=xb[i % 2][:],
                 in_=zin[i * 128:(i + 1) * 128, :], inc=ld, inck=16)
            p.wait("vector", ld)
            p.op("vector", "scalar_tensor_tensor", xsum[:], xa[i % 2][:], 1.0,
                 xb[i % 2][:], ALU.mult, ALU.add, accum_out=kit.s1[:], inc=ve)
            ve_f[i] = ve.n
            if i >= 2:
                p.wait("vector", st, st_f[i - 2])
            emit_ln(p, kit, xsum[:], xo[i % 2][:], se, ve)
            ve_f[i] = ve.n
            p.wait("scalar", ve)
            p.op("scalar", "dma_start", out=d["out4"][i * 128:(i + 1) * 128, :],
                 in_=xo[i % 2][:], inc=st, inck=16)
            st_f[i] = st.n
    return st.n


def build_stageB(nc, ctx, p, d, s):
    sb = lambda n_, s_, t_=F32: ctx.enter_context(nc.sbuf_tensor(n_, s_, t_))
    ld, st, gd = s["ld"], s["st"], s["gd"]
    pe, ve, se, gp, ig = s["pe"], s["ve"], s["se"], s["gp"], s["ig"]
    NT, HC, TC = 64, 8, 16
    import os as _os

    identf = sb("B_identf", [128, 128])
    identb = sb("B_identb", [128, 128], BF16)
    pmf = sb("B_pmf", [128, 128])
    ones1r = sb("B_ones1r", [1, 128])
    onescol = sb("B_onescol", [128, 1], BF16)
    onescf = sb("B_onescf", [128, 1])
    glrep = sb("B_glrep", [128, 1024])
    ghrep = sb("B_ghrep", [128, 1024])
    qtokr = sb("B_qtokr", [128, 1024])
    ktokr = sb("B_ktokr", [128, 1024])
    sbq = sb("B_sbq", [128, 64])
    sbkv = sb("B_sbkv", [128, 64])
    idxf = sb("B_idxf", [128, 64])
    kit = LNKit(sb, "B")
    btiny = sb("B_tiny", [128, 1], BF16)

    p.op("gpsimd", "iota", pmf[:], pattern=[[-1, 128]], base=0, channel_multiplier=1,
         allow_small_or_imprecise_dtypes=True)
    p.op("gpsimd", "iota", idxf[:], pattern=[[128, 64]], base=0, channel_multiplier=1,
         allow_small_or_imprecise_dtypes=True)
    p.op("gpsimd", "memset", ones1r[:], 1.0)
    p.op("gpsimd", "memset", onescol[:], 1.0)
    p.op("gpsimd", "memset", onescf[:], 1.0, inc=gp)
    p.wait("vector", gp)
    p.op("vector", "tensor_scalar", identf[:], pmf[:], 0.0, None, ALU.is_equal, inc=ve)
    p.wait("vector", ve)
    p.op("vector", "tensor_copy", identb[:], identf[:], inc=ve)
    ident_ready = ve.n

    with ExitStack() as bctx:
        rps = bctx.enter_context(nc.psum_tensor("B_rps", [128, 512], F32))
        grow = bctx.enter_context(nc.sbuf_tensor("B_grow", [1, 1024], F32))
        for src_name, dst in (("gamma_light", glrep), ("gamma_heavy", ghrep),
                              ("q_tok", qtokr), ("kv_tok", ktokr)):
            p.op("sync", "dma_start", out=grow[:],
                 in_=d[src_name].rearrange("(o f) -> o f", o=1), inc=ld, inck=16)
            p.wait("tensor", ld)
            for half in range(2):
                emit_replicate_row(p, nc, grow[0:1, half * 512:(half + 1) * 512],
                                   dst[:, half * 512:(half + 1) * 512], rps,
                                   ones1r, s, 512)
            p.wait("sync", ve)
        rep_done = ve.n

    with ExitStack() as bctx:
        bsb = lambda n_, s_, t_=F32: bctx.enter_context(nc.sbuf_tensor(n_, s_, t_))
        xin = [bsb(f"B_xin{i}", [128, 1024]) for i in range(2)]
        xnb = [bsb(f"B_xnb{i}", [128, 1024], BF16) for i in range(2)]
        rms = bsb("B_rms", [128, 1])
        scr2 = bsb("B_scr2", [128, 1024])
        b0_st0 = st.n
        ve_t = {}
        for i in range(NT):
            if i >= 2:
                p.wait("sync", ve, ve_t[i - 2])
            p.op("sync", "dma_start", out=xin[i % 2][:],
                 in_=d["x1"][i * 128:(i + 1) * 128, :], inc=ld, inck=16)
            p.wait("scalar", ld)
            p.op("scalar", "activation", kit.scr[:], xin[i % 2][:], AF.Square,
                 accum_out=kit.s2[:])
            p.op("scalar", "activation", btiny[:], kit.s2[:], AF.Copy, bias=0.0,
                 scale=1.0, inc=se)
            p.wait("vector", se)
            if i == 0:
                p.wait("vector", ve, rep_done)
            p.op("vector", "scalar_tensor_tensor", scr2[:], xin[i % 2][:], 1.0,
                 qtokr[:], ALU.mult, ALU.mult, accum_out=sbq[:, i:i + 1])
            p.op("vector", "scalar_tensor_tensor", scr2[:], xin[i % 2][:], 1.0,
                 ktokr[:], ALU.mult, ALU.mult, accum_out=sbkv[:, i:i + 1], inc=ve)
            p.wait("scalar", ve)
            p.op("scalar", "activation", kit.sd[:], kit.s2[:], AF.Sqrt)
            p.op("scalar", "activation", btiny[:], kit.sd[:], AF.Copy, bias=0.0,
                 scale=1.0, inc=se)
            p.wait("vector", se)
            p.op("vector", "tensor_scalar", kit.sd[:], kit.sd[:], 1e-12, None, ALU.max,
                 inc=ve)
            p.wait("vector", ve)
            p.op("vector", "reciprocal", rms[:], kit.sd[:], inc=ve)
            p.wait("vector", ve)
            p.op("vector", "tensor_scalar", scr2[:], xin[i % 2][:], rms[:], 32.0,
                 ALU.mult, ALU.mult, inc=ve)
            p.wait("vector", ve)
            if i >= 2:
                p.wait("vector", st, b0_st0 + (i - 1) * 16)
            p.op("vector", "tensor_tensor", xnb[i % 2][:], scr2[:], glrep[:], ALU.mult,
                 inc=ve)
            ve_t[i] = ve.n
            p.wait("scalar", ve)
            p.op("scalar", "dma_start", out=d["xnbf"][i * 128:(i + 1) * 128, :],
                 in_=xnb[i % 2][:], inc=st, inck=16)
        xnbf_done = st.n
        s_ready = ve.n

    if "b1" in _os.environ.get("BPH", "b1,b2"):
        with ExitStack() as bctx:
            bsb = lambda n_, s_, t_=F32: bctx.enter_context(nc.sbuf_tensor(n_, s_, t_))
            bps = lambda n_, s_, t_=F32: bctx.enter_context(nc.psum_tensor(n_, s_, t_))
            wcat = bsb("B_wcat", [128, HC * 1536], BF16)
            wol = bsb("B_wol", [128, 4 * 1024], BF16)
            xts = [bsb(f"B_xts{g}", [128, HC * 512], BF16) for g in range(2)]
            strip = [bsb(f"B_strip{g}", [128, 12 * 512], BF16) for g in range(2)]
            vtok = bsb("B_vtok", [128, 64], BF16)
            esbB = bsb("B_esbB", [128, 128], BF16)
            invD = bsb("B_invD", [1, 128])
            rinv = bsb("B_rinv", [64, 128])
            aTB = bsb("B_aTB", [128, 512], BF16)
            xres = [bsb(f"B_xres{i}", [128, 1024]) for i in range(2)]
            btt = [bsb(f"B_btt{i}", [128, 1024]) for i in range(2)]
            psQ = [bps(f"B_psQ{k}", [128, 512]) for k in range(2)]
            poB = bps("B_poB", [128, 512])
            pDb = bps("B_pDb", [128, 512])
            pRb = bps("B_pRb", [128, 512])
            pwo = [bps(f"B_pwo{k}", [128, 512]) for k in range(2)]
            ptbB = bps("B_ptbB", [128, 128], BF16)

            p.wait("gpsimd", st, xnbf_done)
            for hc in range(HC):
                p.op("gpsimd", "dma_start", out=wcat[:, hc * 1536:hc * 1536 + 512],
                     in_=d["wq_light"][hc * 128:(hc + 1) * 128, :], inc=gd, inck=16)
                p.op("gpsimd", "dma_start", out=wcat[:, hc * 1536 + 512:(hc + 1) * 1536],
                     in_=d["wkv_light"][hc * 128:(hc + 1) * 128, :], inc=gd, inck=16)
            for b4 in range(4):
                p.op("gpsimd", "dma_start", out=wol[:, b4 * 1024:(b4 + 1) * 1024],
                     in_=d["wo_light"][b4 * 128:(b4 + 1) * 128, :], inc=gd, inck=16)
            w_ready = gd.n
            p.wait("sync", st, xnbf_done)
            pe_tc = {}
            ve_strip = {}
            ve_attn_end = 0
            st_bt = {}
            for tc in range(TC):
                g = tc % 2
                if tc >= 2:
                    p.wait("sync", pe, pe_tc[tc - 2])
                for hc in range(HC):
                    p.op("sync", "dma_start",
                         out=xts[g][:, hc * 512:(hc + 1) * 512],
                         in_=d["xnbf"][tc * 512:(tc + 1) * 512,
                                       hc * 128:(hc + 1) * 128],
                         transpose=True, inc=ld, inck=16)
                p.wait("tensor", ld)
                if tc == 0:
                    p.wait("tensor", gd, w_ready)
                    p.wait("tensor", ve, ident_ready)
                if tc >= 2:
                    p.wait("tensor", ve, ve_strip[tc - 2])
                str_ve = {}
                for m in range(12):
                    kk = m % 2
                    if m >= 2:
                        p.wait("tensor", ve, str_ve[m - 2])
                    if m < 2 and ve_attn_end:
                        p.wait("tensor", ve, ve_attn_end)
                    for hc in range(HC):
                        p.op("tensor", "matmul", psQ[kk][:],
                             wcat[:, hc * 1536 + m * 128: hc * 1536 + (m + 1) * 128],
                             xts[g][:, hc * 512:(hc + 1) * 512],
                             start=(hc == 0), stop=(hc == HC - 1),
                             inc=(pe if hc == HC - 1 else None))
                    p.wait("vector", pe)
                    p.op("vector", "tensor_copy", strip[g][:, m * 512:(m + 1) * 512],
                         psQ[kk][:], inc=ve)
                    str_ve[m] = ve.n
                ve_strip[tc] = ve.n
                for w4 in range(4):
                    i = tc * 4 + w4
                    if i >= 2:
                        p.wait("sync", ve)
                    p.op("sync", "dma_start", out=xres[i % 2][:],
                         in_=d["x1"][i * 128:(i + 1) * 128, :], inc=ld, inck=16)
                    xres_ld = ld.n
                    for hh in range(8):
                        ch_q = hh // 2
                        ch_k = 4 + hh // 2
                        ch_v = 8 + hh // 2
                        pb = (hh % 2) * 64
                        p.wait("tensor", ve, max(ve_strip[tc], ve_attn_end))
                        p.wait("tensor", ve)
                        p.op("tensor", "transpose", ptbB[0:128, 0:64],
                             strip[g][pb:pb + 64,
                                      ch_v * 512 + w4 * 128:ch_v * 512 + (w4 + 1) * 128],
                             identb[pb:pb + 64, pb:pb + 64], inc=pe)
                        p.wait("vector", pe)
                        p.op("vector", "tensor_copy", vtok[:], ptbB[0:128, 0:64],
                             inc=ve)
                        p.wait("tensor", ve)
                        p.op("tensor", "matmul", psQ[0][:, 0:128],
                             strip[g][pb:pb + 64,
                                      ch_k * 512 + w4 * 128:ch_k * 512 + (w4 + 1) * 128],
                             strip[g][pb:pb + 64,
                                      ch_q * 512 + w4 * 128:ch_q * 512 + (w4 + 1) * 128],
                             start=True, stop=True, inc=pe)
                        p.wait("scalar", pe)
                        p.op("scalar", "activation", esbB[:], psQ[0][:, 0:128], AF.Exp,
                             scale=0.125)
                        p.op("scalar", "activation", btiny[:], esbB[:, 127:128],
                             AF.Copy, bias=0.0, scale=1.0, inc=se)
                        p.wait("tensor", se)
                        p.op("tensor", "matmul", poB[0:64, 0:128], vtok[:, 0:64],
                             esbB[:], start=True, stop=True)
                        p.op("tensor", "matmul", pDb[0:1, 0:128], onescol[:],
                             esbB[:], start=True, stop=True, inc=pe)
                        p.wait("vector", pe)
                        p.op("vector", "reciprocal", invD[:], pDb[0:1, 0:128],
                             inc=ve)
                        p.wait("tensor", ve)
                        p.op("tensor", "matmul", pRb[0:64, 0:128], ones1r[0:1, 0:64],
                             invD[:], start=True, stop=True, inc=pe)
                        p.wait("vector", pe)
                        p.op("vector", "tensor_copy", rinv[:], pRb[0:64, 0:128],
                             inc=ve)
                        p.wait("vector", ve)
                        p.op("vector", "tensor_tensor",
                             aTB[pb:pb + 64, (hh // 2) * 128:(hh // 2 + 1) * 128],
                             poB[0:64, 0:128], rinv[:], ALU.mult, inc=ve)
                    attn_ve = ve.n
                    p.wait("tensor", ve, attn_ve)
                    for b4 in range(4):
                        for half in range(2):
                            p.op("tensor", "matmul", pwo[half][:],
                                 aTB[:, b4 * 128:(b4 + 1) * 128],
                                 wol[:, b4 * 1024 + half * 512:
                                     b4 * 1024 + (half + 1) * 512],
                                 start=(b4 == 0), stop=(b4 == 3),
                                 inc=(pe if (b4 == 3 and half == 1) else None))
                    pe_tc[tc] = pe.n
                    p.wait("vector", pe)
                    p.wait("vector", ld, xres_ld)
                    if i >= 2:
                        p.wait("vector", st, st_bt[i - 2])
                    for half in range(2):
                        p.op("vector", "scalar_tensor_tensor",
                             btt[i % 2][:, half * 512:(half + 1) * 512],
                             xres[i % 2][:, half * 512:(half + 1) * 512], 1.0,
                             pwo[half][:], ALU.mult, ALU.add,
                             inc=(ve if half == 1 else None))
                    ve_attn_end = ve.n
                    p.wait("scalar", ve)
                    p.op("scalar", "dma_start",
                         out=d["btmp"][i * 128:(i + 1) * 128, :], in_=btt[i % 2][:],
                         inc=st, inck=16)
                    st_bt[i] = st.n
            b1_done = st.n
    else:
        b1_done = st.n

    if "b2" in _os.environ.get("BPH", "b1,b2"):
        with ExitStack() as bctx:
            bsb = lambda n_, s_, t_=F32: bctx.enter_context(nc.sbuf_tensor(n_, s_, t_))
            bps = lambda n_, s_, t_=F32: bctx.enter_context(nc.psum_tensor(n_, s_, t_))
            KSEL = {"q": 1024, "kv": 2048}
            LOGK = {"q": float(np.log(np.float32(1152.0))),
                    "kv": float(np.log(np.float32(2304.0)))}
            svec = {"q": sbq, "kv": sbkv}
            bvec = {r: bsb(f"B_bv_{r}", [128, 64]) for r in ("q", "kv")}
            tsb = {r: bsb(f"B_tsb_{r}", [128, 64]) for r in ("q", "kv")}
            rmaxc = {r: bsb(f"B_rmax_{r}", [128, 1]) for r in ("q", "kv")}
            scal1 = {r: bsb(f"B_sc1_{r}", [1, 1]) for r in ("q", "kv")}
            scal2 = {r: bsb(f"B_sc2_{r}", [1, 1]) for r in ("q", "kv")}
            mrep = {r: bsb(f"B_mrep_{r}", [128, 1]) for r in ("q", "kv")}
            scores = {r: bsb(f"B_scores_{r}", [128, 64]) for r in ("q", "kv")}
            etile = {r: bsb(f"B_etile_{r}", [128, 64]) for r in ("q", "kv")}
            dbgst = bsb("B_dbgst", [1, 64])
            lo = {r: bsb(f"B_lo_{r}", [1, 1]) for r in ("q", "kv")}
            hi = {r: bsb(f"B_hi_{r}", [1, 1]) for r in ("q", "kv")}
            mid = {r: bsb(f"B_mid_{r}", [1, 1]) for r in ("q", "kv")}
            cnt = {r: bsb(f"B_cnt_{r}", [1, 1]) for r in ("q", "kv")}
            cond = {r: bsb(f"B_cond_{r}", [1, 1]) for r in ("q", "kv")}
            midrep = {r: bsb(f"B_midrep_{r}", [128, 1]) for r in ("q", "kv")}
            condi = {r: bsb(f"B_condi_{r}", [1, 1], I32) for r in ("q", "kv")}
            selmi = {r: bsb(f"B_selmi_{r}", [128, 64], I32) for r in ("q", "kv")}
            msk = {r: bsb(f"B_msk_{r}", [128, 64]) for r in ("q", "kv")}
            selm = {r: bsb(f"B_selm_{r}", [128, 64]) for r in ("q", "kv")}
            eqm = {r: bsb(f"B_eqm_{r}", [128, 64]) for r in ("q", "kv")}
            rank2 = {r: bsb(f"B_rank2_{r}", [128, 64]) for r in ("q", "kv")}
            idxi32 = {r: bsb(f"B_idxi_{r}", [128, 64], I32) for r in ("q", "kv")}
            toki2 = bsb("B_toki2", [128, 128], I32)
            mT = bsb("B_mT", [64, 128])
            prefA = bsb("B_prefA", [64, 128])
            prefB = bsb("B_prefB", [64, 128])
            rowtot = bsb("B_rowtot", [64, 1])
            rowA = bsb("B_rowA", [1, 64])
            rowB = bsb("B_rowB", [1, 64])
            roffc = bsb("B_roffc", [64, 1])
            rankT = bsb("B_rankT", [64, 128])

            pctxB = ExitStack()
            ptB = pctxB.enter_context(nc.psum_tensor("B_ptB", [128, 512], F32))

            p.wait("tensor", st, b1_done)
            p.wait("vector", st, b1_done)
            p.wait("scalar", st, b1_done)
            p.wait("gpsimd", st, b1_done)
            p.wait("sync", st, b1_done)
            p.wait("vector", ve, s_ready)
            for r in ("q", "kv"):
                p.op("vector", "tensor_scalar", bvec[r][:], svec[r][:], -1.0, None,
                     ALU.mult, inc=ve)
            for it in range(int(_os.environ.get("B2IT", 50))):
                for r in ("q", "kv"):
                    p.wait("vector", ve)
                    p.op("vector", "tensor_tensor", tsb[r][:], svec[r][:], bvec[r][:],
                         ALU.add, inc=ve)
                    p.wait("vector", ve)
                    p.op("vector", "tensor_reduce", rmaxc[r][:], tsb[r][:], AX.X,
                         ALU.max, inc=ve)
                for r in ("q", "kv"):
                    p.wait("tensor", ve)
                    p.op("tensor", "transpose", ptB[0:1, 0:128], rmaxc[r][:],
                         identf[:], inc=pe)
                    p.wait("vector", pe)
                    p.op("vector", "tensor_reduce", scal1[r][:], ptB[0:1, 0:128],
                         AX.X, ALU.max, inc=ve)
                for r in ("q", "kv"):
                    p.wait("tensor", ve)
                    p.op("tensor", "matmul", ptB[:, 128:129], ones1r[:], scal1[r][:],
                         start=True, stop=True, inc=pe)
                    p.wait("vector", pe)
                    p.op("vector", "tensor_scalar", mrep[r][:], ptB[:, 128:129], -1.0,
                         None, ALU.mult, inc=ve)
                for r in ("q", "kv"):
                    p.wait("scalar", ve)
                    p.op("scalar", "activation", etile[r][:], tsb[r][:], AF.Exp,
                         bias=mrep[r][:], scale=1.0)
                    p.op("scalar", "activation", btiny[:], etile[r][:, 63:64],
                         AF.Exp, scale=1e-6, inc=se)
                for r in ("q", "kv"):
                    p.wait("tensor", se)
                    p.wait("tensor", ve)
                    p.op("tensor", "matmul", ptB[0:1, 192:256], onescf[:],
                         etile[r][:], start=True, stop=True, inc=pe)
                    p.wait("vector", pe)
                    p.op("vector", "tensor_reduce", scal2[r][:], ptB[0:1, 192:256],
                         AX.X, ALU.add, inc=ve)
                for r in ("q", "kv"):
                    if r == "q" and it == 1 and "dbgB" in d:
                        p.wait("vector", ve)
                        p.op("vector", "tensor_copy", dbgst[0:1, 0:1], scal2[r][:])
                        p.op("vector", "tensor_copy", dbgst[0:1, 1:2], scal1[r][:],
                             inc=ve)
                        p.wait("sync", ve)
                        p.op("sync", "dma_start", out=d["dbgB"][0:1, 32:96],
                             in_=dbgst[:], inc=ld, inck=16)
                        p.wait("scalar", ld)
                        p.wait("vector", ld)
                    p.wait("scalar", ve)
                    p.op("scalar", "activation", scal2[r][:], scal2[r][:], AF.Ln)
                    p.op("scalar", "activation", btiny[0:1, 0:1], scal2[r][:],
                         AF.Exp, scale=1e-6, inc=se)
                    p.wait("vector", se)
                    p.op("vector", "tensor_tensor", scal2[r][:], scal2[r][:],
                         scal1[r][:], ALU.add, inc=ve)
                    p.wait("vector", ve)
                    p.op("vector", "tensor_scalar", scal2[r][:], scal2[r][:], -1.0,
                         LOGK[r], ALU.mult, ALU.add, inc=ve)
                for r in ("q", "kv"):
                    p.wait("tensor", ve)
                    p.op("tensor", "matmul", ptB[:, 128:129], ones1r[:], scal2[r][:],
                         start=True, stop=True, inc=pe)
                    p.wait("vector", pe)
                    p.op("vector", "tensor_copy", mrep[r][:], ptB[:, 128:129], inc=ve)
                for r in ("q", "kv"):
                    p.wait("vector", ve)
                    p.op("vector", "tensor_scalar", tsb[r][:], svec[r][:],
                         mrep[r][:], None, ALU.add, inc=ve)
                    p.wait("vector", ve)
                    p.op("vector", "tensor_scalar", bvec[r][:], tsb[r][:], 0.0, -1.0,
                         ALU.max, ALU.mult, inc=ve)
            for r in ("q", "kv"):
                p.wait("vector", ve)
                p.op("vector", "tensor_scalar", tsb[r][:], svec[r][:], mrep[r][:],
                     None, ALU.add, inc=ve)
                p.wait("vector", ve)
                p.op("vector", "tensor_tensor", tsb[r][:], tsb[r][:], bvec[r][:],
                     ALU.add, inc=ve)
                p.wait("scalar", ve)
                p.op("scalar", "activation", scores[r][:], tsb[r][:], AF.Exp)
                p.op("scalar", "activation", btiny[:], scores[r][:, 63:64], AF.Copy,
                     bias=0.0, scale=1.0, inc=se)
            if "dbgB" in d:
                p.wait("sync", se)
                p.wait("sync", ve)
                p.op("sync", "dma_start", out=d["dbgB"][:, 0:64],
                     in_=scores["q"][:], inc=ld, inck=16)
                p.op("sync", "dma_start", out=d["dbgB"][:, 64:128],
                     in_=sbq[:], inc=ld, inck=16)
            p.wait("vector", se)
            for r in ("q", "kv"):
                p.op("vector", "memset", lo[r][:], -1.0)
                p.op("vector", "memset", hi[r][:], 2.0, inc=ve)

            def count_gt(r, thr_ap, out_cnt):
                p.wait("tensor", ve)
                p.op("tensor", "matmul", ptB[:, 128:129], ones1r[:], thr_ap,
                     start=True, stop=True, inc=pe)
                p.wait("vector", pe)
                p.op("vector", "tensor_copy", midrep[r][:], ptB[:, 128:129], inc=ve)
                p.wait("vector", ve)
                p.op("vector", "tensor_scalar", msk[r][:], scores[r][:],
                     midrep[r][:], None, ALU.is_gt, inc=ve)
                p.wait("tensor", ve)
                p.op("tensor", "matmul", ptB[0:1, 192:256], onescf[:], msk[r][:],
                     start=True, stop=True, inc=pe)
                p.wait("vector", pe)
                p.op("vector", "tensor_reduce", out_cnt, ptB[0:1, 192:256], AX.X,
                     ALU.add, inc=ve)

            for it in range(45):
                for r in ("q", "kv"):
                    p.wait("vector", ve)
                    p.op("vector", "tensor_tensor", mid[r][:], lo[r][:], hi[r][:],
                         ALU.add, inc=ve)
                    p.wait("vector", ve)
                    p.op("vector", "tensor_scalar", mid[r][:], mid[r][:], 0.5, None,
                         ALU.mult, inc=ve)
                for r in ("q", "kv"):
                    count_gt(r, mid[r][:], cnt[r][:])
                for r in ("q", "kv"):
                    p.wait("vector", ve)
                    p.op("vector", "tensor_scalar", cond[r][:], cnt[r][:],
                         float(KSEL[r]), None, ALU.is_ge, inc=ve)
                    p.wait("vector", ve)
                    p.op("vector", "tensor_copy", condi[r][:], cond[r][:], inc=ve)
                    p.wait("vector", ve)
                    p.op("vector", "select", lo[r][:], condi[r][:], mid[r][:],
                         lo[r][:])
                    p.op("vector", "select", hi[r][:], condi[r][:], hi[r][:],
                         mid[r][:], inc=ve)
            for r in ("q", "kv"):
                count_gt(r, hi[r][:], cnt[r][:])
                p.wait("vector", ve)
                p.op("vector", "tensor_copy", selm[r][:], msk[r][:], inc=ve)
                count_gt(r, lo[r][:], scal1[r][:])
                p.wait("vector", ve)
                p.op("vector", "tensor_tensor", eqm[r][:], msk[r][:], selm[r][:],
                     ALU.subtract, inc=ve)
            for r in ("q", "kv"):
                p.wait("vector", ve)
                p.op("vector", "tensor_scalar", scal2[r][:], cnt[r][:], -1.0,
                     float(KSEL[r]), ALU.mult, ALU.add, inc=ve)
                p.op("vector", "memset", lo[r][:], 0.0)
                p.op("vector", "memset", hi[r][:], 8192.0, inc=ve)
            for it in range(14):
                for r in ("q", "kv"):
                    p.wait("vector", ve)
                    p.op("vector", "tensor_tensor", mid[r][:], lo[r][:], hi[r][:],
                         ALU.add, inc=ve)
                    p.wait("vector", ve)
                    p.op("vector", "tensor_scalar", mid[r][:], mid[r][:], 0.5, None,
                         ALU.mult, inc=ve)
                for r in ("q", "kv"):
                    p.wait("tensor", ve)
                    p.op("tensor", "matmul", ptB[:, 128:129], ones1r[:], mid[r][:],
                         start=True, stop=True, inc=pe)
                    p.wait("vector", pe)
                    p.op("vector", "tensor_copy", midrep[r][:], ptB[:, 128:129],
                         inc=ve)
                    p.wait("vector", ve)
                    p.op("vector", "tensor_scalar", msk[r][:], idxf[:], midrep[r][:],
                         None, ALU.is_lt, inc=ve)
                    p.wait("vector", ve)
                    p.op("vector", "tensor_tensor", msk[r][:], msk[r][:], eqm[r][:],
                         ALU.mult, inc=ve)
                    p.wait("tensor", ve)
                    p.op("tensor", "matmul", ptB[0:1, 192:256], onescf[:], msk[r][:],
                         start=True, stop=True, inc=pe)
                    p.wait("vector", pe)
                    p.op("vector", "tensor_reduce", cnt[r][:], ptB[0:1, 192:256],
                         AX.X, ALU.add, inc=ve)
                for r in ("q", "kv"):
                    p.wait("vector", ve)
                    p.op("vector", "tensor_scalar", cond[r][:], cnt[r][:],
                         scal2[r][0:1, 0:1], None, ALU.is_ge, inc=ve)
                    p.wait("vector", ve)
                    p.op("vector", "tensor_copy", condi[r][:], cond[r][:], inc=ve)
                    p.wait("vector", ve)
                    p.op("vector", "select", hi[r][:], condi[r][:], mid[r][:],
                         hi[r][:])
                    p.op("vector", "select", lo[r][:], condi[r][:], lo[r][:],
                         mid[r][:], inc=ve)
            for r in ("q", "kv"):
                p.wait("tensor", ve)
                p.op("tensor", "matmul", ptB[:, 128:129], ones1r[:], hi[r][:],
                     start=True, stop=True, inc=pe)
                p.wait("vector", pe)
                p.op("vector", "tensor_copy", midrep[r][:], ptB[:, 128:129], inc=ve)
                p.wait("vector", ve)
                p.op("vector", "tensor_scalar", msk[r][:], idxf[:], midrep[r][:],
                     None, ALU.is_lt, inc=ve)
                p.wait("vector", ve)
                p.op("vector", "tensor_tensor", msk[r][:], msk[r][:], eqm[r][:],
                     ALU.mult, inc=ve)
                p.wait("vector", ve)
                p.op("vector", "tensor_tensor", selm[r][:], selm[r][:], msk[r][:],
                     ALU.add, inc=ve)
            sel_ve = ve.n

            p.op("gpsimd", "iota", toki2[:].rearrange("p (f two) -> p f two", two=2),
                 pattern=[[128, 64], [0, 2]], base=0, channel_multiplier=1, inc=gp)
            for r in ("q", "kv"):
                p.wait("tensor", ve, sel_ve)
                p.wait("tensor", ve)
                p.op("tensor", "transpose", ptB[0:64, 0:128], selm[r][:], identf[:],
                     inc=pe)
                p.wait("vector", pe)
                p.op("vector", "tensor_copy", mT[:], ptB[0:64, 0:128], inc=ve)
                src, dst = mT, prefA
                for k in range(7):
                    sh = 1 << k
                    p.wait("vector", ve)
                    p.op("vector", "tensor_copy", dst[:, 0:sh], src[:, 0:sh])
                    p.op("vector", "tensor_tensor", dst[:, sh:128], src[:, sh:128],
                         src[:, 0:128 - sh], ALU.add, inc=ve)
                    src, dst = dst, (prefB if dst is prefA else prefA)
                pref = src
                p.wait("vector", ve)
                p.op("vector", "tensor_copy", rowtot[:], pref[:, 127:128], inc=ve)
                p.wait("tensor", ve)
                p.op("tensor", "transpose", ptB[0:1, 0:64], rowtot[:],
                     identf[0:64, 0:64], inc=pe)
                p.wait("vector", pe)
                p.op("vector", "tensor_copy", rowA[:], ptB[0:1, 0:64], inc=ve)
                srcr, dstr = rowA, rowB
                for k in range(6):
                    sh = 1 << k
                    p.wait("vector", ve)
                    p.op("vector", "tensor_copy", dstr[:, 0:sh], srcr[:, 0:sh])
                    p.op("vector", "tensor_tensor", dstr[:, sh:64], srcr[:, sh:64],
                         srcr[:, 0:64 - sh], ALU.add, inc=ve)
                    srcr, dstr = dstr, srcr
                p.wait("vector", ve)
                p.op("vector", "memset", dstr[:, 0:1], 0.0)
                p.op("vector", "tensor_copy", dstr[:, 1:64], srcr[:, 0:63], inc=ve)
                p.wait("tensor", ve)
                p.op("tensor", "transpose", ptB[0:64, 0:1], dstr[:],
                     identf[0:1, 0:1], inc=pe)
                p.wait("vector", pe)
                p.op("vector", "tensor_copy", roffc[:], ptB[0:64, 0:1], inc=ve)
                p.wait("vector", ve)
                p.op("vector", "tensor_tensor", rankT[:], pref[:], mT[:],
                     ALU.subtract, inc=ve)
                p.wait("vector", ve)
                p.op("vector", "tensor_scalar", rankT[:], rankT[:], roffc[:], None,
                     ALU.add, inc=ve)
                p.wait("tensor", ve)
                p.op("tensor", "transpose", ptB[0:128, 0:64], rankT[:],
                     identf[0:64, 0:64], inc=pe)
                p.wait("vector", pe)
                p.op("vector", "tensor_copy", rank2[r][:], ptB[0:128, 0:64], inc=ve)
                p.wait("vector", ve)
                p.op("vector", "tensor_tensor", msk[r][:], idxf[:], rank2[r][:],
                     ALU.subtract, inc=ve)
                p.wait("vector", ve)
                p.op("vector", "tensor_scalar", msk[r][:], msk[r][:],
                     float(KSEL[r]), None, ALU.add, inc=ve)
                p.wait("vector", ve)
                p.op("vector", "tensor_copy", selmi[r][:], selm[r][:], inc=ve)
                p.wait("vector", ve)
                p.op("vector", "select", msk[r][:], selmi[r][:], rank2[r][:],
                     msk[r][:], inc=ve)
                p.wait("gpsimd", ve)
                p.op("gpsimd", "tensor_copy", idxi32[r][:], msk[r][:], inc=gp)
            breg = {}
            def _mkreg(e, _k=None):
                pass
            for r in ("q", "kv"):
                breg[r] = nc.gpsimd.alloc_register(name=f"bchk_{r}")
                def loadreg(e, _reg=breg[r], _v=KSEL[r] - 1):
                    e.reg_mov(_reg, _v)
                p.emit("gpsimd", loadreg)
            for r, dname in (("q", "qidxd"), ("kv", "kidxd")):
                for f in range(64):
                    p.op("gpsimd", "indirect_dma_start",
                         out=d[dname],
                         out_offset=IOA(idxi32[r][:, f:f + 1], 0),
                         in_=toki2[:, 2 * f:2 * f + 2],
                         in_offset=None, bounds_check=breg[r],
                         oob_is_err=False, inc=ig, inck=16)
            scatter_ig = ig.n
            pctxB.close()

            idxpool = bsb("B_idxpool", [128, 24], I32)
            idxg = bsb("B_idxg", [128, 24], I32)

            xqT = bsb("B_xqT", [128, HC * 1024], BF16)
            ctT = bsb("B_ctT", [128, HC * 2048], BF16)
            rnt = [bsb(f"B_rnt{i}", [128, 1024], BF16) for i in range(2)]
            rmsB = bsb("B_rmsB", [128, 1])
            scrB = bsb("B_scrB", [128, 1024])
            ptH = [bps(f"B_ptH{k}", [128, 128], BF16) for k in range(2)]
            with ExitStack() as gctx:
                xg = gctx.enter_context(nc.sbuf_tensor("B_xg", [128, 8 * 1024], F32))
                cg = gctx.enter_context(nc.sbuf_tensor("B_cg", [128, 16 * 1024], F32))
                p.wait("gpsimd", ig, scatter_ig)
                for bi in range(8):
                    p.op("gpsimd", "dma_start", out=idxpool[:, bi:bi + 1],
                         in_=d["qidxd"][bi * 128:(bi + 1) * 128, 0:1],
                         allow_slow_non_contiguous=True, inc=gd, inck=16)
                for bi in range(16):
                    p.op("gpsimd", "dma_start", out=idxpool[:, 8 + bi:9 + bi],
                         in_=d["kidxd"][bi * 128:(bi + 1) * 128, 0:1],
                         allow_slow_non_contiguous=True, inc=gd, inck=16)
                p.wait("gpsimd", gd)
                p.op("gpsimd", "tensor_copy", idxg[:], idxpool[:], inc=gp)
                for bi in range(8):
                    p.op("gpsimd", "indirect_dma_start",
                         out=xg[:, bi * 1024:(bi + 1) * 1024], out_offset=None,
                         in_=d["x1"], in_offset=IOA(idxg[:, bi:bi + 1], 0),
                         inc=ig, inck=16)
                for bi in range(16):
                    p.op("gpsimd", "indirect_dma_start",
                         out=cg[:, bi * 1024:(bi + 1) * 1024], out_offset=None,
                         in_=d["x1"], in_offset=IOA(idxg[:, 8 + bi:9 + bi], 0),
                         inc=ig, inck=16)
                gather_ig = ig.n
                ntr = 0
                tr_ve = {}
                for t24 in range(24):
                    p.wait("scalar", ig, gather_ig)
                    src = (xg[:, t24 * 1024:(t24 + 1) * 1024] if t24 < 8 else
                           cg[:, (t24 - 8) * 1024:(t24 - 7) * 1024])
                    if t24 >= 2:
                        p.wait("scalar", ve, tr_ve[t24 - 2])
                    p.op("scalar", "activation", kit.scr[:], src, AF.Square,
                         accum_out=kit.s2[:])
                    p.op("scalar", "activation", btiny[:], kit.s2[:], AF.Copy,
                         bias=0.0, scale=1.0, inc=se)
                    p.wait("scalar", se)
                    p.op("scalar", "activation", kit.sd[:], kit.s2[:], AF.Sqrt)
                    p.op("scalar", "activation", btiny[:], kit.sd[:], AF.Copy,
                         bias=0.0, scale=1.0, inc=se)
                    p.wait("vector", se)
                    p.op("vector", "tensor_scalar", kit.sd[:], kit.sd[:], 1e-12,
                         None, ALU.max, inc=ve)
                    p.wait("vector", ve)
                    p.op("vector", "reciprocal", rmsB[:], kit.sd[:], inc=ve)
                    p.wait("vector", ve)
                    p.op("vector", "tensor_scalar", scrB[:], src, rmsB[:], 32.0,
                         ALU.mult, ALU.mult, inc=ve)
                    p.wait("vector", ve)
                    p.op("vector", "tensor_tensor", rnt[t24 % 2][:], scrB[:],
                         ghrep[:], ALU.mult, inc=ve)
                    p.wait("tensor", ve)
                    for hc in range(HC):
                        kk = ntr % 2
                        if ntr >= 2:
                            p.wait("tensor", ve)
                        p.op("tensor", "transpose", ptH[kk][:],
                             rnt[t24 % 2][:, hc * 128:(hc + 1) * 128], identb[:],
                             inc=pe)
                        p.wait("vector", pe)
                        if t24 < 8:
                            dst = xqT[:, hc * 1024 + t24 * 128:
                                      hc * 1024 + (t24 + 1) * 128]
                        else:
                            dst = ctT[:, hc * 2048 + (t24 - 8) * 128:
                                      hc * 2048 + (t24 - 7) * 128]
                        p.op("vector", "tensor_copy", dst, ptH[kk][:], inc=ve)
                        ntr += 1
                    tr_ve[t24] = ve.n
                rn_ve = ve.n
            wkvT = bsb("B_wkvT", [128, HC * 1024], BF16)
            wqh = bsb("B_wqh", [128, HC * 512], BF16)
            woh = bsb("B_woh", [128, 4 * 1024], BF16)
            wtmpB = bsb("B_wtmpB", [128, 1024], BF16)
            qhT = bsb("B_qhT", [128, 4 * 1024], BF16)
            kvT = bsb("B_kvT", [128, 8 * 2048], BF16)
            vtk = bsb("B_vtk", [128, 8 * 16 * 64], BF16)
            eTh = bsb("B_eTh", [128, 1024], BF16)
            invH = bsb("B_invH", [1, 1024])
            rinvH = bsb("B_rinvH", [64, 1024])
            ohT = bsb("B_ohT", [128, 4 * 1024], BF16)
            hvf = [bsb("B_hvf0", [128, 1024])] * 2
            p.wait("gpsimd", ve, rn_ve)
            for hc in range(HC):
                p.op("gpsimd", "dma_start", out=wqh[:, hc * 512:(hc + 1) * 512],
                     in_=d["wq_heavy"][hc * 128:(hc + 1) * 128, :], inc=gd, inck=16)
            for b4 in range(4):
                p.op("gpsimd", "dma_start", out=woh[:, b4 * 1024:(b4 + 1) * 1024],
                     in_=d["wo_heavy"][b4 * 128:(b4 + 1) * 128, :], inc=gd, inck=16)
            wh_ready = gd.n
            with ExitStack() as wctxB:
                ptwB = [wctxB.enter_context(
                    nc.psum_tensor(f"B_ptwB{k}", [128, 128], BF16)) for k in range(2)]
                emit_wT_load(p, nc, d["wkv_heavy"], wkvT, wtmpB, ptwB, identb, s,
                             ident_ready=ident_ready)
            wkvT_ve = ve.n
            pH = [bps(f"B_pH{k}", [128, 512]) for k in range(6)]

            p.wait("tensor", gd, wh_ready)
            p.wait("tensor", ve, rn_ve)
            for m in range(4):
                k2 = (m % 2) * 2
                if m >= 2:
                    p.wait("tensor", ve)
                for hc in range(HC):
                    for half in range(2):
                        p.op("tensor", "matmul", pH[k2 + half][:],
                             wqh[:, hc * 512 + m * 128: hc * 512 + (m + 1) * 128],
                             xqT[:, hc * 1024 + half * 512:
                                 hc * 1024 + (half + 1) * 512],
                             start=(hc == 0), stop=(hc == HC - 1),
                             inc=(pe if (hc == HC - 1 and half == 1) else None))
                p.wait("vector", pe)
                for half in range(2):
                    p.op("vector", "tensor_copy",
                         qhT[:, m * 1024 + half * 512: m * 1024 + (half + 1) * 512],
                         pH[k2 + half][:], inc=(ve if half == 1 else None))
            p.wait("tensor", ve, wkvT_ve)
            for m in range(8):
                p.wait("tensor", ve)
                for hc in range(HC):
                    for q4 in range(4):
                        p.op("tensor", "matmul", pH[q4][:],
                             wkvT[:, hc * 1024 + m * 128: hc * 1024 + (m + 1) * 128],
                             ctT[:, hc * 2048 + q4 * 512: hc * 2048 + (q4 + 1) * 512],
                             start=(hc == 0), stop=(hc == HC - 1),
                             inc=(pe if (hc == HC - 1 and q4 == 3) else None))
                p.wait("vector", pe)
                for q4 in range(4):
                    p.op("vector", "tensor_copy",
                         kvT[:, m * 2048 + q4 * 512: m * 2048 + (q4 + 1) * 512],
                         pH[q4][:], inc=(ve if q4 == 3 else None))
            kv_ve = ve.n
            ntr = 0
            for h in range(8):
                for nt in range(16):
                    kk = ntr % 2
                    p.wait("tensor", ve)
                    p.op("tensor", "transpose", ptH[kk][0:128, 0:64],
                         kvT[64:128, h * 2048 + nt * 128: h * 2048 + (nt + 1) * 128],
                         identb[64:128, 64:128], inc=pe)
                    p.wait("vector", pe)
                    p.op("vector", "tensor_copy",
                         vtk[:, (h * 16 + nt) * 64:(h * 16 + nt + 1) * 64],
                         ptH[kk][0:128, 0:64], inc=ve)
                    ntr += 1
            vtk_ve = ve.n
            qcur = bsb("B_qcur", [64, 1024], BF16)
            for h in range(8):
                pbq = (h % 2) * 64
                chq = h // 2
                p.wait("vector", ve)
                p.op("vector", "tensor_copy", qcur[:],
                     qhT[pbq:pbq + 64, chq * 1024:(chq + 1) * 1024], inc=ve)
                for nt in range(16):
                    p.wait("tensor", ve)
                    p.wait("tensor", se)
                    for half in range(2):
                        p.op("tensor", "matmul", pH[half][:],
                             kvT[0:64, h * 2048 + nt * 128: h * 2048 + (nt + 1) * 128],
                             qcur[0:64, half * 512:(half + 1) * 512],
                             start=True, stop=True,
                             inc=(pe if half == 1 else None))
                    p.wait("scalar", pe)
                    for half in range(2):
                        p.op("scalar", "activation",
                             eTh[:, half * 512:(half + 1) * 512], pH[half][:],
                             AF.Exp, scale=0.125)
                    p.op("scalar", "activation", btiny[:], eTh[:, 1023:1024],
                         AF.Copy, bias=0.0, scale=1.0, inc=se)
                    p.wait("tensor", se)
                    p.op("tensor", "matmul", pH[2][0:64, :],
                         vtk[:, (h * 16 + nt) * 64:(h * 16 + nt + 1) * 64],
                         eTh[:, 0:512], start=(nt == 0), stop=(nt == 15))
                    p.op("tensor", "matmul", pH[3][0:64, :],
                         vtk[:, (h * 16 + nt) * 64:(h * 16 + nt + 1) * 64],
                         eTh[:, 512:1024], start=(nt == 0), stop=(nt == 15))
                    p.op("tensor", "matmul", pH[4][0:1, :], onescol[:],
                         eTh[:, 0:512], start=(nt == 0), stop=(nt == 15))
                    p.op("tensor", "matmul", pH[5][0:1, :], onescol[:],
                         eTh[:, 512:1024], start=(nt == 0), stop=(nt == 15),
                         inc=pe)
                p.wait("vector", pe)
                p.op("vector", "reciprocal", invH[0:1, 0:512], pH[4][0:1, :])
                p.op("vector", "reciprocal", invH[0:1, 512:1024], pH[5][0:1, :],
                     inc=ve)
                p.wait("tensor", ve)
                p.op("tensor", "matmul", pH[0][0:64, :], ones1r[0:1, 0:64],
                     invH[0:1, 0:512], start=True, stop=True)
                p.op("tensor", "matmul", pH[1][0:64, :], ones1r[0:1, 0:64],
                     invH[0:1, 512:1024], start=True, stop=True, inc=pe)
                p.wait("vector", pe)
                p.op("vector", "tensor_copy", rinvH[:, 0:512], pH[0][0:64, :])
                p.op("vector", "tensor_copy", rinvH[:, 512:1024], pH[1][0:64, :],
                     inc=ve)
                p.wait("vector", ve)
                for half in range(2):
                    p.op("vector", "tensor_tensor",
                         ohT[pbq:pbq + 64,
                             chq * 1024 + half * 512: chq * 1024 + (half + 1) * 512],
                         pH[2 + half][0:64, :], rinvH[:, half * 512:(half + 1) * 512],
                         ALU.mult, inc=(ve if half == 1 else None))
            oh_ve = ve.n
            p.wait("gpsimd", st, b1_done)
            hv_ve = {}
            for tqt in range(8):
                p.wait("tensor", ve, oh_ve if tqt == 0 else hv_ve[tqt - 1])
                for b4 in range(4):
                    for half in range(2):
                        p.op("tensor", "matmul", pH[half][:],
                             ohT[:, b4 * 1024 + tqt * 128: b4 * 1024 + (tqt + 1) * 128],
                             woh[:, b4 * 1024 + half * 512:
                                 b4 * 1024 + (half + 1) * 512],
                             start=(b4 == 0), stop=(b4 == 3),
                             inc=(pe if (b4 == 3 and half == 1) else None))
                p.wait("vector", pe)
                if tqt >= 1:
                    p.wait("vector", ig)
                for half in range(2):
                    p.op("vector", "tensor_copy",
                         hvf[tqt % 2][:, half * 512:(half + 1) * 512], pH[half][:],
                         inc=(ve if half == 1 else None))
                hv_ve[tqt] = ve.n
                p.wait("gpsimd", ve, hv_ve[tqt])
                p.op("gpsimd", "indirect_dma_start",
                     out=d["btmp"],
                     out_offset=IOA(idxg[:, tqt:tqt + 1], 0),
                     in_=hvf[tqt % 2][:], in_offset=None,
                     compute_op=ALU.add, inc=ig, inck=16)
            heavy_ig = ig.n
    else:
        heavy_ig = ig.n

    with ExitStack() as bctx:
        bsb2 = lambda n_, s_, t_=F32: bctx.enter_context(nc.sbuf_tensor(n_, s_, t_))
        bt = [bsb2(f"B3_bt{i}", [128, 1024]) for i in range(2)]
        xo = [bsb2(f"B3_xo{i}", [128, 1024]) for i in range(2)]
        p.wait("sync", st, b1_done)
        p.wait("sync", ig, heavy_ig)
        p.wait("vector", ig, heavy_ig)
        st_f = {}
        ve_f = {}
        for i in range(NT):
            if i >= 2:
                p.wait("sync", ve, ve_f[i - 2])
            p.op("sync", "dma_start", out=bt[i % 2][:],
                 in_=d["btmp"][i * 128:(i + 1) * 128, :], inc=ld, inck=16)
            p.wait("vector", ld)
            p.op("vector", "tensor_reduce", kit.s1[:], bt[i % 2][:], AX.X, ALU.add,
                 inc=ve)
            if i >= 2:
                p.wait("vector", st, st_f[i - 2])
            emit_ln(p, kit, bt[i % 2][:], xo[i % 2][:], se, ve)
            ve_f[i] = ve.n
            p.wait("scalar", ve)
            p.op("scalar", "dma_start", out=d["xs2"][i * 128:(i + 1) * 128, :],
                 in_=xo[i % 2][:], inc=st, inck=16)
            st_f[i] = st.n
    return st.n

from concourse.bass_utils import run_bass_kernel_spmd

S, D_MODEL, NB = 8192, 1024, 8


def _mk_tables():
    half = np.arange(0, 64, 2, dtype=np.float64)
    inv_freq = 1.0 / 10000.0 ** (half / 64)
    t = np.arange(S, dtype=np.float64)
    freqs = np.concatenate([t[:, None] * inv_freq[None, :]] * 2, 1)
    base = (half + 0.4 * 64) / (1.4 * 64)
    power = (t - S // 2) / 256.0
    scale = np.concatenate([base[None, :] ** power[:, None]] * 2, 1)
    cos, sin = np.cos(freqs), np.sin(freqs)
    return (np.ascontiguousarray((cos * scale).astype(np.float32)),
            np.ascontiguousarray((sin * scale).astype(np.float32)),
            np.ascontiguousarray((cos / scale).astype(np.float32)),
            np.ascontiguousarray((sin / scale).astype(np.float32)))


def _prep_d_weights(w1, b1, w2, b2):
    L = w1.shape[0]
    w1e = np.zeros((L, 1152, 5632), np.float32)
    w2e = np.zeros((L, 2944, 1024), np.float32)
    for l in range(L):
        for k in range(22):
            a0, a1 = 128 * k, min(128 * (k + 1), 2730)
            n = a1 - a0
            w1e[l, 0:1024, 256 * k:256 * k + n] = w1[l][:, a0:a1]
            w1e[l, 1024, 256 * k:256 * k + n] = b1[l][a0:a1]
            w1e[l, 0:1024, 256 * k + 128:256 * k + 128 + n] = w1[l][:, 2730 + a0:2730 + a1]
            w1e[l, 1024, 256 * k + 128:256 * k + 128 + n] = b1[l][2730 + a0:2730 + a1]
            w2e[l, 128 * k:128 * k + n, :] = w2[l][a0:a1, :]
        w2e[l, 2816, :] = b2[l]
    return w1e, w2e


def _build_stage(stage, shapes_extra=None):
    nc = bass.Bass("TRN2", target_bir_lowering=False)
    d = {}

    def inp(name, shape):
        d[name] = nc.dram_tensor(name, shape, F32, kind="ExternalInput").ap()

    def intern(name, shape, dt=F32):
        d[name] = nc.dram_tensor(name, shape, dt, kind="Internal").ap()

    def outp(name, shape):
        d[name] = nc.dram_tensor(name, shape, F32, kind="ExternalOutput").ap()

    with ExitStack() as ctx:
        p = Prog(nc, ctx)
        s = make_sems(p)
        if stage == "A":
            inp("x0", [S, 1024]); inp("Wq", [1024, 1024]); inp("bq", [1024])
            intern("xbf", [S, 1024], BF16); intern("vtok", [S, 1024], BF16)
            intern("ebuf", [64, 128, S], BF16)
            outp("xs1", [S, 1024])
            done = build_stageA(nc, ctx, p, d, s)
        elif stage == "B":
            inp("x1", [S, 1024])
            inp("gamma_light", [1024]); inp("gamma_heavy", [1024])
            inp("q_tok", [1024]); inp("kv_tok", [1024])
            inp("wq_light", [1024, 512]); inp("wkv_light", [1024, 1024])
            inp("wo_light", [512, 1024])
            inp("wq_heavy", [1024, 512]); inp("wkv_heavy", [1024, 1024])
            inp("wo_heavy", [512, 1024])
            intern("xnbf", [S, 1024], BF16); intern("btmp", [S, 1024])
            intern("qidxd", [S, 2], I32); intern("kidxd", [S, 2], I32)
            outp("xs2", [S, 1024])
            done = build_stageB(nc, ctx, p, d, s)
        elif stage == "C":
            inp("x2", [S, 1024]); inp("wj", [1024, 1024]); inp("wjv", [1024, 1024])
            inp("wo", [1024, 1024])
            intern("x2bf", [S, 1024], BF16); intern("vtokC", [S, 1024], BF16)
            intern("scd", [64, 128])
            outp("xs3", [S, 1024])
            done = build_stageC(nc, ctx, p, d, s)
        else:
            inp("x3", [S, 1024]); inp("wqkv", [2, 1024, 1536])
            inp("wo_l", [2, 512, 1024]); inp("qs_t", [2, 512]); inp("ks_t", [2, 512])
            inp("lng", [2, 1024]); inp("lnb", [2, 1024])
            inp("w1e", [2, 1152, 5632]); inp("w2e", [2, 2944, 1024])
            inp("cq", [S, 64]); inp("sq", [S, 64]); inp("ck", [S, 64])
            inp("sk", [S, 64])
            intern("zbf", [S, 1024], BF16); intern("z2", [S, 1024])
            intern("z3a", [S, 1024]); intern("z3b", [S, 1024])
            intern("hnbf", [S, 1152], BF16); intern("h2bf", [S, 2944], BF16)
            outp("out4", [S, 1024])
            done = build_stageD(nc, ctx, p, d, s)
        p.wait("sync", s["st"], done)
        p.finalize()
    return nc


def kernel(L0_attn_inputs, L0_attn_Wq, L0_attn_bq, L0_ffn_gamma_light,
           L0_ffn_wq_light, L0_ffn_wkv_light, L0_ffn_wo_light,
           L0_ffn_q_routing_token, L0_ffn_kv_routing_token, L0_ffn_gamma_heavy,
           L0_ffn_wq_heavy, L0_ffn_wkv_heavy, L0_ffn_wo_heavy,
           L1_attn_wj_w, L1_attn_wjv_w, L1_attn_wo_w, L1_ffn_wqkv,
           L1_ffn_q_scale, L1_ffn_k_scale, L1_ffn_wo, L1_ffn_ln_g, L1_ffn_ln_b,
           L1_ffn_w1, L1_ffn_b1, L1_ffn_w2, L1_ffn_b2):
    f32 = lambda a: np.ascontiguousarray(np.asarray(a, dtype=np.float32))
    x0 = f32(L0_attn_inputs)
    cores = list(range(NB))

    nc = _build_stage("A")
    maps = [{"x0": x0[b], "Wq": f32(L0_attn_Wq), "bq": f32(L0_attn_bq)}
            for b in range(NB)]
    res = run_bass_kernel_spmd(nc, maps, core_ids=cores)
    x1 = np.stack([res.results[b]["xs1"] for b in range(NB)])

    nc = _build_stage("B")
    maps = [{"x1": x1[b], "gamma_light": f32(L0_ffn_gamma_light),
             "gamma_heavy": f32(L0_ffn_gamma_heavy),
             "q_tok": f32(L0_ffn_q_routing_token),
             "kv_tok": f32(L0_ffn_kv_routing_token),
             "wq_light": f32(L0_ffn_wq_light), "wkv_light": f32(L0_ffn_wkv_light),
             "wo_light": f32(L0_ffn_wo_light), "wq_heavy": f32(L0_ffn_wq_heavy),
             "wkv_heavy": f32(L0_ffn_wkv_heavy), "wo_heavy": f32(L0_ffn_wo_heavy)}
            for b in range(NB)]
    res = run_bass_kernel_spmd(nc, maps, core_ids=cores)
    x2 = np.stack([res.results[b]["xs2"] for b in range(NB)])

    nc = _build_stage("C")
    maps = [{"x2": x2[b], "wj": f32(L1_attn_wj_w), "wjv": f32(L1_attn_wjv_w),
             "wo": f32(L1_attn_wo_w)} for b in range(NB)]
    res = run_bass_kernel_spmd(nc, maps, core_ids=cores)
    x3 = np.stack([res.results[b]["xs3"] for b in range(NB)])

    w1e, w2e = _prep_d_weights(f32(L1_ffn_w1), f32(L1_ffn_b1), f32(L1_ffn_w2),
                               f32(L1_ffn_b2))
    qs_t = np.ascontiguousarray(np.tile(f32(L1_ffn_q_scale), (1, 8)))
    ks_t = np.ascontiguousarray(np.tile(f32(L1_ffn_k_scale), (1, 8)))
    cq, sq, ck, sk = _mk_tables()
    nc = _build_stage("D")
    maps = [{"x3": x3[b], "wqkv": f32(L1_ffn_wqkv), "wo_l": f32(L1_ffn_wo),
             "qs_t": qs_t, "ks_t": ks_t, "lng": f32(L1_ffn_ln_g),
             "lnb": f32(L1_ffn_ln_b), "w1e": w1e, "w2e": w2e,
             "cq": cq, "sq": sq, "ck": ck, "sk": sk} for b in range(NB)]
    res = run_bass_kernel_spmd(nc, maps, core_ids=cores)
    out = np.stack([res.results[b]["out4"] for b in range(NB)])
    return out


# version 02
# speedup vs baseline: 1.1736x; 1.1736x over previous
import numpy as np
from contextlib import ExitStack
import concourse.bass as bass
import concourse.mybir as mybir

F32 = mybir.dt.float32
BF16 = mybir.dt.bfloat16
U32 = mybir.dt.uint32
I32 = mybir.dt.int32
AF = mybir.ActivationFunctionType
ALU = mybir.AluOpType
AX = mybir.AxisListType
IOA = bass.IndirectOffsetOnAxis

ENGINES = ("sync", "tensor", "vector", "scalar", "gpsimd")


class _Sem:
    def __init__(self, h):
        self.h = h
        self.n = 0


class Prog:
    def __init__(self, nc, ctx):
        self.nc = nc
        self.ctx = ctx
        self.script = []

    def sem(self, name):
        return _Sem(self.ctx.enter_context(self.nc.semaphore(name)))

    def emit(self, eng, fn, log=""):
        self.script.append((eng, fn, log))

    def wait(self, eng, sem, n=None):
        v = sem.n if n is None else n
        if v <= 0:
            return
        self.emit(eng, lambda e, _s=sem.h, _v=v: e.wait_ge(_s, _v),
                  log=f"wait_ge({sem.h.name if hasattr(sem.h,'name') else '?'},{v})")

    def op(self, eng, opname, *args, inc=None, inck=1, **kwargs):
        if inc is not None:
            inc.n += inck
        import traceback as _tb
        _site = "".join(_tb.format_stack(limit=4)[:-1])
        def f(e, _a=args, _k=kwargs, _inc=inc, _inck=inck, _op=opname, _site=_site):
            try:
                ins = getattr(e, _op)(*_a, **_k)
            except Exception as ex:
                raise RuntimeError(f"op {_op} failed: {ex}\nemitted at:\n{_site}")
            if _inc is not None:
                ins.then_inc(_inc.h, _inck)
        lg = opname
        if inc is not None:
            lg += f" ->inc({inc.h.name if hasattr(inc.h,'name') else '?'},{inck})={inc.n}"
        self.emit(eng, f, log=lg)

    def finalize(self):
        nc = self.nc
        script = self.script
        with nc.Block() as block:
            for eng in ENGINES:
                items = [f for (e, f, _l) in script if e == eng]
                if not items:
                    continue
                def section(engobj, _items=items):
                    for f in _items:
                        f(engobj)
                getattr(block, eng)(section)


def make_sems(p):
    return {k: p.sem(k) for k in ("ld", "st", "gd", "pe", "ve", "se", "gp", "ig")}



class LNKit:
    def __init__(self, sb, tag, n=1024):
        self.s1 = sb(f"ln_s1_{tag}", [128, 1])
        self.s2 = sb(f"ln_s2_{tag}", [128, 1])
        self.mu = sb(f"ln_mu_{tag}", [128, 1])
        self.var = sb(f"ln_var_{tag}", [128, 1])
        self.sd = sb(f"ln_sd_{tag}", [128, 1])
        self.rstd = sb(f"ln_rstd_{tag}", [128, 1])
        self.scr = sb(f"ln_scr_{tag}", [128, n])
        self.tiny = sb(f"ln_tiny_{tag}", [128, 1])
        self.xc = sb(f"ln_xc_{tag}", [128, n])


def emit_ln(p, kit, xpo, out_ap, se, ve, eps=1e-5, n=1024.0):
    p.wait("vector", ve)
    p.op("vector", "tensor_scalar", kit.mu[:], kit.s1[:], 1.0 / n, None, ALU.mult,
         inc=ve)
    p.wait("vector", ve)
    p.op("vector", "tensor_scalar", kit.xc[:], xpo, kit.mu[:], None, ALU.subtract,
         inc=ve)
    p.wait("scalar", ve)
    p.op("scalar", "activation", kit.scr[:], kit.xc[:], AF.Square, accum_out=kit.s2[:])
    p.op("scalar", "activation", kit.tiny[:], kit.s2[:], AF.Copy, bias=0.0, scale=1.0,
         inc=se)
    p.wait("vector", se)
    p.op("vector", "tensor_scalar", kit.var[:], kit.s2[:], 1.0 / n, eps, ALU.mult,
         ALU.add, inc=ve)
    p.wait("scalar", ve)
    p.op("scalar", "activation", kit.sd[:], kit.var[:], AF.Sqrt)
    p.op("scalar", "activation", kit.tiny[:], kit.sd[:], AF.Copy, bias=0.0, scale=1.0,
         inc=se)
    p.wait("vector", se)
    p.op("vector", "reciprocal", kit.rstd[:], kit.sd[:], inc=ve)
    p.wait("vector", ve)
    p.op("vector", "tensor_scalar", out_ap, kit.xc[:], kit.rstd[:], None, ALU.mult,
         inc=ve)



def build_stageA(nc, ctx, p, d, s, phases=("pre", "a0", "a0b", "a1", "a2")):
    import os as _os
    sb = lambda n_, s_, t_=F32: ctx.enter_context(nc.sbuf_tensor(n_, s_, t_))
    ld, st, gd = s["ld"], s["st"], s["gd"]
    pe, ve, se, gp = s["pe"], s["ve"], s["se"], s["gp"]

    NT, HC, TC = 64, 8, 16

    bqc = sb("A_bq", [128, 8])
    identf = sb("A_identf", [128, 128])
    identb = sb("A_identb", [128, 128], BF16)
    pmf = sb("A_pmf", [128, 128])
    dpart = sb("A_dpart", [128, 16])
    dbuf = sb("A_dbuf", [128, 64])
    dinv = sb("A_dinv", [128, 64])
    kit = LNKit(sb, "A")

    p.op("gpsimd", "iota", pmf[:], pattern=[[-1, 128]], base=0, channel_multiplier=1,
         allow_small_or_imprecise_dtypes=True, inc=gp)
    p.wait("vector", gp)
    p.op("vector", "tensor_scalar", identf[:], pmf[:], 0.0, None, ALU.is_equal, inc=ve)
    p.wait("vector", ve)
    p.op("vector", "tensor_copy", identb[:], identf[:], inc=ve)
    ident_ready = ve.n
    p.op("sync", "dma_start", out=bqc[:], in_=d["bq"].rearrange("(m p) -> p m", p=128),
         allow_slow_non_contiguous=True, inc=ld, inck=16)
    bq_ready = ld.n

    if "pre" not in phases:
        return st.n
    with ExitStack() as pctx:
        xin = [pctx.enter_context(nc.sbuf_tensor(f"A_xin{i}", [128, 1024], F32))
               for i in range(2)]
        xbsb = [pctx.enter_context(nc.sbuf_tensor(f"A_xbsb{i}", [128, 1024], BF16))
                for i in range(2)]
        pre_ve0, pre_st0 = ve.n, st.n
        for i in range(NT):
            if i >= 2:
                p.wait("sync", ve, pre_ve0 + i - 1)
            p.op("sync", "dma_start", out=xin[i % 2][:],
                 in_=d["x0"][i * 128:(i + 1) * 128, :], inc=ld, inck=16)
            p.wait("vector", ld)
            if i >= 2:
                p.wait("vector", st, pre_st0 + (i - 1) * 16)
            p.op("vector", "tensor_copy", xbsb[i % 2][:], xin[i % 2][:], inc=ve)
            p.wait("scalar", ve)
            p.op("scalar", "dma_start", out=d["xbf"][i * 128:(i + 1) * 128, :],
                 in_=xbsb[i % 2][:], inc=st, inck=16)
        xbf_done = st.n

    with ExitStack() as actx:
        asb = lambda n_, s_, t_=F32: actx.enter_context(nc.sbuf_tensor(n_, s_, t_))
        aps = lambda n_, s_, t_=F32: actx.enter_context(nc.psum_tensor(n_, s_, t_))
        qt = asb("A_qt", [128, HC * 8192], BF16)
        wq = asb("A_wq", [128, HC * 1024], BF16)
        xts = [asb(f"A_xts{g}", [128, HC * 512], BF16) for g in range(2)]
        pctx0 = ExitStack()
        pq = [pctx0.enter_context(nc.psum_tensor(f"A_pq{m}", [128, 512], F32))
              for m in range(8)]

        p.wait("gpsimd", st, xbf_done)
        for hc in range(HC):
            p.op("gpsimd", "dma_start", out=wq[:, hc * 1024:(hc + 1) * 1024],
                 in_=d["Wq"][hc * 128:(hc + 1) * 128, :], inc=gd, inck=16)
        wq_ready = gd.n

        if "a0" not in phases:
            return st.n
        p.wait("sync", st, xbf_done)
        a0_pe0, a0_ve0 = pe.n, ve.n
        for tc in range(TC):
            g = tc % 2
            if tc >= 2:
                p.wait("sync", pe, a0_pe0 + tc - 1)
            for hc in range(HC):
                p.op("sync", "dma_start",
                     out=xts[g][:, hc * 512:(hc + 1) * 512],
                     in_=d["xbf"][tc * 512:(tc + 1) * 512, hc * 128:(hc + 1) * 128],
                     transpose=True, inc=ld, inck=16)
            p.wait("tensor", ld)
            if tc == 0:
                p.wait("tensor", gd, wq_ready)
            if tc >= 1:
                p.wait("tensor", ve, a0_ve0 + tc * 8)
            for hc in range(HC):
                for m in range(8):
                    p.op("tensor", "matmul", pq[m][:],
                         wq[:, hc * 1024 + m * 128: hc * 1024 + (m + 1) * 128],
                         xts[g][:, hc * 512:(hc + 1) * 512],
                         start=(hc == 0), stop=(hc == HC - 1),
                         inc=(pe if (hc == HC - 1 and m == 7) else None))
            p.wait("vector", pe)
            if tc == 0:
                p.wait("vector", ld, bq_ready)
            for m in range(8):
                p.op("vector", "tensor_scalar",
                     qt[:, m * 8192 + tc * 512: m * 8192 + (tc + 1) * 512],
                     pq[m][:], bqc[:, m:m + 1], None, ALU.add, inc=ve)

        pctx0.close()
        if "a0b" in phases:
            vst = [asb(f"A_vst{i}", [128, 1024], BF16) for i in range(2)]
            pctx1 = ExitStack()
            ptb = [pctx1.enter_context(nc.psum_tensor(f"A_ptb{k}", [128, 128], BF16))
                   for k in range(2)]
            b_ve0, b_st0 = ve.n, st.n
            p.wait("tensor", ve, ident_ready)
            p.wait("tensor", ve, b_ve0)
            nt_tr = 0
            for i in range(NT):
                for dc in range(8):
                    k = nt_tr % 2
                    if nt_tr >= 2:
                        p.wait("tensor", ve, b_ve0 + nt_tr - 1)
                    p.op("tensor", "transpose", ptb[k][:],
                         qt[:, dc * 8192 + i * 128: dc * 8192 + (i + 1) * 128],
                         identb[:], inc=pe)
                    p.wait("vector", pe)
                    if dc == 0 and i >= 2:
                        p.wait("vector", st, b_st0 + (i - 1) * 16)
                    p.op("vector", "tensor_copy",
                         vst[i % 2][:, dc * 128:(dc + 1) * 128], ptb[k][:], inc=ve)
                    nt_tr += 1
                p.wait("scalar", ve)
                p.op("scalar", "dma_start", out=d["vtok"][i * 128:(i + 1) * 128, :],
                     in_=vst[i % 2][:], inc=st, inck=16)
            pctx1.close()
        vtok_done = st.n

        if "a1" not in phases:
            return st.n
        esb = [asb(f"A_esb{i}", [128, 8192], BF16) for i in range(2)]
        a1tiny = asb("A_a1tiny", [128, 1], BF16)
        a1tiny2 = asb("A_a1tiny2", [128, 1])
        import os as _os2
        if _os2.environ.get("A1_SERIAL"):
            _p = aps("A_ps1s", [128, 512], F32)
            ps1 = [_p, _p]
        elif _os2.environ.get("A1_SPACED"):
            ps1 = []
            _dummies = []
            for k in range(2):
                ps1.append(aps(f"A_ps1{k}", [128, 512], F32))
                _dummies.append(aps(f"A_ps1d{k}", [128, 512], F32))
        else:
            ps1 = [aps(f"A_ps1{k}", [128, 512], F32) for k in range(2)]
        if _os.environ.get("A1_QTDRAM"):
            for c in range(HC):
                p.op("sync", "dma_start", out=qt[:, c * 8192:(c + 1) * 8192],
                     in_=d["qtdbg"][:, c * 8192:(c + 1) * 8192], inc=ld, inck=16)
            p.wait("tensor", ld)
        a1_se0, a1_ve0, a1_st0 = se.n, ve.n, st.n
        _a1n = int(_os.environ.get("A1N", NT))
        nsc = 0
        _a1sc = int(_os.environ.get("A1SC", TC))
        _a1hc = int(_os.environ.get("A1HC", HC))
        a1_ve_guard = ve.n
        se_after_group = {}
        for j in range(_a1n):
            for sc in range(_a1sc):
                k = nsc % 2
                if nsc < 2:
                    p.wait("tensor", ve, a1_ve_guard)
                if nsc >= 2:
                    p.wait("tensor", se, se_after_group[nsc - 2])
                for hc in range(_a1hc):
                    p.op("tensor", "matmul", ps1[k][:],
                         qt[:, hc * 8192 + j * 128: hc * 8192 + (j + 1) * 128],
                         qt[:, hc * 8192 + sc * 512: hc * 8192 + (sc + 1) * 512],
                         start=(hc == 0), stop=(hc == _a1hc - 1),
                         inc=(pe if hc == _a1hc - 1 else None))
                p.wait("scalar", pe)
                if sc == 0 and j >= 2 and not _os.environ.get("A1_NODMA"):
                    p.wait("scalar", st, a1_st0 + (j - 1) * 16)
                if sc == 0 and j >= 1:
                    p.wait("scalar", ve, a1_ve0 + j)
                p.op("scalar", "activation",
                     esb[j % 2][:, sc * 512:(sc + 1) * 512], ps1[k][:], AF.Exp,
                     scale=1.0 / 32.0, accum_out=dpart[:, sc:sc + 1], inc=se)
                se_after_group[nsc] = se.n
                nsc += 1
            p.op("scalar", "activation", a1tiny[:], esb[j % 2][:, 8191:8192], AF.Copy,
                 bias=0.0, scale=1.0)
            p.op("scalar", "activation", a1tiny2[:], dpart[:, _a1sc - 1:_a1sc], AF.Copy,
                 bias=0.0, scale=1.0, inc=se)
            p.wait("vector", se)
            p.op("vector", "tensor_reduce", dbuf[:, j:j + 1], dpart[:], AX.X, ALU.add,
                 inc=ve)
            if not _os.environ.get("A1_NODMA"):
                p.op("scalar", "dma_start", out=d["ebuf"][j, :, :], in_=esb[j % 2][:],
                     inc=st, inck=16)
        p.wait("vector", ve)
        p.op("vector", "reciprocal", dinv[:], dbuf[:], inc=ve)
        dinv_ready = ve.n
        a1_store_done = st.n
        a1_pe_done = pe.n

    if "a2" not in phases:
        return st.n
    with ExitStack() as actx:
        asb = lambda n_, s_, t_=F32: actx.enter_context(nc.sbuf_tensor(n_, s_, t_))
        aps = lambda n_, s_, t_=F32: actx.enter_context(nc.psum_tensor(n_, s_, t_))
        vsb = asb("A_vsb", [128, NT * 1024], BF16)
        es2 = [asb(f"A_es2{i}", [128, 512], BF16) for i in range(4)]
        po = [aps(f"A_po{m}", [128, 512]) for m in range(8)]
        oT = asb("A_oT", [128, 8 * 512])
        xres = [asb(f"A_xres{i}", [128, 1024]) for i in range(2)]
        xpo = asb("A_xpo", [128, 1024])
        xpo2 = asb("A_xpo2", [128, 1024])
        xout = [asb(f"A_xout{i}", [128, 1024]) for i in range(2)]

        p.wait("sync", st, vtok_done)
        p.wait("sync", pe, a1_pe_done)
        for j in range(NT):
            p.op("sync", "dma_start", out=vsb[:, j * 1024:(j + 1) * 1024],
                 in_=d["vtok"][j * 128:(j + 1) * 128, :], inc=ld, inck=16)
        v_loaded = ld.n
        p.wait("sync", st, a1_store_done)
        p.wait("vector", ve, dinv_ready)

        a2_st0 = st.n
        ne = 0
        pe_after_consume = {}
        epi_ve = 0
        xres_ve = {}
        sq_se = 0
        for qc in range(TC):
            for j in range(NT):
                sl = ne % 4
                if ne >= 4:
                    p.wait("sync", pe, pe_after_consume[ne - 4])
                p.op("sync", "dma_start", out=es2[sl][:],
                     in_=d["ebuf"][j, :, qc * 512:(qc + 1) * 512], inc=ld, inck=16)
                p.wait("tensor", ld)
                if j == 0:
                    p.wait("tensor", ld, v_loaded)
                    if qc > 0:
                        p.wait("tensor", ve, epi_ve)
                for dc in range(8):
                    p.op("tensor", "matmul", po[dc][:],
                         vsb[:, j * 1024 + dc * 128: j * 1024 + (dc + 1) * 128],
                         es2[sl][:],
                         start=(j == 0), stop=(j == NT - 1),
                         inc=(pe if dc == 7 else None))
                pe_after_consume[ne] = pe.n
                ne += 1
            p.wait("vector", pe)
            for dc in range(8):
                p.op("vector", "tensor_copy", oT[:, dc * 512:(dc + 1) * 512],
                     po[dc][:], inc=ve)
            evac_done = ve.n
            for tsub in range(4):
                tq = qc * 4 + tsub
                if tq >= 2:
                    p.wait("sync", ve, xres_ve[tq - 2])
                p.op("sync", "dma_start", out=xres[tq % 2][:],
                     in_=d["x0"][tq * 128:(tq + 1) * 128, :], inc=ld, inck=16)
                xres_ld = ld.n
                tr_ve0 = ve.n
                for dc in range(8):
                    bank, slot = dc % 2, (dc // 2) % 2
                    p.wait("tensor", ve, max(evac_done, tr_ve0 + dc - 1))
                    p.op("tensor", "transpose",
                         po[bank][:, slot * 128:(slot + 1) * 128],
                         oT[:, dc * 512 + tsub * 128: dc * 512 + (tsub + 1) * 128],
                         identf[:], inc=pe)
                    p.wait("vector", pe)
                    if dc == 0:
                        p.wait("vector", se, sq_se)
                    p.op("vector", "tensor_scalar", xpo[:, dc * 128:(dc + 1) * 128],
                         po[bank][:, slot * 128:(slot + 1) * 128],
                         dinv[:, tq:tq + 1], None, ALU.mult, inc=ve)
                p.wait("vector", ld, xres_ld)
                p.wait("vector", ve)
                p.op("vector", "scalar_tensor_tensor", xpo2[:], xres[tq % 2][:], 1.0,
                     xpo[:], ALU.mult, ALU.add, accum_out=kit.s1[:], inc=ve)
                xres_ve[tq] = ve.n
                if tq >= 2:
                    p.wait("vector", st, a2_st0 + (tq - 1) * 16)
                p.wait("scalar", ve)
                if _os.environ.get("A2_DBG") == "xpo2":
                    p.op("vector", "tensor_copy", xout[tq % 2][:], xpo2[:], inc=ve)
                    sq_se = se.n
                elif _os.environ.get("A2_DBG") == "attn":
                    p.op("vector", "tensor_copy", xout[tq % 2][:], xpo[:], inc=ve)
                    sq_se = se.n
                else:
                    emit_ln(p, kit, xpo2[:], xout[tq % 2][:], se, ve)
                    sq_se = se.n - 1
                epi_ve = ve.n
                p.wait("scalar", ve)
                p.op("scalar", "dma_start", out=d["xs1"][tq * 128:(tq + 1) * 128, :],
                     in_=xout[tq % 2][:], inc=st, inck=16)
        stageA_done = st.n
    return stageA_done


def emit_cast_pass(p, nc, ctx, d_src, d_dst, s, tag, nt=64, width=1024):
    ld, st, ve = s["ld"], s["st"], s["ve"]
    with ExitStack() as pctx:
        xin = [pctx.enter_context(nc.sbuf_tensor(f"{tag}_ci{i}", [128, width], F32))
               for i in range(2)]
        xbs = [pctx.enter_context(nc.sbuf_tensor(f"{tag}_cb{i}", [128, width], BF16))
               for i in range(2)]
        ve0, st0 = ve.n, st.n
        for i in range(nt):
            if i >= 2:
                p.wait("sync", ve, ve0 + i - 1)
            p.op("sync", "dma_start", out=xin[i % 2][:],
                 in_=d_src[i * 128:(i + 1) * 128, :], inc=ld, inck=16)
            p.wait("vector", ld)
            if i >= 2:
                p.wait("vector", st, st0 + (i - 1) * 16)
            p.op("vector", "tensor_copy", xbs[i % 2][:], xin[i % 2][:], inc=ve)
            p.wait("scalar", ve)
            p.op("scalar", "dma_start", out=d_dst[i * 128:(i + 1) * 128, :],
                 in_=xbs[i % 2][:], inc=st, inck=16)
    return st.n


def emit_wT_load(p, nc, w_dram, wT_sb, wtmp, ptw, identb, s, nchunk_r=8, nchunk_c=8,
                 ident_ready=0):
    gd, pe, ve = s["gd"], s["pe"], s["ve"]
    R = nchunk_r * 128
    C = nchunk_c * 128
    ntr = 0
    p.wait("tensor", ve, ident_ready)
    for rc in range(nchunk_r):
        p.op("gpsimd", "dma_start", out=wtmp[:],
             in_=w_dram[rc * 128:(rc + 1) * 128, :], inc=gd, inck=16)
        p.wait("tensor", gd)
        for cc in range(nchunk_c):
            k = ntr % 2
            if ntr >= 2:
                p.wait("tensor", ve)
            p.op("tensor", "transpose", ptw[k][:],
                 wtmp[:, cc * 128:(cc + 1) * 128], identb[:], inc=pe)
            p.wait("vector", pe)
            p.op("vector", "tensor_copy",
                 wT_sb[:, cc * R + rc * 128: cc * R + (rc + 1) * 128], ptw[k][:],
                 inc=ve)
            ntr += 1
        p.wait("gpsimd", ve)
    return



def build_stageC(nc, ctx, p, d, s):
    sb = lambda n_, s_, t_=F32: ctx.enter_context(nc.sbuf_tensor(n_, s_, t_))
    ld, st, gd = s["ld"], s["st"], s["gd"]
    pe, ve, se, gp = s["pe"], s["ve"], s["se"], s["gp"]
    NT, HC, TC = 64, 8, 16

    identf = sb("C_identf", [128, 128])
    identb = sb("C_identb", [128, 128], BF16)
    pmf = sb("C_pmf", [128, 128])
    ltm = sb("C_ltm", [128, 128])
    mskC = [sb(f"C_msk{r}", [128, 512]) for r in range(4)]
    scbuf = sb("C_scbuf", [128, 64])
    scmb = sb("C_scmb", [128, 64])
    biasDall = sb("C_biasDall", [128, 64])
    dpc = sb("C_dpc", [128, 18])
    dbufC = sb("C_dbufC", [128, 64])
    dinvC = sb("C_dinvC", [128, 64])
    smaxc = sb("C_smaxc", [128, 1])
    nsmaxc = sb("C_nsmaxc", [128, 1])
    ones1r = sb("C_ones1r", [1, 128])
    sm1sb = sb("C_sm1sb", [1, 1])
    scT = sb("C_scT", [64, 128])
    scrow = sb("C_scrow", [1, 8192])
    screp = sb("C_screp", [128, 8192])
    woT = sb("C_woT", [128, HC * 1024], BF16)
    kit = LNKit(sb, "C")
    ctiny = sb("C_ctiny", [128, 1], BF16)

    p.op("gpsimd", "iota", pmf[:], pattern=[[-1, 128]], base=0, channel_multiplier=1,
         allow_small_or_imprecise_dtypes=True)
    for r in range(4):
        p.op("gpsimd", "iota", mskC[r][:], pattern=[[1, 512]], base=-128 * r,
             channel_multiplier=-1, allow_small_or_imprecise_dtypes=True)
    p.op("gpsimd", "memset", ones1r[:], 1.0, inc=gp)
    p.wait("vector", gp)
    p.op("vector", "tensor_scalar", identf[:], pmf[:], 0.0, None, ALU.is_equal, inc=ve)
    p.wait("vector", ve)
    p.op("vector", "tensor_copy", identb[:], identf[:])
    p.op("vector", "tensor_scalar", ltm[:], pmf[:], 0.0, None, ALU.is_ge)
    for r in range(4):
        p.op("vector", "tensor_scalar", mskC[r][:], mskC[r][:], 0.0, None, ALU.is_ge)
    p.op("vector", "nop", inc=ve)
    ident_ready = ve.n

    x2bf_done = emit_cast_pass(p, nc, ctx, d["x2"], d["x2bf"], s, "C")

    with ExitStack() as cctx:
        csb = lambda n_, s_, t_=F32: cctx.enter_context(nc.sbuf_tensor(n_, s_, t_))
        cps = lambda n_, s_, t_=F32: cctx.enter_context(nc.psum_tensor(n_, s_, t_))
        wjT = csb("C_wjT", [128, HC * 1024], BF16)
        wjvT = csb("C_wjvT", [128, HC * 1024], BF16)
        wtmp = csb("C_wtmp", [128, 1024], BF16)
        xts = [csb(f"C_xts{g}", [128, HC * 512], BF16) for g in range(2)]
        vstg = [csb(f"C_vstg{i}", [128, 1024], BF16) for i in range(2)]
        scp = csb("C_scp", [128, 2])
        wctx = ExitStack()
        ptw = [wctx.enter_context(nc.psum_tensor(f"C_ptw{k}", [128, 128], BF16))
               for k in range(2)]
        p.wait("gpsimd", st, x2bf_done)
        emit_wT_load(p, nc, d["wj"], wjT, wtmp, ptw, identb, s, ident_ready=ident_ready)
        emit_wT_load(p, nc, d["wjv"], wjvT, wtmp, ptw, identb, s, ident_ready=ident_ready)
        emit_wT_load(p, nc, d["wo"], woT, wtmp, ptw, identb, s, ident_ready=ident_ready)
        wT_ready_ve = ve.n
        wctx.close()

        py = [cps(f"C_py{m}", [128, 512]) for m in range(8)]
        p.wait("sync", st, x2bf_done)
        st_i = {}
        pe_tc = {}
        se_tile = {}
        ve_tile = {}
        for tc in range(TC):
            g = tc % 2
            if tc >= 2:
                p.wait("sync", pe, pe_tc[tc - 2])
            for hc in range(HC):
                p.op("sync", "dma_start",
                     out=xts[g][:, hc * 512:(hc + 1) * 512],
                     in_=d["x2bf"][tc * 512:(tc + 1) * 512, hc * 128:(hc + 1) * 128],
                     transpose=True, inc=ld, inck=16)
            p.wait("tensor", ld)
            if tc == 0:
                p.wait("tensor", ve, wT_ready_ve)
            for i4 in range(4):
                i = tc * 4 + i4
                bg = 4 * (i % 2)
                if i >= 2:
                    p.wait("tensor", se, se_tile[i - 2])
                    p.wait("tensor", ve, ve_tile[i - 2])
                for hc in range(HC):
                    lhs = xts[g][:, hc * 512 + i4 * 128: hc * 512 + (i4 + 1) * 128]
                    for half in range(2):
                        p.op("tensor", "matmul", py[bg + half][:],
                             lhs, wjT[:, hc * 1024 + half * 512: hc * 1024 + (half + 1) * 512],
                             start=(hc == 0), stop=(hc == HC - 1))
                        p.op("tensor", "matmul", py[bg + 2 + half][:],
                             lhs, wjvT[:, hc * 1024 + half * 512: hc * 1024 + (half + 1) * 512],
                             start=(hc == 0), stop=(hc == HC - 1),
                             inc=(pe if (hc == HC - 1 and half == 1) else None))
                pe_tc[tc] = pe.n
                p.wait("scalar", pe)
                p.op("scalar", "activation", kit.scr[:, 0:512], py[bg][:], AF.Square,
                     accum_out=scp[:, 0:1])
                p.op("scalar", "activation", kit.scr[:, 0:512], py[bg + 1][:], AF.Square,
                     accum_out=scp[:, 1:2])
                p.op("scalar", "activation", ctiny[:], scp[:, 1:2], AF.Copy,
                     bias=0.0, scale=1.0, inc=se)
                se_tile[i] = se.n
                p.wait("vector", se)
                if i >= 2:
                    p.wait("vector", st, st_i[i - 2])
                p.op("vector", "tensor_scalar", scbuf[:, i:i + 1], scp[:, 0:1],
                     scp[:, 1:2], 1.0 / 32.0, ALU.add, ALU.mult)
                p.op("vector", "tensor_copy", vstg[i % 2][:, 0:512], py[bg + 2][:])
                p.op("vector", "tensor_copy", vstg[i % 2][:, 512:1024], py[bg + 3][:],
                     inc=ve)
                ve_tile[i] = ve.n
                p.wait("scalar", ve)
                p.op("scalar", "dma_start", out=d["vtokC"][i * 128:(i + 1) * 128, :],
                     in_=vstg[i % 2][:], inc=st, inck=16)
                st_i[i] = st.n
        vtokC_done = st.n

    with ExitStack() as cctx:
        csb = lambda n_, s_, t_=F32: cctx.enter_context(nc.sbuf_tensor(n_, s_, t_))
        cps = lambda n_, s_, t_=F32: cctx.enter_context(nc.psum_tensor(n_, s_, t_))
        pctx = ExitStack()
        ptf = pctx.enter_context(nc.psum_tensor("C_ptf", [128, 512], F32))
        p.wait("vector", ve)
        p.op("vector", "tensor_reduce", smaxc[:], scbuf[:], AX.X, ALU.max, inc=ve)
        p.wait("tensor", ve)
        p.op("tensor", "transpose", ptf[0:1, 0:128], smaxc[:], identf[:], inc=pe)
        p.wait("vector", pe)
        p.op("vector", "tensor_reduce", sm1sb[:], ptf[0:1, 0:128], AX.X, ALU.max,
             inc=ve)
        p.wait("tensor", ve)
        p.op("tensor", "transpose", ptf[0:64, 0:128], scbuf[:, 0:64], identf[:], inc=pe)
        p.wait("vector", pe)
        p.op("vector", "tensor_copy", scT[:], ptf[0:64, 0:128], inc=ve)
        p.wait("scalar", ve)
        p.op("scalar", "dma_start", out=d["scd"][:, :], in_=scT[:], inc=st, inck=16)
        p.wait("sync", st)
        p.op("sync", "dma_start", out=scrow[:],
             in_=d["scd"].rearrange("(o a) b -> o (a b)", o=1), inc=ld, inck=16)
        p.wait("tensor", ld)
        p.wait("tensor", ve)
        for c in range(16):
            p.op("tensor", "matmul", ptf[:],
                 ones1r[:], scrow[0:1, c * 512:(c + 1) * 512],
                 start=True, stop=True, inc=pe)
            p.wait("vector", pe)
            p.op("vector", "tensor_copy", screp[:, c * 512:(c + 1) * 512], ptf[:],
                 inc=ve)
            p.wait("tensor", ve)
        p.op("tensor", "matmul", ptf[:, 0:1], ones1r[:], sm1sb[:],
             start=True, stop=True, inc=pe)
        p.wait("vector", pe)
        p.op("vector", "tensor_copy", smaxc[:], ptf[:, 0:1], inc=ve)
        p.wait("vector", ve)
        p.op("vector", "tensor_scalar", nsmaxc[:], smaxc[:], -1.0, None, ALU.mult)
        p.op("vector", "tensor_scalar", scmb[:], scbuf[:], smaxc[:, 0:1], None,
             ALU.subtract)
        p.op("vector", "tensor_scalar", biasDall[:], scbuf[:], nsmaxc[:, 0:1], None,
             ALU.mult, inc=ve)
        prep_ve = ve.n

        edia = csb("C_edia", [128, 128])
        dscr = kit.scr
        dve_i = 0
        p.wait("scalar", ve, prep_ve)
        for i in range(NT):
            fc = (i * 128) // 512
            rm = i * 128 - fc * 512
            if i >= 1:
                p.wait("scalar", ve)
            p.op("vector", "memset", dpc[:], 0.0, inc=ve)
            p.wait("scalar", ve)
            ncol = 0
            for c in range(fc):
                p.op("scalar", "activation", dscr[:, 0:512],
                     screp[:, c * 512:(c + 1) * 512], AF.Exp,
                     scale=scbuf[:, i:i + 1], bias=biasDall[:, i:i + 1],
                     accum_out=dpc[:, ncol:ncol + 1])
                ncol += 1
            if rm > 0:
                p.op("scalar", "activation", dscr[:, 0:rm],
                     screp[:, fc * 512:fc * 512 + rm], AF.Exp,
                     scale=scbuf[:, i:i + 1], bias=biasDall[:, i:i + 1],
                     accum_out=dpc[:, ncol:ncol + 1])
                ncol += 1
            p.op("scalar", "activation", edia[:],
                 screp[:, i * 128:(i + 1) * 128], AF.Exp,
                 scale=scbuf[:, i:i + 1], bias=biasDall[:, i:i + 1])
            p.op("scalar", "activation", ctiny[:], edia[:, 127:128], AF.Copy,
                 bias=0.0, scale=1.0, inc=se)
            p.wait("vector", se)
            p.op("vector", "tensor_tensor", edia[:], edia[:], ltm[:], ALU.mult, inc=ve)
            p.wait("vector", ve)
            p.op("vector", "tensor_reduce", kit.s1[:], edia[:], AX.X, ALU.add, inc=ve)
            p.wait("vector", ve)
            p.op("vector", "tensor_reduce", kit.s2[:], dpc[:], AX.X, ALU.add, inc=ve)
            p.wait("vector", ve)
            p.op("vector", "tensor_tensor", dbufC[:, i:i + 1], kit.s1[:], kit.s2[:],
                 ALU.add, inc=ve)
        p.wait("vector", ve)
        p.op("vector", "reciprocal", dinvC[:], dbufC[:], inc=ve)
        dinv_ready = ve.n

        pctx.close()
        esbC = [csb(f"C_esb{k}", [128, 512], BF16) for k in range(2)]
        vstr = [csb(f"C_vstr{k}", [128, 1024], BF16) for k in range(4)]
        oTC = csb("C_oTC", [128, 8 * 512], BF16)
        xpoC = csb("C_xpoC", [128, 1024])
        xpo2C = csb("C_xpo2C", [128, 1024])
        xresC = [csb(f"C_xresC{k}", [128, 1024]) for k in range(2)]
        xoutC = [csb(f"C_xoutC{k}", [128, 1024]) for k in range(2)]
        poC = [cps(f"C_poC{m}", [128, 512]) for m in range(8)]
        p.wait("sync", st, vtokC_done)
        nv = 0
        pe_v = {}
        se_et = {}
        c2_st0 = st.n
        epi_ve = 0
        xres_ve = {}
        for qc in range(TC):
            jmax = qc * 4 + 4
            for j in range(jmax):
                slv = nv % 4
                if nv >= 4:
                    p.wait("sync", pe, pe_v[nv - 4])
                p.op("sync", "dma_start", out=vstr[slv][:],
                     in_=d["vtokC"][j * 128:(j + 1) * 128, :], inc=ld, inck=16)
                ke = j % 2
                if j >= 2 or qc > 0:
                    p.wait("scalar", pe)
                p.wait("scalar", ve, dinv_ready)
                p.op("scalar", "activation", esbC[ke][:],
                     screp[:, qc * 512:(qc + 1) * 512], AF.Exp,
                     scale=scmb[:, j:j + 1])
                p.op("scalar", "activation", ctiny[:], esbC[ke][:, 511:512], AF.Copy,
                     bias=0.0, scale=1.0, inc=se)
                se_et[(qc, j)] = se.n
                if j >= qc * 4:
                    r = j - qc * 4
                    p.wait("vector", se)
                    p.op("vector", "tensor_tensor", esbC[ke][:], esbC[ke][:],
                         mskC[r][:], ALU.mult, inc=ve)
                    p.wait("tensor", ve)
                else:
                    p.wait("tensor", se)
                p.wait("tensor", ld)
                if qc == 0 and j == 0:
                    pass
                if j == 0 and qc > 0:
                    p.wait("tensor", ve, epi_ve)
                for dc in range(8):
                    p.op("tensor", "matmul", poC[dc][:],
                         vstr[slv][:, dc * 128:(dc + 1) * 128],
                         esbC[ke][:],
                         start=(j == 0), stop=(j == jmax - 1),
                         inc=(pe if dc == 7 else None))
                pe_v[nv] = pe.n
                nv += 1
            p.wait("vector", pe)
            for dc in range(8):
                p.op("vector", "tensor_copy", oTC[:, dc * 512:(dc + 1) * 512],
                     poC[dc][:], inc=ve)
            evac_done = ve.n
            for tsub in range(4):
                tq = qc * 4 + tsub
                if tq >= 2:
                    p.wait("sync", ve, xres_ve[tq - 2])
                p.op("sync", "dma_start", out=xresC[tq % 2][:],
                     in_=d["x2"][tq * 128:(tq + 1) * 128, :], inc=ld, inck=16)
                xres_ld = ld.n
                p.wait("tensor", ve, evac_done if tsub == 0 else xres_ve[tq - 1])
                for hc2 in range(HC):
                    for half in range(2):
                        p.op("tensor", "matmul", poC[half][:],
                             oTC[:, hc2 * 512 + tsub * 128: hc2 * 512 + (tsub + 1) * 128],
                             woT[:, hc2 * 1024 + half * 512: hc2 * 1024 + (half + 1) * 512],
                             start=(hc2 == 0), stop=(hc2 == HC - 1),
                             inc=(pe if (hc2 == HC - 1 and half == 1) else None))
                p.wait("vector", pe)
                for half in range(2):
                    p.op("vector", "tensor_scalar",
                         xpoC[:, half * 512:(half + 1) * 512], poC[half][:],
                         dinvC[:, tq:tq + 1], None, ALU.mult,
                         inc=(ve if half == 1 else None))
                p.wait("vector", ld, xres_ld)
                p.wait("vector", ve)
                p.op("vector", "scalar_tensor_tensor", xpo2C[:], xresC[tq % 2][:], 1.0,
                     xpoC[:], ALU.mult, ALU.add, accum_out=kit.s1[:], inc=ve)
                xres_ve[tq] = ve.n
                if tq >= 2:
                    p.wait("vector", st, c2_st0 + (tq - 1) * 16)
                emit_ln(p, kit, xpo2C[:], xoutC[tq % 2][:], se, ve)
                epi_ve = ve.n
                p.wait("scalar", ve)
                p.op("scalar", "dma_start", out=d["xs3"][tq * 128:(tq + 1) * 128, :],
                     in_=xoutC[tq % 2][:], inc=st, inck=16)
        stageC_done = st.n
    return stageC_done


def emit_ln_affine(p, kit, xpo, out_ap, se, ve, lgrep=None, lbrep=None,
                   eps=1e-5, n=1024.0):
    p.wait("vector", ve)
    p.op("vector", "tensor_scalar", kit.mu[:], kit.s1[:], 1.0 / n, None, ALU.mult,
         inc=ve)
    p.wait("vector", ve)
    p.op("vector", "tensor_scalar", kit.xc[:], xpo, kit.mu[:], None, ALU.subtract,
         inc=ve)
    p.wait("scalar", ve)
    p.op("scalar", "activation", kit.scr[:], kit.xc[:], AF.Square, accum_out=kit.s2[:])
    p.op("scalar", "activation", kit.tiny[:], kit.s2[:], AF.Copy, bias=0.0, scale=1.0,
         inc=se)
    p.wait("vector", se)
    p.op("vector", "tensor_scalar", kit.var[:], kit.s2[:], 1.0 / n, eps, ALU.mult,
         ALU.add, inc=ve)
    p.wait("scalar", ve)
    p.op("scalar", "activation", kit.sd[:], kit.var[:], AF.Sqrt)
    p.op("scalar", "activation", kit.tiny[:], kit.sd[:], AF.Copy, bias=0.0, scale=1.0,
         inc=se)
    p.wait("vector", se)
    p.op("vector", "reciprocal", kit.rstd[:], kit.sd[:], inc=ve)
    p.wait("vector", ve)
    if lgrep is None:
        p.op("vector", "tensor_scalar", out_ap, kit.xc[:], kit.rstd[:], None, ALU.mult,
             inc=ve)
    else:
        p.op("vector", "scalar_tensor_tensor", kit.scr[:], kit.xc[:], kit.rstd[:],
             lgrep, ALU.mult, ALU.mult, inc=ve)
        p.wait("vector", ve)
        p.op("vector", "tensor_tensor", out_ap, kit.scr[:], lbrep, ALU.add, inc=ve)


def emit_replicate_row(p, nc, row_ap, out_sb, ptf, ones1r, s, width):
    pe, ve = s["pe"], s["ve"]
    p.wait("tensor", ve)
    p.op("tensor", "matmul", ptf[:, 0:width], ones1r[:], row_ap, start=True, stop=True,
         inc=pe)
    p.wait("vector", pe)
    p.op("vector", "tensor_copy", out_sb, ptf[:, 0:width], inc=ve)


def build_stageD(nc, ctx, p, d, s):
    sb = lambda n_, s_, t_=F32: ctx.enter_context(nc.sbuf_tensor(n_, s_, t_))
    ld, st, gd = s["ld"], s["st"], s["gd"]
    pe, ve, se, gp = s["pe"], s["ve"], s["se"], s["gp"]
    NT, HC, TC = 64, 8, 16

    identf = sb("D_identf", [128, 128])
    identb = sb("D_identb", [128, 128], BF16)
    pmf = sb("D_pmf", [128, 128])
    mskD = [sb(f"D_msk{r}", [128, 512]) for r in range(8)]
    ones1r = sb("D_ones1r", [1, 128])
    onescol = sb("D_onescol", [128, 1], BF16)
    kit = LNKit(sb, "D")
    dtiny = sb("D_tiny", [128, 1], BF16)

    p.op("gpsimd", "iota", pmf[:], pattern=[[-1, 128]], base=0, channel_multiplier=1,
         allow_small_or_imprecise_dtypes=True)
    p.op("gpsimd", "memset", ones1r[:], 1.0)
    p.op("gpsimd", "memset", onescol[:], 1.0, inc=gp)
    p.wait("vector", gp)
    p.op("vector", "tensor_scalar", identf[:], pmf[:], 0.0, None, ALU.is_equal, inc=ve)
    p.wait("vector", ve)
    p.op("vector", "tensor_copy", identb[:], identf[:])
    for r in range(8):
        p.op("gpsimd", "iota", kit.scr[:, 0:512], pattern=[[1, 512]],
             base=512 - 128 * r, channel_multiplier=-1,
             allow_small_or_imprecise_dtypes=True, inc=gp)
        p.wait("vector", gp)
        p.op("vector", "tensor_scalar", kit.xc[:, 0:512], kit.scr[:, 0:512], 0.0, None,
             ALU.is_ge, inc=ve)
        p.wait("vector", ve)
        p.op("vector", "tensor_scalar", kit.scr[:, 0:512], kit.scr[:, 0:512], 512.0,
             None, ALU.is_le, inc=ve)
        p.wait("vector", ve)
        p.op("vector", "tensor_tensor", mskD[r][:], kit.xc[:, 0:512],
             kit.scr[:, 0:512], ALU.mult, inc=ve)
        p.wait("gpsimd", ve)
    ident_ready = ve.n

    import os as _os
    _dph = set(_os.environ.get("DPH", "d12,d3a,d3b,d3c,fin").split(","))
    _dnl = int(_os.environ.get("DNL", 2))
    zin = d["x3"]
    for l in range(_dnl):
        p.wait("sync", st)
        zbf_done = emit_cast_pass(p, nc, ctx, zin, d["zbf"], s, f"D{l}")
        if "d12" not in _dph:
            return st.n
        with ExitStack() as dctx:
            dsb = lambda n_, s_, t_=F32: dctx.enter_context(nc.sbuf_tensor(f"{n_}_{l}", s_, t_))
            dps = lambda n_, s_, t_=F32: dctx.enter_context(nc.psum_tensor(f"{n_}_{l}", s_, t_))
            wqkvsb = dsb("D_wqkv", [128, HC * 1536], BF16)
            wosb = dsb("D_wosb", [128, 4 * 1024], BF16)
            qsrow = dsb("D_qsrow", [1, 512])
            ksrow = dsb("D_ksrow", [1, 512])
            qsrep = dsb("D_qsrep", [128, 512])
            ksrep = dsb("D_ksrep", [128, 512])
            ush = [[dsb(f"D_ush{g}_{hc}", [128, 512], BF16) for hc in range(HC)]
                   for g in range(2)]
            shs = [[dsb(f"D_shs{g}_{hc}", [128, 512], BF16) for hc in range(4)]
                   for g in range(2)]
            qwork = dsb("D_qwork", [128, 512])
            kwork = dsb("D_kwork", [128, 512])
            sqs = dsb("D_sqs", [128, 8])
            sqs2 = dsb("D_sqs2", [128, 8])
            rwork = dsb("D_rwork", [128, 512])
            t1w = dsb("D_t1w", [128, 512])
            t2w = dsb("D_t2w", [128, 512])
            qfb = dsb("D_qfb", [128, 512], BF16)
            kfb = dsb("D_kfb", [128, 512], BF16)
            tblq = [dsb(f"D_tblq{i}", [128, 64 * 4]) for i in range(2)]
            vring = [dsb(f"D_vring{i}", [128, 512], BF16) for i in range(8)]
            kTr = dsb("D_kTr", [64, 8 * 1024], BF16)
            qTw = dsb("D_qTw", [64, 8 * 512], BF16)
            esbD = [dsb(f"D_esbD{i}", [128, 512], BF16) for i in range(2)]
            invDsb = dsb("D_invDsb", [1, 512])
            rinvsb = dsb("D_rinvsb", [64, 512])
            aT = dsb("D_aT", [128, 4 * 512], BF16)
            xw = dsb("D_xw", [128, 1024])
            zres = [dsb(f"D_zres{i}", [128, 1024]) for i in range(2)]
            z2t = [dsb(f"D_z2t{i}", [128, 1024]) for i in range(2)]
            pq = [dps(f"D_pq{m}", [128, 512]) for m in range(6)]
            ptb = [dps(f"D_ptb{k}", [128, 128], BF16) for k in range(2)]

            p.wait("gpsimd", st, zbf_done)
            p.wait("sync", st, zbf_done)
            p.wait("vector", st, zbf_done)
            for hc in range(HC):
                p.op("gpsimd", "dma_start", out=wqkvsb[:, hc * 1536:(hc + 1) * 1536],
                     in_=d["wqkv"][l, hc * 128:(hc + 1) * 128, :], inc=gd, inck=16)
            for b4 in range(4):
                p.op("gpsimd", "dma_start", out=wosb[:, b4 * 1024:(b4 + 1) * 1024],
                     in_=d["wo_l"][l, b4 * 128:(b4 + 1) * 128, :], inc=gd, inck=16)
            w_ready = gd.n
            p.op("sync", "dma_start", out=qsrow[:], in_=d["qs_t"][l:l + 1, :],
                 inc=ld, inck=16)
            p.op("sync", "dma_start", out=ksrow[:], in_=d["ks_t"][l:l + 1, :],
                 inc=ld, inck=16)
            p.wait("tensor", ld)
            emit_replicate_row(p, nc, qsrow[:], qsrep[:], pq[0], ones1r, s, 512)
            emit_replicate_row(p, nc, ksrow[:], ksrep[:], pq[0], ones1r, s, 512)
            rep_ready = ve.n

            p.wait("sync", st, zbf_done)
            ve_slab = {}
            ve_rot = {}
            pe_tile = {}
            ve_evac = {}
            se_l2 = {}
            ve_attn_done = 0
            st_z2 = {}
            _d12tc = int(_os.environ.get("D12TC", TC))
            for tc in range(_d12tc):
                g = tc % 2
                if tc >= 2:
                    p.wait("sync", pe, pe_tile[(tc - 2, 3)])
                for hc in range(HC):
                    p.op("sync", "dma_start", out=ush[g][hc][:],
                         in_=d["zbf"][tc * 512:(tc + 1) * 512,
                                      hc * 128:(hc + 1) * 128],
                         transpose=True, inc=ld, inck=16)
                slab_ld = ld.n
                p.wait("vector", ld, slab_ld)
                if tc >= 2:
                    p.wait("vector", pe, pe_tile[(tc - 2, 3)])
                for hc in range(4):
                    if tc == 0:
                        p.op("vector", "memset", shs[g][hc][:, 0:1], 0.0)
                    else:
                        p.op("vector", "tensor_copy", shs[g][hc][:, 0:1],
                             ush[g ^ 1][hc + 4][:, 511:512])
                    p.op("vector", "tensor_copy", shs[g][hc][:, 1:512],
                         ush[g][hc + 4][:, 0:511],
                         inc=(ve if hc == 3 else None))
                ve_slab[tc] = ve.n

                for i4 in range(4):
                    i = tc * 4 + i4
                    bq3 = 3 * (i % 2)
                    if i >= 2:
                        p.wait("sync", ve, ve_rot.get(i - 2, 0))
                    p.op("sync", "dma_start", out=tblq[i % 2][:, 0:64],
                         in_=d["cq"][i * 128:(i + 1) * 128, :], inc=ld, inck=16)
                    p.op("sync", "dma_start", out=tblq[i % 2][:, 64:128],
                         in_=d["sq"][i * 128:(i + 1) * 128, :], inc=ld, inck=16)
                    p.op("sync", "dma_start", out=tblq[i % 2][:, 128:192],
                         in_=d["ck"][i * 128:(i + 1) * 128, :], inc=ld, inck=16)
                    p.op("sync", "dma_start", out=tblq[i % 2][:, 192:256],
                         in_=d["sk"][i * 128:(i + 1) * 128, :], inc=ld, inck=16)
                    tbl_ld = ld.n
                    p.wait("tensor", ve, ve_slab[tc])
                    if i == 0:
                        p.wait("tensor", gd, w_ready)
                        p.wait("tensor", ve, max(ident_ready, rep_ready))
                    if i >= 2:
                        p.wait("tensor", ve, ve_evac.get(i - 2, 0))
                        p.wait("tensor", se, se_l2.get(i - 2, 0))
                    if i4 == 0 and ve_attn_done:
                        p.wait("tensor", ve, ve_attn_done)
                        p.wait("tensor", se)
                    for hc in range(HC):
                        lhs = (ush[g][hc] if hc < 4 else shs[g][hc - 4])
                        lhs = lhs[:, i4 * 128:(i4 + 1) * 128]
                        for part in range(3):
                            p.op("tensor", "matmul", pq[bq3 + part][:],
                                 lhs,
                                 wqkvsb[:, hc * 1536 + part * 512:
                                        hc * 1536 + (part + 1) * 512],
                                 start=(hc == 0), stop=(hc == HC - 1),
                                 inc=(pe if (hc == HC - 1 and part == 2) else None))
                    pe_tile[(tc, i4)] = pe.n
                    p.wait("vector", pe)
                    p.op("vector", "tensor_copy", vring[i % 8][:], pq[bq3 + 2][:])
                    p.op("vector", "tensor_copy", qwork[:], pq[bq3][:])
                    p.op("vector", "tensor_copy", kwork[:], pq[bq3 + 1][:], inc=ve)
                    ve_evac[i] = ve.n
                    for which, work, srep, ct_off, st_off, outb in (
                            ("q", qwork, qsrep, 0, 64, qfb),
                            ("k", kwork, ksrep, 128, 192, kfb)):
                        w3 = work[:].rearrange("p (h e) -> p h e", e=64)
                        p.wait("vector", ve)
                        p.op("vector", "tensor_tensor",
                             rwork[:].rearrange("p (h e) -> p h e", e=64), w3, w3,
                             ALU.mult, inc=ve)
                        p.wait("vector", ve)
                        p.op("vector", "tensor_reduce", sqs[:],
                             rwork[:].rearrange("p (h e) -> p h e", e=64), AX.X,
                             ALU.add, inc=ve)
                        p.wait("scalar", ve)
                        p.op("scalar", "activation", sqs2[:], sqs[:], AF.Sqrt)
                        p.op("scalar", "activation", dtiny[:], sqs2[:, 7:8], AF.Copy,
                             bias=0.0, scale=1.0, inc=se)
                        p.wait("vector", se)
                        p.op("vector", "tensor_scalar", sqs2[:], sqs2[:], 1e-12, None,
                             ALU.max, inc=ve)
                        p.wait("vector", ve)
                        p.op("vector", "reciprocal", sqs2[:], sqs2[:], inc=ve)
                        p.wait("vector", ve)
                        sqb = sqs2[:].rearrange("p (h one) -> p h one", one=1) \
                            .broadcast_to([128, 8, 64])
                        p.op("vector", "tensor_tensor",
                             work[:].rearrange("p (h e) -> p h e", e=64), w3, sqb,
                             ALU.mult, inc=ve)
                        p.wait("vector", ve)
                        p.op("vector", "tensor_tensor", work[:], work[:], srep[:],
                             ALU.mult, inc=ve)
                        p.wait("vector", ve)
                        w4 = work[:].rearrange("p (h two e) -> p h two e", two=2, e=32)
                        r4 = rwork[:].rearrange("p (h two e) -> p h two e", two=2, e=32)
                        p.op("vector", "tensor_scalar", r4[:, :, 0:1, :],
                             w4[:, :, 1:2, :], -1.0, None, ALU.mult)
                        p.op("vector", "tensor_scalar", r4[:, :, 1:2, :],
                             w4[:, :, 0:1, :], 1.0, None, ALU.mult, inc=ve)
                        p.wait("vector", ld, tbl_ld)
                        p.wait("vector", ve)
                        ctb = tblq[i % 2][:, ct_off:ct_off + 64] \
                            .rearrange("p (one e) -> p one e", one=1) \
                            .broadcast_to([128, 8, 64])
                        stb = tblq[i % 2][:, st_off:st_off + 64] \
                            .rearrange("p (one e) -> p one e", one=1) \
                            .broadcast_to([128, 8, 64])
                        p.op("vector", "tensor_tensor",
                             t1w[:].rearrange("p (h e) -> p h e", e=64),
                             work[:].rearrange("p (h e) -> p h e", e=64), ctb,
                             ALU.mult)
                        p.op("vector", "tensor_tensor",
                             t2w[:].rearrange("p (h e) -> p h e", e=64),
                             rwork[:].rearrange("p (h e) -> p h e", e=64), stb,
                             ALU.mult, inc=ve)
                        p.wait("vector", ve)
                        p.op("vector", "tensor_tensor", outb[:], t1w[:], t2w[:],
                             ALU.add, inc=ve)
                    se_l2[i] = se.n
                    ve_rot[i] = ve.n
                    if "dbgD" in d and i == int(_os.environ.get("DDUMPI", -1)):
                        p.wait("scalar", ve)
                        p.op("scalar", "dma_start", out=d["dbgD"][0:128, 0:512],
                             in_=ush[g][0][:], inc=st, inck=16)
                        p.op("scalar", "dma_start", out=d["dbgD"][0:128, 512:1024],
                             in_=shs[g][0][:], inc=st, inck=16)
                        p.op("scalar", "dma_start", out=d["dbgD"][0:128, 1024:1536],
                             in_=wqkvsb[:, 0:512], inc=st, inck=16)
                        p.op("scalar", "dma_start", out=d["dbgD"][0:128, 1536:2048],
                             in_=vring[i % 8][:], inc=st, inck=16)
                    ntr0 = 0
                    for which, src, dstbase in (("q", qfb, None), ("k", kfb, None)):
                        for hh in range(8):
                            kk = ntr0 % 2
                            p.wait("tensor", ve)
                            p.op("tensor", "transpose", ptb[kk][0:64, 0:128],
                                 src[:, hh * 64:(hh + 1) * 64], identb[:], inc=pe)
                            p.wait("vector", pe)
                            if which == "q":
                                dst = qTw[0:64, hh * 512 + i4 * 128:
                                          hh * 512 + (i4 + 1) * 128]
                            else:
                                dst = kTr[0:64, hh * 1024 + (i % 8) * 128:
                                          hh * 1024 + ((i % 8) + 1) * 128]
                            p.op("vector", "tensor_copy", dst, ptb[kk][0:64, 0:128],
                                 inc=ve)
                            ntr0 += 1
                if _os.environ.get("D12SUB") == "qkv":
                    continue
                r0 = 4 if tc == 0 else 0
                qkv_done_ve = ve.n
                _skipattn = _os.environ.get("D12SUB") == "skipattn"
                psT = [pq[0], pq[1]]
                poW = pq[2]
                pD = pq[3]
                pR = pq[4]
                for hh in range(0 if _skipattn else 8):
                    if hh == 0:
                        p.wait("tensor", ve, qkv_done_ve)
                    first = True
                    for r in range(r0, 8):
                        gt = (tc - 1) * 4 + r
                        rp = gt % 8
                        kk = r % 2
                        p.wait("tensor", se)
                        p.op("tensor", "matmul", psT[kk][:],
                             kTr[0:64, hh * 1024 + rp * 128:hh * 1024 + (rp + 1) * 128],
                             qTw[0:64, hh * 512:(hh + 1) * 512],
                             start=True, stop=True, inc=pe)
                        p.wait("scalar", pe)
                        p.op("scalar", "activation", esbD[kk][:], psT[kk][:], AF.Exp,
                             scale=8.0)
                        p.op("scalar", "activation", dtiny[:], esbD[kk][:, 511:512],
                             AF.Copy, bias=0.0, scale=1.0, inc=se)
                        p.wait("vector", se)
                        p.op("vector", "tensor_tensor", esbD[kk][:], esbD[kk][:],
                             mskD[r][:], ALU.mult, inc=ve)
                        p.wait("tensor", ve)
                        p.op("tensor", "matmul", poW[0:64, :],
                             vring[rp][:, hh * 64:(hh + 1) * 64], esbD[kk][:],
                             start=first, stop=(r == 7))
                        p.op("tensor", "matmul", pD[0:1, :],
                             onescol[:], esbD[kk][:],
                             start=first, stop=(r == 7),
                             inc=(pe if r == 7 else None))
                        first = False
                    p.wait("vector", pe)
                    p.op("vector", "reciprocal", invDsb[:], pD[0:1, :], inc=ve)
                    p.wait("tensor", ve)
                    p.op("tensor", "matmul", pR[0:64, :], ones1r[0:1, 0:64],
                         invDsb[:], start=True, stop=True, inc=pe)
                    p.wait("vector", pe)
                    p.op("vector", "tensor_copy", rinvsb[:], pR[0:64, :], inc=ve)
                    p.wait("vector", ve)
                    p.op("vector", "tensor_tensor",
                         aT[(hh % 2) * 64:(hh % 2) * 64 + 64,
                            (hh // 2) * 512:(hh // 2 + 1) * 512],
                         poW[0:64, :], rinvsb[:], ALU.mult, inc=ve)
                ve_attn = ve.n if not _skipattn else qkv_done_ve
                if _os.environ.get("D12SUB") == "attn":
                    continue
                pwo = [pq[4], pq[5]]
                _dwo = _os.environ.get("D12WO", "")
                for i4 in range(4):
                    i = tc * 4 + i4
                    if i >= 2:
                        p.wait("sync", ve)
                    p.op("sync", "dma_start", out=zres[i % 2][:],
                         in_=zin[i * 128:(i + 1) * 128, :], inc=ld, inck=16)
                    zld = ld.n
                    p.wait("tensor", ve, max(ve_attn, ve_attn_done))
                    if _dwo != "nomm":
                        for hp in range(4):
                            for half in range(2):
                                p.op("tensor", "matmul", pwo[half][:],
                                     aT[:, hp * 512 + i4 * 128:hp * 512 + (i4 + 1) * 128],
                                     wosb[:, hp * 1024 + half * 512:
                                          hp * 1024 + (half + 1) * 512],
                                     start=(hp == 0), stop=(hp == 3),
                                     inc=(pe if (hp == 3 and half == 1) else None))
                        p.wait("vector", pe)
                    p.wait("vector", ld, zld)
                    if i >= 2 and _dwo != "nostore":
                        p.wait("vector", st, st_z2[i - 2])
                    for half in range(2):
                        if _dwo == "nostt":
                            p.op("vector", "tensor_copy",
                                 z2t[i % 2][:, half * 512:(half + 1) * 512],
                                 zres[i % 2][:, half * 512:(half + 1) * 512],
                                 inc=(ve if half == 1 else None))
                        else:
                            p.op("vector", "scalar_tensor_tensor",
                                 z2t[i % 2][:, half * 512:(half + 1) * 512],
                                 zres[i % 2][:, half * 512:(half + 1) * 512], 1.0,
                                 pwo[half][:], ALU.mult, ALU.add,
                                 inc=(ve if half == 1 else None))
                    ve_attn_done = ve.n
                    p.wait("scalar", ve)
                    if _dwo != "nostore":
                        p.op("scalar", "dma_start",
                             out=d["z2"][i * 128:(i + 1) * 128, :], in_=z2t[i % 2][:],
                             inc=st, inck=16)
                        st_z2[i] = st.n
            d12_done = st.n

        if "d3a" not in _dph:
            return st.n
        with ExitStack() as dctx:
            dsb = lambda n_, s_, t_=F32: dctx.enter_context(nc.sbuf_tensor(f"{n_}_{l}", s_, t_))
            dps = lambda n_, s_, t_=F32: dctx.enter_context(nc.psum_tensor(f"{n_}_{l}", s_, t_))
            lgrow = dsb("D_lgrow", [1, 1024])
            lbrow = dsb("D_lbrow", [1, 1024])
            lgrep = dsb("D_lgrep", [128, 1024])
            lbrep = dsb("D_lbrep", [128, 1024])
            htile = [dsb(f"D_ht{i}", [128, 1024]) for i in range(2)]
            hnstage = [dsb(f"D_hn{i}", [128, 1152], BF16) for i in range(2)]
            ptf2 = dps("D_ptf2", [128, 512])
            p.wait("sync", st, d12_done)
            p.wait("vector", st, d12_done)
            p.op("sync", "dma_start", out=lgrow[:], in_=d["lng"][l:l + 1, :],
                 inc=ld, inck=16)
            p.op("sync", "dma_start", out=lbrow[:], in_=d["lnb"][l:l + 1, :],
                 inc=ld, inck=16)
            p.wait("tensor", ld)
            for half in range(2):
                emit_replicate_row(p, nc, lgrow[0:1, half * 512:(half + 1) * 512],
                                   lgrep[:, half * 512:(half + 1) * 512], ptf2,
                                   ones1r, s, 512)
                emit_replicate_row(p, nc, lbrow[0:1, half * 512:(half + 1) * 512],
                                   lbrep[:, half * 512:(half + 1) * 512], ptf2,
                                   ones1r, s, 512)
            p.op("vector", "memset", hnstage[0][:, 1024:1152], 0.0)
            p.op("vector", "memset", hnstage[1][:, 1024:1152], 0.0)
            p.op("vector", "memset", hnstage[0][:, 1024:1025], 1.0)
            p.op("vector", "memset", hnstage[1][:, 1024:1025], 1.0, inc=ve)
            p.wait("sync", st, d12_done)
            st_hn = {}
            ve_h = {}
            for i in range(NT):
                if i >= 2:
                    p.wait("sync", ve, ve_h[i - 2])
                p.op("sync", "dma_start", out=htile[i % 2][:, 0:512],
                     in_=d["z2"][i * 128:(i + 1) * 128, 0:512], inc=ld, inck=16)
                if i == 0:
                    p.op("sync", "dma_start", out=htile[0][1:128, 512:1024],
                         in_=d["z2"][0:127, 512:1024], inc=ld, inck=16)
                else:
                    p.op("sync", "dma_start", out=htile[i % 2][:, 512:1024],
                         in_=d["z2"][i * 128 - 1:(i + 1) * 128 - 1, 512:1024],
                         inc=ld, inck=16)
                p.wait("vector", ld)
                if i == 0:
                    p.op("vector", "memset", htile[0][0:1, 512:1024], 0.0)
                p.op("vector", "tensor_reduce", kit.s1[:], htile[i % 2][:], AX.X,
                     ALU.add, inc=ve)
                ve_h[i] = ve.n
                if i >= 2:
                    p.wait("vector", st, st_hn[i - 2])
                emit_ln_affine(p, kit, htile[i % 2][:], hnstage[i % 2][:, 0:1024],
                               se, ve, lgrep=lgrep[:], lbrep=lbrep[:])
                ve_h[i] = ve.n
                p.wait("scalar", ve)
                p.op("scalar", "dma_start", out=d["hnbf"][i * 128:(i + 1) * 128, :],
                     in_=hnstage[i % 2][:], inc=st, inck=16)
                st_hn[i] = st.n
            d3a_done = st.n

        if "d3b" not in _dph:
            return st.n
        with ExitStack() as dctx:
            dsb = lambda n_, s_, t_=F32: dctx.enter_context(nc.sbuf_tensor(f"{n_}_{l}", s_, t_))
            dps = lambda n_, s_, t_=F32: dctx.enter_context(nc.psum_tensor(f"{n_}_{l}", s_, t_))
            w1sb = dsb("D_w1sb", [128, 9 * 5632], BF16)
            hnT = [[dsb(f"D_hnT{g}_{c}", [128, 512], BF16) for c in range(9)]
                   for g in range(2)]
            gsc = dsb("D_gsc", [128, 256])
            h2st = [dsb(f"D_h2st{i}", [128, 2944], BF16) for i in range(2)]
            psW = [dps(f"D_psW{k}", [128, 512]) for k in range(2)]
            p.wait("gpsimd", st, d3a_done)
            p.wait("vector", st, d3a_done)
            for c in range(9):
                p.op("gpsimd", "dma_start", out=w1sb[:, c * 5632:(c + 1) * 5632],
                     in_=d["w1e"][l, c * 128:(c + 1) * 128, :], inc=gd, inck=16)
            w1_ready = gd.n
            p.op("vector", "memset", h2st[0][:, 2816:2944], 0.0)
            p.op("vector", "memset", h2st[1][:, 2816:2944], 0.0)
            p.op("vector", "memset", h2st[0][:, 2816:2817], 1.0)
            p.op("vector", "memset", h2st[1][:, 2816:2817], 1.0, inc=ve)
            h2const_ve = ve.n
            p.wait("sync", st, d3a_done)
            pe_g = {}
            st_h2 = {}
            ve_h2 = {}
            se_h2 = {}
            ngrp = 0
            for tc in range(TC):
                g = tc % 2
                if tc >= 2:
                    p.wait("sync", pe, pe_g[(tc - 2, 3, 10)])
                for c in range(9):
                    p.op("sync", "dma_start", out=hnT[g][c][:],
                         in_=d["hnbf"][tc * 512:(tc + 1) * 512,
                                       c * 128:(c + 1) * 128],
                         transpose=True, inc=ld, inck=16)
                p.wait("tensor", ld)
                if tc == 0:
                    p.wait("tensor", gd, w1_ready)
                    p.wait("tensor", ve, h2const_ve)
                for i4 in range(4):
                    i = tc * 4 + i4
                    for grp in range(11):
                        kk = ngrp % 2
                        if ngrp >= 2:
                            p.wait("tensor", se, se_h2.get(ngrp - 2, 0))
                            p.wait("tensor", ve, ve_h2.get(ngrp - 2, 0))
                        for c in range(9):
                            p.op("tensor", "matmul", psW[kk][:],
                                 hnT[g][c][:, i4 * 128:(i4 + 1) * 128],
                                 w1sb[:, c * 5632 + grp * 512:
                                      c * 5632 + (grp + 1) * 512],
                                 start=(c == 0), stop=(c == 8),
                                 inc=(pe if c == 8 else None))
                        pe_g[(tc, i4, grp)] = pe.n
                        p.wait("scalar", pe)
                        p.wait("scalar", ve, ve_h2.get(ngrp - 1, 0))
                        pg = psW[kk][:].rearrange("p (a b) -> p a b", b=256)
                        p.op("scalar", "activation",
                             gsc[:].rearrange("p (a b) -> p a b", a=2),
                             pg[:, :, 128:256], AF.Gelu)
                        p.op("scalar", "activation", dtiny[:], gsc[:, 255:256],
                             AF.Copy, bias=0.0, scale=1.0, inc=se)
                        se_h2[ngrp] = se.n
                        p.wait("vector", se)
                        if i >= 2 and grp == 0:
                            p.wait("vector", st, st_h2[i - 2])
                        p.op("vector", "tensor_tensor",
                             h2st[i % 2][:, grp * 256:(grp + 1) * 256]
                             .rearrange("p (a b) -> p a b", a=2),
                             pg[:, :, 0:128],
                             gsc[:].rearrange("p (a b) -> p a b", a=2), ALU.mult,
                             inc=ve)
                        ve_h2[ngrp] = ve.n
                        ngrp += 1
                    p.wait("scalar", ve)
                    p.op("scalar", "dma_start",
                         out=d["h2bf"][i * 128:(i + 1) * 128, :], in_=h2st[i % 2][:],
                         inc=st, inck=16)
                    st_h2[i] = st.n
                pe_g[(tc, 3, 10)] = pe.n
            d3b_done = st.n

        if "d3c" not in _dph:
            return st.n
        zout = d["z3a"] if l == 0 else d["z3b"]
        with ExitStack() as dctx:
            dsb = lambda n_, s_, t_=F32: dctx.enter_context(nc.sbuf_tensor(f"{n_}_{l}", s_, t_))
            dps = lambda n_, s_, t_=F32: dctx.enter_context(nc.psum_tensor(f"{n_}_{l}", s_, t_))
            w2sb = dsb("D_w2sb", [128, 23 * 1024], BF16)
            h2T = [[dsb(f"D_h2T{g}_{c}", [128, 512], BF16) for c in range(23)]
                   for g in range(2)]
            z2res = [dsb(f"D_z2res{i}", [128, 1024]) for i in range(2)]
            z3t = [dsb(f"D_z3t{i}", [128, 1024]) for i in range(2)]
            psW2 = [dps(f"D_psW2{k}", [128, 512]) for k in range(2)]
            p.wait("gpsimd", st, d3b_done)
            for c in range(23):
                p.op("gpsimd", "dma_start", out=w2sb[:, c * 1024:(c + 1) * 1024],
                     in_=d["w2e"][l, c * 128:(c + 1) * 128, :], inc=gd, inck=16)
            w2_ready = gd.n
            p.wait("sync", st, d3b_done)
            pe_t = {}
            st_z3 = {}
            ve_z3 = {}
            for tc in range(TC):
                g = tc % 2
                if tc >= 2:
                    p.wait("sync", pe, pe_t[(tc - 2, 3)])
                for c in range(23):
                    p.op("sync", "dma_start", out=h2T[g][c][:],
                         in_=d["h2bf"][tc * 512:(tc + 1) * 512,
                                       c * 128:(c + 1) * 128],
                         transpose=True, inc=ld, inck=16)
                p.wait("tensor", ld)
                if tc == 0:
                    p.wait("tensor", gd, w2_ready)
                for i4 in range(4):
                    i = tc * 4 + i4
                    if i >= 2:
                        p.wait("sync", ve, ve_z3.get(i - 2, 0))
                    p.op("sync", "dma_start", out=z2res[i % 2][:],
                         in_=d["z2"][i * 128:(i + 1) * 128, :], inc=ld, inck=16)
                    zld = ld.n
                    if i >= 1:
                        p.wait("tensor", ve, ve_z3.get(i - 1, 0))
                    for half in range(2):
                        for c in range(23):
                            p.op("tensor", "matmul", psW2[half][:],
                                 h2T[g][c][:, i4 * 128:(i4 + 1) * 128],
                                 w2sb[:, c * 1024 + half * 512:
                                      c * 1024 + (half + 1) * 512],
                                 start=(c == 0), stop=(c == 22),
                                 inc=(pe if (c == 22 and half == 1) else None))
                    pe_t[(tc, i4)] = pe.n
                    p.wait("vector", pe)
                    p.wait("vector", ld, zld)
                    if i >= 2:
                        p.wait("vector", st, st_z3[i - 2])
                    for half in range(2):
                        p.op("vector", "scalar_tensor_tensor",
                             z3t[i % 2][:, half * 512:(half + 1) * 512],
                             z2res[i % 2][:, half * 512:(half + 1) * 512], 1.0,
                             psW2[half][:], ALU.mult, ALU.add,
                             inc=(ve if half == 1 else None))
                    ve_z3[i] = ve.n
                    p.wait("scalar", ve)
                    p.op("scalar", "dma_start",
                         out=zout[i * 128:(i + 1) * 128, :], in_=z3t[i % 2][:],
                         inc=st, inck=16)
                    st_z3[i] = st.n
            d3c_done = st.n
        zin = zout

    if "fin" not in _dph:
        return st.n
    with ExitStack() as dctx:
        dsb = lambda n_, s_, t_=F32: dctx.enter_context(nc.sbuf_tensor(f"{n_}_{l}", s_, t_))
        xa = [dsb(f"D_fxa{i}", [128, 1024]) for i in range(2)]
        xb = [dsb(f"D_fxb{i}", [128, 1024]) for i in range(2)]
        xo = [dsb(f"D_fxo{i}", [128, 1024]) for i in range(2)]
        xsum = dsb("D_fxsum", [128, 1024])
        p.wait("sync", st, d3c_done)
        st_f = {}
        ve_f = {}
        for i in range(NT):
            if i >= 2:
                p.wait("sync", ve, ve_f[i - 2])
            p.op("sync", "dma_start", out=xa[i % 2][:],
                 in_=d["x3"][i * 128:(i + 1) * 128, :], inc=ld, inck=16)
            p.op("sync", "dma_start", out=xb[i % 2][:],
                 in_=zin[i * 128:(i + 1) * 128, :], inc=ld, inck=16)
            p.wait("vector", ld)
            p.op("vector", "scalar_tensor_tensor", xsum[:], xa[i % 2][:], 1.0,
                 xb[i % 2][:], ALU.mult, ALU.add, accum_out=kit.s1[:], inc=ve)
            ve_f[i] = ve.n
            if i >= 2:
                p.wait("vector", st, st_f[i - 2])
            emit_ln(p, kit, xsum[:], xo[i % 2][:], se, ve)
            ve_f[i] = ve.n
            p.wait("scalar", ve)
            p.op("scalar", "dma_start", out=d["out4"][i * 128:(i + 1) * 128, :],
                 in_=xo[i % 2][:], inc=st, inck=16)
            st_f[i] = st.n
    return st.n


def build_stageB(nc, ctx, p, d, s):
    sb = lambda n_, s_, t_=F32: ctx.enter_context(nc.sbuf_tensor(n_, s_, t_))
    ld, st, gd = s["ld"], s["st"], s["gd"]
    pe, ve, se, gp, ig = s["pe"], s["ve"], s["se"], s["gp"], s["ig"]
    NT, HC, TC = 64, 8, 16
    import os as _os

    identf = sb("B_identf", [128, 128])
    identb = sb("B_identb", [128, 128], BF16)
    pmf = sb("B_pmf", [128, 128])
    ones1r = sb("B_ones1r", [1, 128])
    onescol = sb("B_onescol", [128, 1], BF16)
    onescf = sb("B_onescf", [128, 1])
    glrep = sb("B_glrep", [128, 1024])
    ghrep = sb("B_ghrep", [128, 1024])
    qtokr = sb("B_qtokr", [128, 1024])
    ktokr = sb("B_ktokr", [128, 1024])
    sbq = sb("B_sbq", [128, 64])
    sbkv = sb("B_sbkv", [128, 64])
    idxf = sb("B_idxf", [128, 64])
    kit = LNKit(sb, "B")
    btiny = sb("B_tiny", [128, 1], BF16)

    p.op("gpsimd", "iota", pmf[:], pattern=[[-1, 128]], base=0, channel_multiplier=1,
         allow_small_or_imprecise_dtypes=True)
    p.op("gpsimd", "iota", idxf[:], pattern=[[128, 64]], base=0, channel_multiplier=1,
         allow_small_or_imprecise_dtypes=True)
    p.op("gpsimd", "memset", ones1r[:], 1.0)
    p.op("gpsimd", "memset", onescol[:], 1.0)
    p.op("gpsimd", "memset", onescf[:], 1.0, inc=gp)
    p.wait("vector", gp)
    p.op("vector", "tensor_scalar", identf[:], pmf[:], 0.0, None, ALU.is_equal, inc=ve)
    p.wait("vector", ve)
    p.op("vector", "tensor_copy", identb[:], identf[:], inc=ve)
    ident_ready = ve.n

    with ExitStack() as bctx:
        rps = bctx.enter_context(nc.psum_tensor("B_rps", [128, 512], F32))
        grow = bctx.enter_context(nc.sbuf_tensor("B_grow", [1, 1024], F32))
        for src_name, dst in (("gamma_light", glrep), ("gamma_heavy", ghrep),
                              ("q_tok", qtokr), ("kv_tok", ktokr)):
            p.op("sync", "dma_start", out=grow[:],
                 in_=d[src_name].rearrange("(o f) -> o f", o=1), inc=ld, inck=16)
            p.wait("tensor", ld)
            for half in range(2):
                emit_replicate_row(p, nc, grow[0:1, half * 512:(half + 1) * 512],
                                   dst[:, half * 512:(half + 1) * 512], rps,
                                   ones1r, s, 512)
            p.wait("sync", ve)
        rep_done = ve.n

    with ExitStack() as bctx:
        bsb = lambda n_, s_, t_=F32: bctx.enter_context(nc.sbuf_tensor(n_, s_, t_))
        xin = [bsb(f"B_xin{i}", [128, 1024]) for i in range(2)]
        xnb = [bsb(f"B_xnb{i}", [128, 1024], BF16) for i in range(2)]
        rms = bsb("B_rms", [128, 1])
        scr2 = bsb("B_scr2", [128, 1024])
        b0_st0 = st.n
        ve_t = {}
        for i in range(NT):
            if i >= 2:
                p.wait("sync", ve, ve_t[i - 2])
            p.op("sync", "dma_start", out=xin[i % 2][:],
                 in_=d["x1"][i * 128:(i + 1) * 128, :], inc=ld, inck=16)
            p.wait("scalar", ld)
            p.op("scalar", "activation", kit.scr[:], xin[i % 2][:], AF.Square,
                 accum_out=kit.s2[:])
            p.op("scalar", "activation", btiny[:], kit.s2[:], AF.Copy, bias=0.0,
                 scale=1.0, inc=se)
            p.wait("vector", se)
            if i == 0:
                p.wait("vector", ve, rep_done)
            p.op("vector", "scalar_tensor_tensor", scr2[:], xin[i % 2][:], 1.0,
                 qtokr[:], ALU.mult, ALU.mult, accum_out=sbq[:, i:i + 1])
            p.op("vector", "scalar_tensor_tensor", scr2[:], xin[i % 2][:], 1.0,
                 ktokr[:], ALU.mult, ALU.mult, accum_out=sbkv[:, i:i + 1], inc=ve)
            p.wait("scalar", ve)
            p.op("scalar", "activation", kit.sd[:], kit.s2[:], AF.Sqrt)
            p.op("scalar", "activation", btiny[:], kit.sd[:], AF.Copy, bias=0.0,
                 scale=1.0, inc=se)
            p.wait("vector", se)
            p.op("vector", "tensor_scalar", kit.sd[:], kit.sd[:], 1e-12, None, ALU.max,
                 inc=ve)
            p.wait("vector", ve)
            p.op("vector", "reciprocal", rms[:], kit.sd[:], inc=ve)
            p.wait("vector", ve)
            p.op("vector", "tensor_scalar", scr2[:], xin[i % 2][:], rms[:], 32.0,
                 ALU.mult, ALU.mult, inc=ve)
            p.wait("vector", ve)
            if i >= 2:
                p.wait("vector", st, b0_st0 + (i - 1) * 16)
            p.op("vector", "tensor_tensor", xnb[i % 2][:], scr2[:], glrep[:], ALU.mult,
                 inc=ve)
            ve_t[i] = ve.n
            p.wait("scalar", ve)
            p.op("scalar", "dma_start", out=d["xnbf"][i * 128:(i + 1) * 128, :],
                 in_=xnb[i % 2][:], inc=st, inck=16)
        xnbf_done = st.n
        s_ready = ve.n

    if "b1" in _os.environ.get("BPH", "b1,b2"):
        with ExitStack() as bctx:
            bsb = lambda n_, s_, t_=F32: bctx.enter_context(nc.sbuf_tensor(n_, s_, t_))
            bps = lambda n_, s_, t_=F32: bctx.enter_context(nc.psum_tensor(n_, s_, t_))
            wcat = bsb("B_wcat", [128, HC * 1536], BF16)
            wol = bsb("B_wol", [128, 4 * 1024], BF16)
            xts = [bsb(f"B_xts{g}", [128, HC * 512], BF16) for g in range(2)]
            strip = [bsb(f"B_strip{g}", [128, 12 * 512], BF16) for g in range(2)]
            vtok = bsb("B_vtok", [128, 64], BF16)
            esbB = bsb("B_esbB", [128, 128], BF16)
            invD = bsb("B_invD", [1, 128])
            rinv = bsb("B_rinv", [64, 128])
            aTB = bsb("B_aTB", [128, 512], BF16)
            xres = [bsb(f"B_xres{i}", [128, 1024]) for i in range(2)]
            btt = [bsb(f"B_btt{i}", [128, 1024]) for i in range(2)]
            psQ = [bps(f"B_psQ{k}", [128, 512]) for k in range(2)]
            poB = bps("B_poB", [128, 512])
            pDb = bps("B_pDb", [128, 512])
            pRb = bps("B_pRb", [128, 512])
            pwo = [bps(f"B_pwo{k}", [128, 512]) for k in range(2)]
            ptbB = bps("B_ptbB", [128, 128], BF16)

            p.wait("gpsimd", st, xnbf_done)
            for hc in range(HC):
                p.op("gpsimd", "dma_start", out=wcat[:, hc * 1536:hc * 1536 + 512],
                     in_=d["wq_light"][hc * 128:(hc + 1) * 128, :], inc=gd, inck=16)
                p.op("gpsimd", "dma_start", out=wcat[:, hc * 1536 + 512:(hc + 1) * 1536],
                     in_=d["wkv_light"][hc * 128:(hc + 1) * 128, :], inc=gd, inck=16)
            for b4 in range(4):
                p.op("gpsimd", "dma_start", out=wol[:, b4 * 1024:(b4 + 1) * 1024],
                     in_=d["wo_light"][b4 * 128:(b4 + 1) * 128, :], inc=gd, inck=16)
            w_ready = gd.n
            p.wait("sync", st, xnbf_done)
            pe_tc = {}
            ve_strip = {}
            ve_attn_end = 0
            st_bt = {}
            for tc in range(TC):
                g = tc % 2
                if tc >= 2:
                    p.wait("sync", pe, pe_tc[tc - 2])
                for hc in range(HC):
                    p.op("sync", "dma_start",
                         out=xts[g][:, hc * 512:(hc + 1) * 512],
                         in_=d["xnbf"][tc * 512:(tc + 1) * 512,
                                       hc * 128:(hc + 1) * 128],
                         transpose=True, inc=ld, inck=16)
                p.wait("tensor", ld)
                if tc == 0:
                    p.wait("tensor", gd, w_ready)
                    p.wait("tensor", ve, ident_ready)
                if tc >= 2:
                    p.wait("tensor", ve, ve_strip[tc - 2])
                str_ve = {}
                for m in range(12):
                    kk = m % 2
                    if m >= 2:
                        p.wait("tensor", ve, str_ve[m - 2])
                    if m < 2 and ve_attn_end:
                        p.wait("tensor", ve, ve_attn_end)
                    for hc in range(HC):
                        p.op("tensor", "matmul", psQ[kk][:],
                             wcat[:, hc * 1536 + m * 128: hc * 1536 + (m + 1) * 128],
                             xts[g][:, hc * 512:(hc + 1) * 512],
                             start=(hc == 0), stop=(hc == HC - 1),
                             inc=(pe if hc == HC - 1 else None))
                    p.wait("vector", pe)
                    p.op("vector", "tensor_copy", strip[g][:, m * 512:(m + 1) * 512],
                         psQ[kk][:], inc=ve)
                    str_ve[m] = ve.n
                ve_strip[tc] = ve.n
                for w4 in range(4):
                    i = tc * 4 + w4
                    if i >= 2:
                        p.wait("sync", ve)
                    p.op("sync", "dma_start", out=xres[i % 2][:],
                         in_=d["x1"][i * 128:(i + 1) * 128, :], inc=ld, inck=16)
                    xres_ld = ld.n
                    for hh in range(8):
                        ch_q = hh // 2
                        ch_k = 4 + hh // 2
                        ch_v = 8 + hh // 2
                        pb = (hh % 2) * 64
                        p.wait("tensor", ve, max(ve_strip[tc], ve_attn_end))
                        p.wait("tensor", ve)
                        p.op("tensor", "transpose", ptbB[0:128, 0:64],
                             strip[g][pb:pb + 64,
                                      ch_v * 512 + w4 * 128:ch_v * 512 + (w4 + 1) * 128],
                             identb[pb:pb + 64, pb:pb + 64], inc=pe)
                        p.wait("vector", pe)
                        p.op("vector", "tensor_copy", vtok[:], ptbB[0:128, 0:64],
                             inc=ve)
                        p.wait("tensor", ve)
                        p.op("tensor", "matmul", psQ[0][:, 0:128],
                             strip[g][pb:pb + 64,
                                      ch_k * 512 + w4 * 128:ch_k * 512 + (w4 + 1) * 128],
                             strip[g][pb:pb + 64,
                                      ch_q * 512 + w4 * 128:ch_q * 512 + (w4 + 1) * 128],
                             start=True, stop=True, inc=pe)
                        p.wait("scalar", pe)
                        p.op("scalar", "activation", esbB[:], psQ[0][:, 0:128], AF.Exp,
                             scale=0.125)
                        p.op("scalar", "activation", btiny[:], esbB[:, 127:128],
                             AF.Copy, bias=0.0, scale=1.0, inc=se)
                        p.wait("tensor", se)
                        p.op("tensor", "matmul", poB[0:64, 0:128], vtok[:, 0:64],
                             esbB[:], start=True, stop=True)
                        p.op("tensor", "matmul", pDb[0:1, 0:128], onescol[:],
                             esbB[:], start=True, stop=True, inc=pe)
                        p.wait("vector", pe)
                        p.op("vector", "reciprocal", invD[:], pDb[0:1, 0:128],
                             inc=ve)
                        p.wait("tensor", ve)
                        p.op("tensor", "matmul", pRb[0:64, 0:128], ones1r[0:1, 0:64],
                             invD[:], start=True, stop=True, inc=pe)
                        p.wait("vector", pe)
                        p.op("vector", "tensor_copy", rinv[:], pRb[0:64, 0:128],
                             inc=ve)
                        p.wait("vector", ve)
                        p.op("vector", "tensor_tensor",
                             aTB[pb:pb + 64, (hh // 2) * 128:(hh // 2 + 1) * 128],
                             poB[0:64, 0:128], rinv[:], ALU.mult, inc=ve)
                    attn_ve = ve.n
                    p.wait("tensor", ve, attn_ve)
                    for b4 in range(4):
                        for half in range(2):
                            p.op("tensor", "matmul", pwo[half][:],
                                 aTB[:, b4 * 128:(b4 + 1) * 128],
                                 wol[:, b4 * 1024 + half * 512:
                                     b4 * 1024 + (half + 1) * 512],
                                 start=(b4 == 0), stop=(b4 == 3),
                                 inc=(pe if (b4 == 3 and half == 1) else None))
                    pe_tc[tc] = pe.n
                    p.wait("vector", pe)
                    p.wait("vector", ld, xres_ld)
                    if i >= 2:
                        p.wait("vector", st, st_bt[i - 2])
                    for half in range(2):
                        p.op("vector", "scalar_tensor_tensor",
                             btt[i % 2][:, half * 512:(half + 1) * 512],
                             xres[i % 2][:, half * 512:(half + 1) * 512], 1.0,
                             pwo[half][:], ALU.mult, ALU.add,
                             inc=(ve if half == 1 else None))
                    ve_attn_end = ve.n
                    p.wait("scalar", ve)
                    p.op("scalar", "dma_start",
                         out=d["btmp"][i * 128:(i + 1) * 128, :], in_=btt[i % 2][:],
                         inc=st, inck=16)
                    st_bt[i] = st.n
            b1_done = st.n
    else:
        b1_done = st.n

    if "b2" in _os.environ.get("BPH", "b1,b2"):
        with ExitStack() as bctx:
            bsb = lambda n_, s_, t_=F32: bctx.enter_context(nc.sbuf_tensor(n_, s_, t_))
            bps = lambda n_, s_, t_=F32: bctx.enter_context(nc.psum_tensor(n_, s_, t_))
            KSEL = {"q": 1024, "kv": 2048}
            LOGK = {"q": float(np.log(np.float32(1152.0))),
                    "kv": float(np.log(np.float32(2304.0)))}
            svec = {"q": sbq, "kv": sbkv}
            bvec = {r: bsb(f"B_bv_{r}", [128, 64]) for r in ("q", "kv")}
            tsb = {r: bsb(f"B_tsb_{r}", [128, 64]) for r in ("q", "kv")}
            rmaxc = {r: bsb(f"B_rmax_{r}", [128, 1]) for r in ("q", "kv")}
            scal1 = {r: bsb(f"B_sc1_{r}", [1, 1]) for r in ("q", "kv")}
            scal2 = {r: bsb(f"B_sc2_{r}", [1, 1]) for r in ("q", "kv")}
            mrep = {r: bsb(f"B_mrep_{r}", [128, 1]) for r in ("q", "kv")}
            scores = {r: bsb(f"B_scores_{r}", [128, 64]) for r in ("q", "kv")}
            etile = {r: bsb(f"B_etile_{r}", [128, 64]) for r in ("q", "kv")}
            dbgst = bsb("B_dbgst", [1, 64])
            lo = {r: bsb(f"B_lo_{r}", [1, 1]) for r in ("q", "kv")}
            hi = {r: bsb(f"B_hi_{r}", [1, 1]) for r in ("q", "kv")}
            mid = {r: bsb(f"B_mid_{r}", [1, 1]) for r in ("q", "kv")}
            cnt = {r: bsb(f"B_cnt_{r}", [1, 1]) for r in ("q", "kv")}
            cond = {r: bsb(f"B_cond_{r}", [1, 1]) for r in ("q", "kv")}
            midrep = {r: bsb(f"B_midrep_{r}", [128, 1]) for r in ("q", "kv")}
            condi = {r: bsb(f"B_condi_{r}", [1, 1], I32) for r in ("q", "kv")}
            selmi = {r: bsb(f"B_selmi_{r}", [128, 64], I32) for r in ("q", "kv")}
            msk = {r: bsb(f"B_msk_{r}", [128, 64]) for r in ("q", "kv")}
            selm = {r: bsb(f"B_selm_{r}", [128, 64]) for r in ("q", "kv")}
            eqm = {r: bsb(f"B_eqm_{r}", [128, 64]) for r in ("q", "kv")}
            rank2 = {r: bsb(f"B_rank2_{r}", [128, 64]) for r in ("q", "kv")}
            idxi32 = {r: bsb(f"B_idxi_{r}", [128, 64], I32) for r in ("q", "kv")}
            toki2 = bsb("B_toki2", [128, 128], I32)
            mT = bsb("B_mT", [64, 128])
            prefA = bsb("B_prefA", [64, 128])
            prefB = bsb("B_prefB", [64, 128])
            rowtot = bsb("B_rowtot", [64, 1])
            rowA = bsb("B_rowA", [1, 64])
            rowB = bsb("B_rowB", [1, 64])
            roffc = bsb("B_roffc", [64, 1])
            rankT = bsb("B_rankT", [64, 128])

            pctxB = ExitStack()
            ptB = pctxB.enter_context(nc.psum_tensor("B_ptB", [128, 512], F32))

            p.wait("tensor", st, b1_done)
            p.wait("vector", st, b1_done)
            p.wait("scalar", st, b1_done)
            p.wait("gpsimd", st, b1_done)
            p.wait("sync", st, b1_done)
            p.wait("vector", ve, s_ready)
            for r in ("q", "kv"):
                p.op("vector", "tensor_scalar", bvec[r][:], svec[r][:], -1.0, None,
                     ALU.mult, inc=ve)
            for it in range(int(_os.environ.get("B2IT", 50))):
                for r in ("q", "kv"):
                    p.wait("vector", ve)
                    p.op("vector", "tensor_tensor", tsb[r][:], svec[r][:], bvec[r][:],
                         ALU.add, inc=ve)
                    p.wait("vector", ve)
                    p.op("vector", "tensor_reduce", rmaxc[r][:], tsb[r][:], AX.X,
                         ALU.max, inc=ve)
                for r in ("q", "kv"):
                    p.wait("tensor", ve)
                    p.op("tensor", "transpose", ptB[0:1, 0:128], rmaxc[r][:],
                         identf[:], inc=pe)
                    p.wait("vector", pe)
                    p.op("vector", "tensor_reduce", scal1[r][:], ptB[0:1, 0:128],
                         AX.X, ALU.max, inc=ve)
                for r in ("q", "kv"):
                    p.wait("tensor", ve)
                    p.op("tensor", "matmul", ptB[:, 128:129], ones1r[:], scal1[r][:],
                         start=True, stop=True, inc=pe)
                    p.wait("vector", pe)
                    p.op("vector", "tensor_scalar", mrep[r][:], ptB[:, 128:129], -1.0,
                         None, ALU.mult, inc=ve)
                for r in ("q", "kv"):
                    p.wait("scalar", ve)
                    p.op("scalar", "activation", etile[r][:], tsb[r][:], AF.Exp,
                         bias=mrep[r][:], scale=1.0)
                    p.op("scalar", "activation", btiny[:], etile[r][:, 63:64],
                         AF.Exp, scale=1e-6, inc=se)
                for r in ("q", "kv"):
                    p.wait("tensor", se)
                    p.wait("tensor", ve)
                    p.op("tensor", "matmul", ptB[0:1, 192:256], onescf[:],
                         etile[r][:], start=True, stop=True, inc=pe)
                    p.wait("vector", pe)
                    p.op("vector", "tensor_reduce", scal2[r][:], ptB[0:1, 192:256],
                         AX.X, ALU.add, inc=ve)
                for r in ("q", "kv"):
                    if r == "q" and it == 1 and "dbgB" in d:
                        p.wait("vector", ve)
                        p.op("vector", "tensor_copy", dbgst[0:1, 0:1], scal2[r][:])
                        p.op("vector", "tensor_copy", dbgst[0:1, 1:2], scal1[r][:],
                             inc=ve)
                        p.wait("sync", ve)
                        p.op("sync", "dma_start", out=d["dbgB"][0:1, 32:96],
                             in_=dbgst[:], inc=ld, inck=16)
                        p.wait("scalar", ld)
                        p.wait("vector", ld)
                    p.wait("scalar", ve)
                    p.op("scalar", "activation", scal2[r][:], scal2[r][:], AF.Ln)
                    p.op("scalar", "activation", btiny[0:1, 0:1], scal2[r][:],
                         AF.Exp, scale=1e-6, inc=se)
                    p.wait("vector", se)
                    p.op("vector", "tensor_tensor", scal2[r][:], scal2[r][:],
                         scal1[r][:], ALU.add, inc=ve)
                    p.wait("vector", ve)
                    p.op("vector", "tensor_scalar", scal2[r][:], scal2[r][:], -1.0,
                         LOGK[r], ALU.mult, ALU.add, inc=ve)
                for r in ("q", "kv"):
                    p.wait("tensor", ve)
                    p.op("tensor", "matmul", ptB[:, 128:129], ones1r[:], scal2[r][:],
                         start=True, stop=True, inc=pe)
                    p.wait("vector", pe)
                    p.op("vector", "tensor_copy", mrep[r][:], ptB[:, 128:129], inc=ve)
                for r in ("q", "kv"):
                    p.wait("vector", ve)
                    p.op("vector", "tensor_scalar", tsb[r][:], svec[r][:],
                         mrep[r][:], None, ALU.add, inc=ve)
                    p.wait("vector", ve)
                    p.op("vector", "tensor_scalar", bvec[r][:], tsb[r][:], 0.0, -1.0,
                         ALU.max, ALU.mult, inc=ve)
            for r in ("q", "kv"):
                p.wait("vector", ve)
                p.op("vector", "tensor_scalar", tsb[r][:], svec[r][:], mrep[r][:],
                     None, ALU.add, inc=ve)
                p.wait("vector", ve)
                p.op("vector", "tensor_tensor", tsb[r][:], tsb[r][:], bvec[r][:],
                     ALU.add, inc=ve)
                p.wait("scalar", ve)
                p.op("scalar", "activation", scores[r][:], tsb[r][:], AF.Exp)
                p.op("scalar", "activation", btiny[:], scores[r][:, 63:64], AF.Copy,
                     bias=0.0, scale=1.0, inc=se)
            if "dbgB" in d:
                p.wait("sync", se)
                p.wait("sync", ve)
                p.op("sync", "dma_start", out=d["dbgB"][:, 0:64],
                     in_=scores["q"][:], inc=ld, inck=16)
                p.op("sync", "dma_start", out=d["dbgB"][:, 64:128],
                     in_=sbq[:], inc=ld, inck=16)
            p.wait("vector", se)
            for r in ("q", "kv"):
                p.op("vector", "memset", lo[r][:], -1.0)
                p.op("vector", "memset", hi[r][:], 2.0, inc=ve)

            def count_gt(r, thr_ap, out_cnt):
                p.wait("tensor", ve)
                p.op("tensor", "matmul", ptB[:, 128:129], ones1r[:], thr_ap,
                     start=True, stop=True, inc=pe)
                p.wait("vector", pe)
                p.op("vector", "tensor_copy", midrep[r][:], ptB[:, 128:129], inc=ve)
                p.wait("vector", ve)
                p.op("vector", "tensor_scalar", msk[r][:], scores[r][:],
                     midrep[r][:], None, ALU.is_gt, inc=ve)
                p.wait("tensor", ve)
                p.op("tensor", "matmul", ptB[0:1, 192:256], onescf[:], msk[r][:],
                     start=True, stop=True, inc=pe)
                p.wait("vector", pe)
                p.op("vector", "tensor_reduce", out_cnt, ptB[0:1, 192:256], AX.X,
                     ALU.add, inc=ve)

            for it in range(45):
                for r in ("q", "kv"):
                    p.wait("vector", ve)
                    p.op("vector", "tensor_tensor", mid[r][:], lo[r][:], hi[r][:],
                         ALU.add, inc=ve)
                    p.wait("vector", ve)
                    p.op("vector", "tensor_scalar", mid[r][:], mid[r][:], 0.5, None,
                         ALU.mult, inc=ve)
                for r in ("q", "kv"):
                    count_gt(r, mid[r][:], cnt[r][:])
                for r in ("q", "kv"):
                    p.wait("vector", ve)
                    p.op("vector", "tensor_scalar", cond[r][:], cnt[r][:],
                         float(KSEL[r]), None, ALU.is_ge, inc=ve)
                    p.wait("vector", ve)
                    p.op("vector", "tensor_copy", condi[r][:], cond[r][:], inc=ve)
                    p.wait("vector", ve)
                    p.op("vector", "select", lo[r][:], condi[r][:], mid[r][:],
                         lo[r][:])
                    p.op("vector", "select", hi[r][:], condi[r][:], hi[r][:],
                         mid[r][:], inc=ve)
            for r in ("q", "kv"):
                count_gt(r, hi[r][:], cnt[r][:])
                p.wait("vector", ve)
                p.op("vector", "tensor_copy", selm[r][:], msk[r][:], inc=ve)
                count_gt(r, lo[r][:], scal1[r][:])
                p.wait("vector", ve)
                p.op("vector", "tensor_tensor", eqm[r][:], msk[r][:], selm[r][:],
                     ALU.subtract, inc=ve)
            for r in ("q", "kv"):
                p.wait("vector", ve)
                p.op("vector", "tensor_scalar", scal2[r][:], cnt[r][:], -1.0,
                     float(KSEL[r]), ALU.mult, ALU.add, inc=ve)
                p.op("vector", "memset", lo[r][:], 0.0)
                p.op("vector", "memset", hi[r][:], 8192.0, inc=ve)
            for it in range(14):
                for r in ("q", "kv"):
                    p.wait("vector", ve)
                    p.op("vector", "tensor_tensor", mid[r][:], lo[r][:], hi[r][:],
                         ALU.add, inc=ve)
                    p.wait("vector", ve)
                    p.op("vector", "tensor_scalar", mid[r][:], mid[r][:], 0.5, None,
                         ALU.mult, inc=ve)
                for r in ("q", "kv"):
                    p.wait("tensor", ve)
                    p.op("tensor", "matmul", ptB[:, 128:129], ones1r[:], mid[r][:],
                         start=True, stop=True, inc=pe)
                    p.wait("vector", pe)
                    p.op("vector", "tensor_copy", midrep[r][:], ptB[:, 128:129],
                         inc=ve)
                    p.wait("vector", ve)
                    p.op("vector", "tensor_scalar", msk[r][:], idxf[:], midrep[r][:],
                         None, ALU.is_lt, inc=ve)
                    p.wait("vector", ve)
                    p.op("vector", "tensor_tensor", msk[r][:], msk[r][:], eqm[r][:],
                         ALU.mult, inc=ve)
                    p.wait("tensor", ve)
                    p.op("tensor", "matmul", ptB[0:1, 192:256], onescf[:], msk[r][:],
                         start=True, stop=True, inc=pe)
                    p.wait("vector", pe)
                    p.op("vector", "tensor_reduce", cnt[r][:], ptB[0:1, 192:256],
                         AX.X, ALU.add, inc=ve)
                for r in ("q", "kv"):
                    p.wait("vector", ve)
                    p.op("vector", "tensor_scalar", cond[r][:], cnt[r][:],
                         scal2[r][0:1, 0:1], None, ALU.is_ge, inc=ve)
                    p.wait("vector", ve)
                    p.op("vector", "tensor_copy", condi[r][:], cond[r][:], inc=ve)
                    p.wait("vector", ve)
                    p.op("vector", "select", hi[r][:], condi[r][:], mid[r][:],
                         hi[r][:])
                    p.op("vector", "select", lo[r][:], condi[r][:], lo[r][:],
                         mid[r][:], inc=ve)
            for r in ("q", "kv"):
                p.wait("tensor", ve)
                p.op("tensor", "matmul", ptB[:, 128:129], ones1r[:], hi[r][:],
                     start=True, stop=True, inc=pe)
                p.wait("vector", pe)
                p.op("vector", "tensor_copy", midrep[r][:], ptB[:, 128:129], inc=ve)
                p.wait("vector", ve)
                p.op("vector", "tensor_scalar", msk[r][:], idxf[:], midrep[r][:],
                     None, ALU.is_lt, inc=ve)
                p.wait("vector", ve)
                p.op("vector", "tensor_tensor", msk[r][:], msk[r][:], eqm[r][:],
                     ALU.mult, inc=ve)
                p.wait("vector", ve)
                p.op("vector", "tensor_tensor", selm[r][:], selm[r][:], msk[r][:],
                     ALU.add, inc=ve)
            sel_ve = ve.n

            p.op("gpsimd", "iota", toki2[:].rearrange("p (f two) -> p f two", two=2),
                 pattern=[[128, 64], [0, 2]], base=0, channel_multiplier=1, inc=gp)
            for r in ("q", "kv"):
                p.wait("tensor", ve, sel_ve)
                p.wait("tensor", ve)
                p.op("tensor", "transpose", ptB[0:64, 0:128], selm[r][:], identf[:],
                     inc=pe)
                p.wait("vector", pe)
                p.op("vector", "tensor_copy", mT[:], ptB[0:64, 0:128], inc=ve)
                src, dst = mT, prefA
                for k in range(7):
                    sh = 1 << k
                    p.wait("vector", ve)
                    p.op("vector", "tensor_copy", dst[:, 0:sh], src[:, 0:sh])
                    p.op("vector", "tensor_tensor", dst[:, sh:128], src[:, sh:128],
                         src[:, 0:128 - sh], ALU.add, inc=ve)
                    src, dst = dst, (prefB if dst is prefA else prefA)
                pref = src
                p.wait("vector", ve)
                p.op("vector", "tensor_copy", rowtot[:], pref[:, 127:128], inc=ve)
                p.wait("tensor", ve)
                p.op("tensor", "transpose", ptB[0:1, 0:64], rowtot[:],
                     identf[0:64, 0:64], inc=pe)
                p.wait("vector", pe)
                p.op("vector", "tensor_copy", rowA[:], ptB[0:1, 0:64], inc=ve)
                srcr, dstr = rowA, rowB
                for k in range(6):
                    sh = 1 << k
                    p.wait("vector", ve)
                    p.op("vector", "tensor_copy", dstr[:, 0:sh], srcr[:, 0:sh])
                    p.op("vector", "tensor_tensor", dstr[:, sh:64], srcr[:, sh:64],
                         srcr[:, 0:64 - sh], ALU.add, inc=ve)
                    srcr, dstr = dstr, srcr
                p.wait("vector", ve)
                p.op("vector", "memset", dstr[:, 0:1], 0.0)
                p.op("vector", "tensor_copy", dstr[:, 1:64], srcr[:, 0:63], inc=ve)
                p.wait("tensor", ve)
                p.op("tensor", "transpose", ptB[0:64, 0:1], dstr[:],
                     identf[0:1, 0:1], inc=pe)
                p.wait("vector", pe)
                p.op("vector", "tensor_copy", roffc[:], ptB[0:64, 0:1], inc=ve)
                p.wait("vector", ve)
                p.op("vector", "tensor_tensor", rankT[:], pref[:], mT[:],
                     ALU.subtract, inc=ve)
                p.wait("vector", ve)
                p.op("vector", "tensor_scalar", rankT[:], rankT[:], roffc[:], None,
                     ALU.add, inc=ve)
                p.wait("tensor", ve)
                p.op("tensor", "transpose", ptB[0:128, 0:64], rankT[:],
                     identf[0:64, 0:64], inc=pe)
                p.wait("vector", pe)
                p.op("vector", "tensor_copy", rank2[r][:], ptB[0:128, 0:64], inc=ve)
                p.wait("vector", ve)
                p.op("vector", "tensor_tensor", msk[r][:], idxf[:], rank2[r][:],
                     ALU.subtract, inc=ve)
                p.wait("vector", ve)
                p.op("vector", "tensor_scalar", msk[r][:], msk[r][:],
                     float(KSEL[r]), None, ALU.add, inc=ve)
                p.wait("vector", ve)
                p.op("vector", "tensor_copy", selmi[r][:], selm[r][:], inc=ve)
                p.wait("vector", ve)
                p.op("vector", "select", msk[r][:], selmi[r][:], rank2[r][:],
                     msk[r][:], inc=ve)
                p.wait("gpsimd", ve)
                p.op("gpsimd", "tensor_copy", idxi32[r][:], msk[r][:], inc=gp)
            breg = {}
            def _mkreg(e, _k=None):
                pass
            for r in ("q", "kv"):
                breg[r] = nc.gpsimd.alloc_register(name=f"bchk_{r}")
                def loadreg(e, _reg=breg[r], _v=KSEL[r] - 1):
                    e.reg_mov(_reg, _v)
                p.emit("gpsimd", loadreg)
            for r, dname in (("q", "qidxd"), ("kv", "kidxd")):
                for f in range(64):
                    p.op("gpsimd", "indirect_dma_start",
                         out=d[dname],
                         out_offset=IOA(idxi32[r][:, f:f + 1], 0),
                         in_=toki2[:, 2 * f:2 * f + 2],
                         in_offset=None, bounds_check=breg[r],
                         oob_is_err=False, inc=ig, inck=16)
            scatter_ig = ig.n
            pctxB.close()

            idxpool = bsb("B_idxpool", [128, 24], I32)
            idxg = bsb("B_idxg", [128, 24], I32)

            xqT = bsb("B_xqT", [128, HC * 1024], BF16)
            ctT = bsb("B_ctT", [128, HC * 2048], BF16)
            rnt = [bsb(f"B_rnt{i}", [128, 1024], BF16) for i in range(2)]
            rmsB = bsb("B_rmsB", [128, 1])
            scrB = bsb("B_scrB", [128, 1024])
            ptH = [bps(f"B_ptH{k}", [128, 128], BF16) for k in range(2)]
            with ExitStack() as gctx:
                xg = gctx.enter_context(nc.sbuf_tensor("B_xg", [128, 8 * 1024], F32))
                cg = gctx.enter_context(nc.sbuf_tensor("B_cg", [128, 16 * 1024], F32))
                p.wait("gpsimd", ig, scatter_ig)
                for bi in range(8):
                    p.op("gpsimd", "dma_start", out=idxpool[:, bi:bi + 1],
                         in_=d["qidxd"][bi * 128:(bi + 1) * 128, 0:1],
                         allow_slow_non_contiguous=True, inc=gd, inck=16)
                for bi in range(16):
                    p.op("gpsimd", "dma_start", out=idxpool[:, 8 + bi:9 + bi],
                         in_=d["kidxd"][bi * 128:(bi + 1) * 128, 0:1],
                         allow_slow_non_contiguous=True, inc=gd, inck=16)
                p.wait("gpsimd", gd)
                p.op("gpsimd", "tensor_copy", idxg[:], idxpool[:], inc=gp)
                for bi in range(8):
                    p.op("gpsimd", "indirect_dma_start",
                         out=xg[:, bi * 1024:(bi + 1) * 1024], out_offset=None,
                         in_=d["x1"], in_offset=IOA(idxg[:, bi:bi + 1], 0),
                         inc=ig, inck=16)
                for bi in range(16):
                    p.op("gpsimd", "indirect_dma_start",
                         out=cg[:, bi * 1024:(bi + 1) * 1024], out_offset=None,
                         in_=d["x1"], in_offset=IOA(idxg[:, 8 + bi:9 + bi], 0),
                         inc=ig, inck=16)
                gather_ig = ig.n
                ntr = 0
                tr_ve = {}
                for t24 in range(24):
                    p.wait("scalar", ig, gather_ig)
                    src = (xg[:, t24 * 1024:(t24 + 1) * 1024] if t24 < 8 else
                           cg[:, (t24 - 8) * 1024:(t24 - 7) * 1024])
                    if t24 >= 2:
                        p.wait("scalar", ve, tr_ve[t24 - 2])
                    p.op("scalar", "activation", kit.scr[:], src, AF.Square,
                         accum_out=kit.s2[:])
                    p.op("scalar", "activation", btiny[:], kit.s2[:], AF.Copy,
                         bias=0.0, scale=1.0, inc=se)
                    p.wait("scalar", se)
                    p.op("scalar", "activation", kit.sd[:], kit.s2[:], AF.Sqrt)
                    p.op("scalar", "activation", btiny[:], kit.sd[:], AF.Copy,
                         bias=0.0, scale=1.0, inc=se)
                    p.wait("vector", se)
                    p.op("vector", "tensor_scalar", kit.sd[:], kit.sd[:], 1e-12,
                         None, ALU.max, inc=ve)
                    p.wait("vector", ve)
                    p.op("vector", "reciprocal", rmsB[:], kit.sd[:], inc=ve)
                    p.wait("vector", ve)
                    p.op("vector", "tensor_scalar", scrB[:], src, rmsB[:], 32.0,
                         ALU.mult, ALU.mult, inc=ve)
                    p.wait("vector", ve)
                    p.op("vector", "tensor_tensor", rnt[t24 % 2][:], scrB[:],
                         ghrep[:], ALU.mult, inc=ve)
                    p.wait("tensor", ve)
                    for hc in range(HC):
                        kk = ntr % 2
                        if ntr >= 2:
                            p.wait("tensor", ve)
                        p.op("tensor", "transpose", ptH[kk][:],
                             rnt[t24 % 2][:, hc * 128:(hc + 1) * 128], identb[:],
                             inc=pe)
                        p.wait("vector", pe)
                        if t24 < 8:
                            dst = xqT[:, hc * 1024 + t24 * 128:
                                      hc * 1024 + (t24 + 1) * 128]
                        else:
                            dst = ctT[:, hc * 2048 + (t24 - 8) * 128:
                                      hc * 2048 + (t24 - 7) * 128]
                        p.op("vector", "tensor_copy", dst, ptH[kk][:], inc=ve)
                        ntr += 1
                    tr_ve[t24] = ve.n
                rn_ve = ve.n
            wkvT = bsb("B_wkvT", [128, HC * 1024], BF16)
            wqh = bsb("B_wqh", [128, HC * 512], BF16)
            woh = bsb("B_woh", [128, 4 * 1024], BF16)
            wtmpB = bsb("B_wtmpB", [128, 1024], BF16)
            qhT = bsb("B_qhT", [128, 4 * 1024], BF16)
            kvT = bsb("B_kvT", [128, 8 * 2048], BF16)
            vtk = bsb("B_vtk", [128, 8 * 16 * 64], BF16)
            eTh = bsb("B_eTh", [128, 1024], BF16)
            invH = bsb("B_invH", [1, 1024])
            rinvH = bsb("B_rinvH", [64, 1024])
            ohT = bsb("B_ohT", [128, 4 * 1024], BF16)
            hvf = [bsb("B_hvf0", [128, 1024])] * 2
            p.wait("gpsimd", ve, rn_ve)
            for hc in range(HC):
                p.op("gpsimd", "dma_start", out=wqh[:, hc * 512:(hc + 1) * 512],
                     in_=d["wq_heavy"][hc * 128:(hc + 1) * 128, :], inc=gd, inck=16)
            for b4 in range(4):
                p.op("gpsimd", "dma_start", out=woh[:, b4 * 1024:(b4 + 1) * 1024],
                     in_=d["wo_heavy"][b4 * 128:(b4 + 1) * 128, :], inc=gd, inck=16)
            wh_ready = gd.n
            with ExitStack() as wctxB:
                ptwB = [wctxB.enter_context(
                    nc.psum_tensor(f"B_ptwB{k}", [128, 128], BF16)) for k in range(2)]
                emit_wT_load(p, nc, d["wkv_heavy"], wkvT, wtmpB, ptwB, identb, s,
                             ident_ready=ident_ready)
            wkvT_ve = ve.n
            pH = [bps(f"B_pH{k}", [128, 512]) for k in range(6)]

            p.wait("tensor", gd, wh_ready)
            p.wait("tensor", ve, rn_ve)
            for m in range(4):
                k2 = (m % 2) * 2
                if m >= 2:
                    p.wait("tensor", ve)
                for hc in range(HC):
                    for half in range(2):
                        p.op("tensor", "matmul", pH[k2 + half][:],
                             wqh[:, hc * 512 + m * 128: hc * 512 + (m + 1) * 128],
                             xqT[:, hc * 1024 + half * 512:
                                 hc * 1024 + (half + 1) * 512],
                             start=(hc == 0), stop=(hc == HC - 1),
                             inc=(pe if (hc == HC - 1 and half == 1) else None))
                p.wait("vector", pe)
                for half in range(2):
                    p.op("vector", "tensor_copy",
                         qhT[:, m * 1024 + half * 512: m * 1024 + (half + 1) * 512],
                         pH[k2 + half][:], inc=(ve if half == 1 else None))
            p.wait("tensor", ve, wkvT_ve)
            for m in range(8):
                p.wait("tensor", ve)
                for hc in range(HC):
                    for q4 in range(4):
                        p.op("tensor", "matmul", pH[q4][:],
                             wkvT[:, hc * 1024 + m * 128: hc * 1024 + (m + 1) * 128],
                             ctT[:, hc * 2048 + q4 * 512: hc * 2048 + (q4 + 1) * 512],
                             start=(hc == 0), stop=(hc == HC - 1),
                             inc=(pe if (hc == HC - 1 and q4 == 3) else None))
                p.wait("vector", pe)
                for q4 in range(4):
                    p.op("vector", "tensor_copy",
                         kvT[:, m * 2048 + q4 * 512: m * 2048 + (q4 + 1) * 512],
                         pH[q4][:], inc=(ve if q4 == 3 else None))
            kv_ve = ve.n
            ntr = 0
            for h in range(8):
                for nt in range(16):
                    kk = ntr % 2
                    p.wait("tensor", ve)
                    p.op("tensor", "transpose", ptH[kk][0:128, 0:64],
                         kvT[64:128, h * 2048 + nt * 128: h * 2048 + (nt + 1) * 128],
                         identb[64:128, 64:128], inc=pe)
                    p.wait("vector", pe)
                    p.op("vector", "tensor_copy",
                         vtk[:, (h * 16 + nt) * 64:(h * 16 + nt + 1) * 64],
                         ptH[kk][0:128, 0:64], inc=ve)
                    ntr += 1
            vtk_ve = ve.n
            qcur = bsb("B_qcur", [64, 1024], BF16)
            for h in range(8):
                pbq = (h % 2) * 64
                chq = h // 2
                p.wait("vector", ve)
                p.op("vector", "tensor_copy", qcur[:],
                     qhT[pbq:pbq + 64, chq * 1024:(chq + 1) * 1024], inc=ve)
                for nt in range(16):
                    p.wait("tensor", ve)
                    p.wait("tensor", se)
                    for half in range(2):
                        p.op("tensor", "matmul", pH[half][:],
                             kvT[0:64, h * 2048 + nt * 128: h * 2048 + (nt + 1) * 128],
                             qcur[0:64, half * 512:(half + 1) * 512],
                             start=True, stop=True,
                             inc=(pe if half == 1 else None))
                    p.wait("scalar", pe)
                    for half in range(2):
                        p.op("scalar", "activation",
                             eTh[:, half * 512:(half + 1) * 512], pH[half][:],
                             AF.Exp, scale=0.125)
                    p.op("scalar", "activation", btiny[:], eTh[:, 1023:1024],
                         AF.Copy, bias=0.0, scale=1.0, inc=se)
                    p.wait("tensor", se)
                    p.op("tensor", "matmul", pH[2][0:64, :],
                         vtk[:, (h * 16 + nt) * 64:(h * 16 + nt + 1) * 64],
                         eTh[:, 0:512], start=(nt == 0), stop=(nt == 15))
                    p.op("tensor", "matmul", pH[3][0:64, :],
                         vtk[:, (h * 16 + nt) * 64:(h * 16 + nt + 1) * 64],
                         eTh[:, 512:1024], start=(nt == 0), stop=(nt == 15))
                    p.op("tensor", "matmul", pH[4][0:1, :], onescol[:],
                         eTh[:, 0:512], start=(nt == 0), stop=(nt == 15))
                    p.op("tensor", "matmul", pH[5][0:1, :], onescol[:],
                         eTh[:, 512:1024], start=(nt == 0), stop=(nt == 15),
                         inc=pe)
                p.wait("vector", pe)
                p.op("vector", "reciprocal", invH[0:1, 0:512], pH[4][0:1, :])
                p.op("vector", "reciprocal", invH[0:1, 512:1024], pH[5][0:1, :],
                     inc=ve)
                p.wait("tensor", ve)
                p.op("tensor", "matmul", pH[0][0:64, :], ones1r[0:1, 0:64],
                     invH[0:1, 0:512], start=True, stop=True)
                p.op("tensor", "matmul", pH[1][0:64, :], ones1r[0:1, 0:64],
                     invH[0:1, 512:1024], start=True, stop=True, inc=pe)
                p.wait("vector", pe)
                p.op("vector", "tensor_copy", rinvH[:, 0:512], pH[0][0:64, :])
                p.op("vector", "tensor_copy", rinvH[:, 512:1024], pH[1][0:64, :],
                     inc=ve)
                p.wait("vector", ve)
                for half in range(2):
                    p.op("vector", "tensor_tensor",
                         ohT[pbq:pbq + 64,
                             chq * 1024 + half * 512: chq * 1024 + (half + 1) * 512],
                         pH[2 + half][0:64, :], rinvH[:, half * 512:(half + 1) * 512],
                         ALU.mult, inc=(ve if half == 1 else None))
            oh_ve = ve.n
            p.wait("gpsimd", st, b1_done)
            hv_ve = {}
            for tqt in range(8):
                p.wait("tensor", ve, oh_ve if tqt == 0 else hv_ve[tqt - 1])
                for b4 in range(4):
                    for half in range(2):
                        p.op("tensor", "matmul", pH[half][:],
                             ohT[:, b4 * 1024 + tqt * 128: b4 * 1024 + (tqt + 1) * 128],
                             woh[:, b4 * 1024 + half * 512:
                                 b4 * 1024 + (half + 1) * 512],
                             start=(b4 == 0), stop=(b4 == 3),
                             inc=(pe if (b4 == 3 and half == 1) else None))
                p.wait("vector", pe)
                if tqt >= 1:
                    p.wait("vector", ig)
                for half in range(2):
                    p.op("vector", "tensor_copy",
                         hvf[tqt % 2][:, half * 512:(half + 1) * 512], pH[half][:],
                         inc=(ve if half == 1 else None))
                hv_ve[tqt] = ve.n
                p.wait("gpsimd", ve, hv_ve[tqt])
                p.op("gpsimd", "indirect_dma_start",
                     out=d["btmp"],
                     out_offset=IOA(idxg[:, tqt:tqt + 1], 0),
                     in_=hvf[tqt % 2][:], in_offset=None,
                     compute_op=ALU.add, inc=ig, inck=16)
            heavy_ig = ig.n
    else:
        heavy_ig = ig.n

    with ExitStack() as bctx:
        bsb2 = lambda n_, s_, t_=F32: bctx.enter_context(nc.sbuf_tensor(n_, s_, t_))
        bt = [bsb2(f"B3_bt{i}", [128, 1024]) for i in range(2)]
        xo = [bsb2(f"B3_xo{i}", [128, 1024]) for i in range(2)]
        p.wait("sync", st, b1_done)
        p.wait("sync", ig, heavy_ig)
        p.wait("vector", ig, heavy_ig)
        st_f = {}
        ve_f = {}
        for i in range(NT):
            if i >= 2:
                p.wait("sync", ve, ve_f[i - 2])
            p.op("sync", "dma_start", out=bt[i % 2][:],
                 in_=d["btmp"][i * 128:(i + 1) * 128, :], inc=ld, inck=16)
            p.wait("vector", ld)
            p.op("vector", "tensor_reduce", kit.s1[:], bt[i % 2][:], AX.X, ALU.add,
                 inc=ve)
            if i >= 2:
                p.wait("vector", st, st_f[i - 2])
            emit_ln(p, kit, bt[i % 2][:], xo[i % 2][:], se, ve)
            ve_f[i] = ve.n
            p.wait("scalar", ve)
            p.op("scalar", "dma_start", out=d["xs2"][i * 128:(i + 1) * 128, :],
                 in_=xo[i % 2][:], inc=st, inck=16)
            st_f[i] = st.n
    return st.n

from concourse.bass_utils import run_bass_kernel_spmd

S, D_MODEL, NB = 8192, 1024, 8


def _mk_tables():
    half = np.arange(0, 64, 2, dtype=np.float64)
    inv_freq = 1.0 / 10000.0 ** (half / 64)
    t = np.arange(S, dtype=np.float64)
    freqs = np.concatenate([t[:, None] * inv_freq[None, :]] * 2, 1)
    base = (half + 0.4 * 64) / (1.4 * 64)
    power = (t - S // 2) / 256.0
    scale = np.concatenate([base[None, :] ** power[:, None]] * 2, 1)
    cos, sin = np.cos(freqs), np.sin(freqs)
    return (np.ascontiguousarray((cos * scale).astype(np.float32)),
            np.ascontiguousarray((sin * scale).astype(np.float32)),
            np.ascontiguousarray((cos / scale).astype(np.float32)),
            np.ascontiguousarray((sin / scale).astype(np.float32)))


def _prep_d_weights(w1, b1, w2, b2):
    L = w1.shape[0]
    w1e = np.zeros((L, 1152, 5632), np.float32)
    w2e = np.zeros((L, 2944, 1024), np.float32)
    for l in range(L):
        for k in range(22):
            a0, a1 = 128 * k, min(128 * (k + 1), 2730)
            n = a1 - a0
            w1e[l, 0:1024, 256 * k:256 * k + n] = w1[l][:, a0:a1]
            w1e[l, 1024, 256 * k:256 * k + n] = b1[l][a0:a1]
            w1e[l, 0:1024, 256 * k + 128:256 * k + 128 + n] = w1[l][:, 2730 + a0:2730 + a1]
            w1e[l, 1024, 256 * k + 128:256 * k + 128 + n] = b1[l][2730 + a0:2730 + a1]
            w2e[l, 128 * k:128 * k + n, :] = w2[l][a0:a1, :]
        w2e[l, 2816, :] = b2[l]
    return w1e, w2e


def _barrier(p, s, done):
    for eng in ENGINES:
        p.wait(eng, s["st"], done)


def _build_fused():
    nc = bass.Bass("TRN2", target_bir_lowering=False)
    d = {}

    def inp(name, shape):
        d[name] = nc.dram_tensor(name, shape, F32, kind="ExternalInput").ap()

    def intern(name, shape, dt=F32):
        d[name] = nc.dram_tensor(name, shape, dt, kind="Internal").ap()

    inp("x0", [S, 1024]); inp("Wq", [1024, 1024]); inp("bq", [1024])
    intern("xbf", [S, 1024], BF16); intern("vtok", [S, 1024], BF16)
    intern("ebuf", [64, 128, S], BF16)
    intern("xs1", [S, 1024])
    inp("gamma_light", [1024]); inp("gamma_heavy", [1024])
    inp("q_tok", [1024]); inp("kv_tok", [1024])
    inp("wq_light", [1024, 512]); inp("wkv_light", [1024, 1024])
    inp("wo_light", [512, 1024])
    inp("wq_heavy", [1024, 512]); inp("wkv_heavy", [1024, 1024])
    inp("wo_heavy", [512, 1024])
    intern("xnbf", [S, 1024], BF16); intern("btmp", [S, 1024])
    intern("qidxd", [S, 2], I32); intern("kidxd", [S, 2], I32)
    intern("xs2", [S, 1024])
    inp("wj", [1024, 1024]); inp("wjv", [1024, 1024]); inp("wo", [1024, 1024])
    intern("x2bf", [S, 1024], BF16); intern("vtokC", [S, 1024], BF16)
    intern("scd", [64, 128])
    intern("xs3", [S, 1024])
    inp("wqkv", [2, 1024, 1536]); inp("wo_l", [2, 512, 1024])
    inp("qs_t", [2, 512]); inp("ks_t", [2, 512])
    inp("lng", [2, 1024]); inp("lnb", [2, 1024])
    inp("w1e", [2, 1152, 5632]); inp("w2e", [2, 2944, 1024])
    inp("cq", [S, 64]); inp("sq", [S, 64]); inp("ck", [S, 64]); inp("sk", [S, 64])
    intern("zbf", [S, 1024], BF16); intern("z2", [S, 1024])
    intern("z3a", [S, 1024]); intern("z3b", [S, 1024])
    intern("hnbf", [S, 1152], BF16); intern("h2bf", [S, 2944], BF16)
    d["out4"] = nc.dram_tensor("out4", [S, 1024], F32, kind="ExternalOutput").ap()

    with ExitStack() as ctx:
        p = Prog(nc, ctx)
        s = make_sems(p)
        with ExitStack() as actx:
            done = build_stageA(nc, actx, p, d, s)
        _barrier(p, s, done)
        with ExitStack() as bctx:
            dB = dict(d); dB["x1"] = d["xs1"]
            done = build_stageB(nc, bctx, p, dB, s)
        _barrier(p, s, done)
        with ExitStack() as cctx:
            dC = dict(d); dC["x2"] = d["xs2"]
            done = build_stageC(nc, cctx, p, dC, s)
        _barrier(p, s, done)
        with ExitStack() as dctx:
            dD = dict(d); dD["x3"] = d["xs3"]
            done = build_stageD(nc, dctx, p, dD, s)
        p.wait("sync", s["st"], done)
        p.finalize()
    return nc


_NC_CACHE = {}


def kernel(L0_attn_inputs, L0_attn_Wq, L0_attn_bq, L0_ffn_gamma_light,
           L0_ffn_wq_light, L0_ffn_wkv_light, L0_ffn_wo_light,
           L0_ffn_q_routing_token, L0_ffn_kv_routing_token, L0_ffn_gamma_heavy,
           L0_ffn_wq_heavy, L0_ffn_wkv_heavy, L0_ffn_wo_heavy,
           L1_attn_wj_w, L1_attn_wjv_w, L1_attn_wo_w, L1_ffn_wqkv,
           L1_ffn_q_scale, L1_ffn_k_scale, L1_ffn_wo, L1_ffn_ln_g, L1_ffn_ln_b,
           L1_ffn_w1, L1_ffn_b1, L1_ffn_w2, L1_ffn_b2):
    f32 = lambda a: np.ascontiguousarray(np.asarray(a, dtype=np.float32))
    x0 = f32(L0_attn_inputs)
    w1e, w2e = _prep_d_weights(f32(L1_ffn_w1), f32(L1_ffn_b1), f32(L1_ffn_w2),
                               f32(L1_ffn_b2))
    qs_t = np.ascontiguousarray(np.tile(f32(L1_ffn_q_scale), (1, 8)))
    ks_t = np.ascontiguousarray(np.tile(f32(L1_ffn_k_scale), (1, 8)))
    cq, sq, ck, sk = _mk_tables()
    nc = _build_fused()
    common = {
        "Wq": f32(L0_attn_Wq), "bq": f32(L0_attn_bq),
        "gamma_light": f32(L0_ffn_gamma_light),
        "gamma_heavy": f32(L0_ffn_gamma_heavy),
        "q_tok": f32(L0_ffn_q_routing_token),
        "kv_tok": f32(L0_ffn_kv_routing_token),
        "wq_light": f32(L0_ffn_wq_light), "wkv_light": f32(L0_ffn_wkv_light),
        "wo_light": f32(L0_ffn_wo_light), "wq_heavy": f32(L0_ffn_wq_heavy),
        "wkv_heavy": f32(L0_ffn_wkv_heavy), "wo_heavy": f32(L0_ffn_wo_heavy),
        "wj": f32(L1_attn_wj_w), "wjv": f32(L1_attn_wjv_w),
        "wo": f32(L1_attn_wo_w), "wqkv": f32(L1_ffn_wqkv),
        "wo_l": f32(L1_ffn_wo), "qs_t": qs_t, "ks_t": ks_t,
        "lng": f32(L1_ffn_ln_g), "lnb": f32(L1_ffn_ln_b),
        "w1e": w1e, "w2e": w2e, "cq": cq, "sq": sq, "ck": ck, "sk": sk,
    }
    maps = [dict(common, x0=x0[b]) for b in range(NB)]
    res = run_bass_kernel_spmd(nc, maps, core_ids=list(range(NB)))
    return np.stack([res.results[b]["out4"] for b in range(NB)])


# version 06
# speedup vs baseline: 1.1811x; 1.0063x over previous
import numpy as np
from contextlib import ExitStack
import concourse.bass as bass
import concourse.mybir as mybir

F32 = mybir.dt.float32
BF16 = mybir.dt.bfloat16
U32 = mybir.dt.uint32
I32 = mybir.dt.int32
AF = mybir.ActivationFunctionType
ALU = mybir.AluOpType
AX = mybir.AxisListType
IOA = bass.IndirectOffsetOnAxis

ENGINES = ("sync", "tensor", "vector", "scalar", "gpsimd")


class _Sem:
    def __init__(self, h):
        self.h = h
        self.n = 0


class Prog:
    def __init__(self, nc, ctx):
        self.nc = nc
        self.ctx = ctx
        self.script = []

    def sem(self, name):
        return _Sem(self.ctx.enter_context(self.nc.semaphore(name)))

    def emit(self, eng, fn, log=""):
        self.script.append((eng, fn, log))

    def wait(self, eng, sem, n=None):
        v = sem.n if n is None else n
        if v <= 0:
            return
        self.emit(eng, lambda e, _s=sem.h, _v=v: e.wait_ge(_s, _v),
                  log=f"wait_ge({sem.h.name if hasattr(sem.h,'name') else '?'},{v})")

    def op(self, eng, opname, *args, inc=None, inck=1, **kwargs):
        if inc is not None:
            inc.n += inck
        import traceback as _tb
        _site = "".join(_tb.format_stack(limit=4)[:-1])
        def f(e, _a=args, _k=kwargs, _inc=inc, _inck=inck, _op=opname, _site=_site):
            try:
                ins = getattr(e, _op)(*_a, **_k)
            except Exception as ex:
                raise RuntimeError(f"op {_op} failed: {ex}\nemitted at:\n{_site}")
            if _inc is not None:
                ins.then_inc(_inc.h, _inck)
        lg = opname
        if inc is not None:
            lg += f" ->inc({inc.h.name if hasattr(inc.h,'name') else '?'},{inck})={inc.n}"
        self.emit(eng, f, log=lg)

    def finalize(self):
        nc = self.nc
        script = self.script
        with nc.Block() as block:
            for eng in ENGINES:
                items = [f for (e, f, _l) in script if e == eng]
                if not items:
                    continue
                def section(engobj, _items=items):
                    for f in _items:
                        f(engobj)
                getattr(block, eng)(section)


def make_sems(p):
    return {k: p.sem(k) for k in ("ld", "st", "gd", "pe", "ve", "se", "gp", "ig")}



class LNKit:
    def __init__(self, sb, tag, n=1024):
        self.s1 = sb(f"ln_s1_{tag}", [128, 1])
        self.s2 = sb(f"ln_s2_{tag}", [128, 1])
        self.mu = sb(f"ln_mu_{tag}", [128, 1])
        self.var = sb(f"ln_var_{tag}", [128, 1])
        self.sd = sb(f"ln_sd_{tag}", [128, 1])
        self.rstd = sb(f"ln_rstd_{tag}", [128, 1])
        self.scr = sb(f"ln_scr_{tag}", [128, n])
        self.tiny = sb(f"ln_tiny_{tag}", [128, 1])
        self.xc = sb(f"ln_xc_{tag}", [128, n])


def emit_ln(p, kit, xpo, out_ap, se, ve, eps=1e-5, n=1024.0):
    p.wait("vector", ve)
    p.op("vector", "tensor_scalar", kit.mu[:], kit.s1[:], 1.0 / n, None, ALU.mult,
         inc=ve)
    p.wait("vector", ve)
    p.op("vector", "tensor_scalar", kit.xc[:], xpo, kit.mu[:], None, ALU.subtract,
         inc=ve)
    p.wait("scalar", ve)
    p.op("scalar", "activation", kit.scr[:], kit.xc[:], AF.Square, accum_out=kit.s2[:])
    p.op("scalar", "activation", kit.tiny[:], kit.s2[:], AF.Copy, bias=0.0, scale=1.0,
         inc=se)
    p.wait("vector", se)
    p.op("vector", "tensor_scalar", kit.var[:], kit.s2[:], 1.0 / n, eps, ALU.mult,
         ALU.add, inc=ve)
    p.wait("scalar", ve)
    p.op("scalar", "activation", kit.sd[:], kit.var[:], AF.Sqrt)
    p.op("scalar", "activation", kit.tiny[:], kit.sd[:], AF.Copy, bias=0.0, scale=1.0,
         inc=se)
    p.wait("vector", se)
    p.op("vector", "reciprocal", kit.rstd[:], kit.sd[:], inc=ve)
    p.wait("vector", ve)
    p.op("vector", "tensor_scalar", out_ap, kit.xc[:], kit.rstd[:], None, ALU.mult,
         inc=ve)



def build_stageA(nc, ctx, p, d, s, phases=("pre", "a0", "a0b", "a1", "a2")):
    import os as _os
    sb = lambda n_, s_, t_=F32: ctx.enter_context(nc.sbuf_tensor(n_, s_, t_))
    ld, st, gd = s["ld"], s["st"], s["gd"]
    pe, ve, se, gp = s["pe"], s["ve"], s["se"], s["gp"]

    NT, HC, TC = 64, 8, 16

    bqc = sb("A_bq", [128, 8])
    identf = sb("A_identf", [128, 128])
    identb = sb("A_identb", [128, 128], BF16)
    pmf = sb("A_pmf", [128, 128])
    dpart = sb("A_dpart", [128, 16])
    dbuf = sb("A_dbuf", [128, 64])
    dinv = sb("A_dinv", [128, 64])
    kit = LNKit(sb, "A")

    p.op("gpsimd", "iota", pmf[:], pattern=[[-1, 128]], base=0, channel_multiplier=1,
         allow_small_or_imprecise_dtypes=True, inc=gp)
    p.wait("vector", gp)
    p.op("vector", "tensor_scalar", identf[:], pmf[:], 0.0, None, ALU.is_equal, inc=ve)
    p.wait("vector", ve)
    p.op("vector", "tensor_copy", identb[:], identf[:], inc=ve)
    ident_ready = ve.n
    p.op("sync", "dma_start", out=bqc[:], in_=d["bq"].rearrange("(m p) -> p m", p=128),
         allow_slow_non_contiguous=True, inc=ld, inck=16)
    bq_ready = ld.n

    if "pre" not in phases:
        return st.n
    with ExitStack() as pctx:
        xin = [pctx.enter_context(nc.sbuf_tensor(f"A_xin{i}", [128, 1024], F32))
               for i in range(2)]
        xbsb = [pctx.enter_context(nc.sbuf_tensor(f"A_xbsb{i}", [128, 1024], BF16))
                for i in range(2)]
        pre_ve0, pre_st0 = ve.n, st.n
        for i in range(NT):
            if i >= 2:
                p.wait("sync", ve, pre_ve0 + i - 1)
            p.op("sync", "dma_start", out=xin[i % 2][:],
                 in_=d["x0"][i * 128:(i + 1) * 128, :], inc=ld, inck=16)
            p.wait("vector", ld)
            if i >= 2:
                p.wait("vector", st, pre_st0 + (i - 1) * 16)
            p.op("vector", "tensor_copy", xbsb[i % 2][:], xin[i % 2][:], inc=ve)
            p.wait("scalar", ve)
            p.op("scalar", "dma_start", out=d["xbf"][i * 128:(i + 1) * 128, :],
                 in_=xbsb[i % 2][:], inc=st, inck=16)
        xbf_done = st.n

    with ExitStack() as actx:
        asb = lambda n_, s_, t_=F32: actx.enter_context(nc.sbuf_tensor(n_, s_, t_))
        aps = lambda n_, s_, t_=F32: actx.enter_context(nc.psum_tensor(n_, s_, t_))
        qt = asb("A_qt", [128, HC * 8192], BF16)
        wq = asb("A_wq", [128, HC * 1024], BF16)
        xts = [asb(f"A_xts{g}", [128, HC * 512], BF16) for g in range(2)]
        pctx0 = ExitStack()
        pq = [pctx0.enter_context(nc.psum_tensor(f"A_pq{m}", [128, 512], F32))
              for m in range(8)]

        p.wait("gpsimd", st, xbf_done)
        for hc in range(HC):
            p.op("gpsimd", "dma_start", out=wq[:, hc * 1024:(hc + 1) * 1024],
                 in_=d["Wq"][hc * 128:(hc + 1) * 128, :], inc=gd, inck=16)
        wq_ready = gd.n

        if "a0" not in phases:
            return st.n
        p.wait("sync", st, xbf_done)
        a0_pe0, a0_ve0 = pe.n, ve.n
        for tc in range(TC):
            g = tc % 2
            if tc >= 2:
                p.wait("sync", pe, a0_pe0 + tc - 1)
            for hc in range(HC):
                p.op("sync", "dma_start",
                     out=xts[g][:, hc * 512:(hc + 1) * 512],
                     in_=d["xbf"][tc * 512:(tc + 1) * 512, hc * 128:(hc + 1) * 128],
                     transpose=True, inc=ld, inck=16)
            p.wait("tensor", ld)
            if tc == 0:
                p.wait("tensor", gd, wq_ready)
            if tc >= 1:
                p.wait("tensor", ve, a0_ve0 + tc * 8)
            for hc in range(HC):
                for m in range(8):
                    p.op("tensor", "matmul", pq[m][:],
                         wq[:, hc * 1024 + m * 128: hc * 1024 + (m + 1) * 128],
                         xts[g][:, hc * 512:(hc + 1) * 512],
                         start=(hc == 0), stop=(hc == HC - 1),
                         inc=(pe if (hc == HC - 1 and m == 7) else None))
            p.wait("vector", pe)
            if tc == 0:
                p.wait("vector", ld, bq_ready)
            for m in range(8):
                p.op("vector", "tensor_scalar",
                     qt[:, m * 8192 + tc * 512: m * 8192 + (tc + 1) * 512],
                     pq[m][:], bqc[:, m:m + 1], None, ALU.add, inc=ve)

        pctx0.close()
        if "a0b" in phases:
            vst = [asb(f"A_vst{i}", [128, 1024], BF16) for i in range(2)]
            pctx1 = ExitStack()
            ptb = [pctx1.enter_context(nc.psum_tensor(f"A_ptb{k}", [128, 128], BF16))
                   for k in range(2)]
            b_ve0, b_st0 = ve.n, st.n
            p.wait("tensor", ve, ident_ready)
            p.wait("tensor", ve, b_ve0)
            nt_tr = 0
            for i in range(NT):
                for dc in range(8):
                    k = nt_tr % 2
                    if nt_tr >= 2:
                        p.wait("tensor", ve, b_ve0 + nt_tr - 1)
                    p.op("tensor", "transpose", ptb[k][:],
                         qt[:, dc * 8192 + i * 128: dc * 8192 + (i + 1) * 128],
                         identb[:], inc=pe)
                    p.wait("vector", pe)
                    if dc == 0 and i >= 2:
                        p.wait("vector", st, b_st0 + (i - 1) * 16)
                    p.op("vector", "tensor_copy",
                         vst[i % 2][:, dc * 128:(dc + 1) * 128], ptb[k][:], inc=ve)
                    nt_tr += 1
                p.wait("scalar", ve)
                p.op("scalar", "dma_start", out=d["vtok"][i * 128:(i + 1) * 128, :],
                     in_=vst[i % 2][:], inc=st, inck=16)
            pctx1.close()
        vtok_done = st.n

        if "a1" not in phases:
            return st.n
        esb = [asb(f"A_esb{i}", [128, 8192], BF16) for i in range(2)]
        a1tiny = asb("A_a1tiny", [128, 1], BF16)
        a1tiny2 = asb("A_a1tiny2", [128, 1])
        import os as _os2
        if _os2.environ.get("A1_SERIAL"):
            _p = aps("A_ps1s", [128, 512], F32)
            ps1 = [_p, _p]
        elif _os2.environ.get("A1_SPACED"):
            ps1 = []
            _dummies = []
            for k in range(2):
                ps1.append(aps(f"A_ps1{k}", [128, 512], F32))
                _dummies.append(aps(f"A_ps1d{k}", [128, 512], F32))
        else:
            ps1 = [aps(f"A_ps1{k}", [128, 512], F32) for k in range(2)]
        if _os.environ.get("A1_QTDRAM"):
            for c in range(HC):
                p.op("sync", "dma_start", out=qt[:, c * 8192:(c + 1) * 8192],
                     in_=d["qtdbg"][:, c * 8192:(c + 1) * 8192], inc=ld, inck=16)
            p.wait("tensor", ld)
        a1_se0, a1_ve0, a1_st0 = se.n, ve.n, st.n
        _a1n = int(_os.environ.get("A1N", NT))
        nsc = 0
        _a1sc = int(_os.environ.get("A1SC", TC))
        _a1hc = int(_os.environ.get("A1HC", HC))
        a1_ve_guard = ve.n
        se_after_group = {}
        for j in range(_a1n):
            for sc in range(_a1sc):
                k = nsc % 2
                if nsc < 2:
                    p.wait("tensor", ve, a1_ve_guard)
                if nsc >= 2:
                    p.wait("tensor", se, se_after_group[nsc - 2])
                for hc in range(_a1hc):
                    p.op("tensor", "matmul", ps1[k][:],
                         qt[:, hc * 8192 + j * 128: hc * 8192 + (j + 1) * 128],
                         qt[:, hc * 8192 + sc * 512: hc * 8192 + (sc + 1) * 512],
                         start=(hc == 0), stop=(hc == _a1hc - 1),
                         inc=(pe if hc == _a1hc - 1 else None))
                p.wait("scalar", pe)
                if sc == 0 and j >= 2 and not _os.environ.get("A1_NODMA"):
                    p.wait("scalar", st, a1_st0 + (j - 1) * 16)
                if sc == 0 and j >= 1:
                    p.wait("scalar", ve, a1_ve0 + j)
                p.op("scalar", "activation",
                     esb[j % 2][:, sc * 512:(sc + 1) * 512], ps1[k][:], AF.Exp,
                     scale=1.0 / 32.0, accum_out=dpart[:, sc:sc + 1], inc=se)
                se_after_group[nsc] = se.n
                nsc += 1
            p.op("scalar", "activation", a1tiny[:], esb[j % 2][:, 8191:8192], AF.Copy,
                 bias=0.0, scale=1.0)
            p.op("scalar", "activation", a1tiny2[:], dpart[:, _a1sc - 1:_a1sc], AF.Copy,
                 bias=0.0, scale=1.0, inc=se)
            p.wait("vector", se)
            p.op("vector", "tensor_reduce", dbuf[:, j:j + 1], dpart[:], AX.X, ALU.add,
                 inc=ve)
            if not _os.environ.get("A1_NODMA"):
                p.op("scalar", "dma_start", out=d["ebuf"][j, :, :], in_=esb[j % 2][:],
                     inc=st, inck=16)
        p.wait("vector", ve)
        p.op("vector", "reciprocal", dinv[:], dbuf[:], inc=ve)
        dinv_ready = ve.n
        a1_store_done = st.n
        a1_pe_done = pe.n

    if "a2" not in phases:
        return st.n
    with ExitStack() as actx:
        asb = lambda n_, s_, t_=F32: actx.enter_context(nc.sbuf_tensor(n_, s_, t_))
        aps = lambda n_, s_, t_=F32: actx.enter_context(nc.psum_tensor(n_, s_, t_))
        vsb = asb("A_vsb", [128, NT * 1024], BF16)
        es2 = [asb(f"A_es2{i}", [128, 512], BF16) for i in range(8)]
        po = [aps(f"A_po{m}", [128, 512]) for m in range(8)]
        oT = asb("A_oT", [128, 8 * 512])
        xres = [asb(f"A_xres{i}", [128, 1024]) for i in range(2)]
        xpo = asb("A_xpo", [128, 1024])
        xpo2 = asb("A_xpo2", [128, 1024])
        xout = [asb(f"A_xout{i}", [128, 1024]) for i in range(2)]

        p.wait("sync", st, vtok_done)
        p.wait("sync", pe, a1_pe_done)
        for j in range(NT):
            p.op("sync", "dma_start", out=vsb[:, j * 1024:(j + 1) * 1024],
                 in_=d["vtok"][j * 128:(j + 1) * 128, :], inc=ld, inck=16)
        v_loaded = ld.n
        p.wait("sync", st, a1_store_done)
        p.wait("gpsimd", st, a1_store_done)
        p.wait("vector", ve, dinv_ready)

        a2_st0 = st.n
        ne = 0
        pe_after_consume = {}
        epi_ve = 0
        xres_ve = {}
        sq_se = 0
        for qc in range(TC):
            for j in range(NT):
                sl = ne % 8
                q_eng = "sync" if (ne % 2 == 0) else "gpsimd"
                q_sem = ld if (ne % 2 == 0) else gd
                if ne >= 8:
                    p.wait(q_eng, pe, pe_after_consume[ne - 8])
                p.op(q_eng, "dma_start", out=es2[sl][:],
                     in_=d["ebuf"][j, :, qc * 512:(qc + 1) * 512], inc=q_sem, inck=16)
                p.wait("tensor", q_sem)
                if j == 0:
                    p.wait("tensor", ld, v_loaded)
                    if qc > 0:
                        p.wait("tensor", ve, epi_ve)
                for dc in range(8):
                    p.op("tensor", "matmul", po[dc][:],
                         vsb[:, j * 1024 + dc * 128: j * 1024 + (dc + 1) * 128],
                         es2[sl][:],
                         start=(j == 0), stop=(j == NT - 1),
                         inc=(pe if dc == 7 else None))
                pe_after_consume[ne] = pe.n
                ne += 1
            p.wait("vector", pe)
            for dc in range(8):
                p.op("vector", "tensor_copy", oT[:, dc * 512:(dc + 1) * 512],
                     po[dc][:], inc=ve)
            evac_done = ve.n
            for tsub in range(4):
                tq = qc * 4 + tsub
                if tq >= 2:
                    p.wait("sync", ve, xres_ve[tq - 2])
                p.op("sync", "dma_start", out=xres[tq % 2][:],
                     in_=d["x0"][tq * 128:(tq + 1) * 128, :], inc=ld, inck=16)
                xres_ld = ld.n
                tr_ve0 = ve.n
                for dc in range(8):
                    bank, slot = dc % 2, (dc // 2) % 2
                    p.wait("tensor", ve, max(evac_done, tr_ve0 + dc - 1))
                    p.op("tensor", "transpose",
                         po[bank][:, slot * 128:(slot + 1) * 128],
                         oT[:, dc * 512 + tsub * 128: dc * 512 + (tsub + 1) * 128],
                         identf[:], inc=pe)
                    p.wait("vector", pe)
                    if dc == 0:
                        p.wait("vector", se, sq_se)
                    p.op("vector", "tensor_scalar", xpo[:, dc * 128:(dc + 1) * 128],
                         po[bank][:, slot * 128:(slot + 1) * 128],
                         dinv[:, tq:tq + 1], None, ALU.mult, inc=ve)
                p.wait("vector", ld, xres_ld)
                p.wait("vector", ve)
                p.op("vector", "scalar_tensor_tensor", xpo2[:], xres[tq % 2][:], 1.0,
                     xpo[:], ALU.mult, ALU.add, accum_out=kit.s1[:], inc=ve)
                xres_ve[tq] = ve.n
                if tq >= 2:
                    p.wait("vector", st, a2_st0 + (tq - 1) * 16)
                p.wait("scalar", ve)
                if _os.environ.get("A2_DBG") == "xpo2":
                    p.op("vector", "tensor_copy", xout[tq % 2][:], xpo2[:], inc=ve)
                    sq_se = se.n
                elif _os.environ.get("A2_DBG") == "attn":
                    p.op("vector", "tensor_copy", xout[tq % 2][:], xpo[:], inc=ve)
                    sq_se = se.n
                else:
                    emit_ln(p, kit, xpo2[:], xout[tq % 2][:], se, ve)
                    sq_se = se.n - 1
                epi_ve = ve.n
                p.wait("scalar", ve)
                p.op("scalar", "dma_start", out=d["xs1"][tq * 128:(tq + 1) * 128, :],
                     in_=xout[tq % 2][:], inc=st, inck=16)
        stageA_done = st.n
    return stageA_done


def emit_cast_pass(p, nc, ctx, d_src, d_dst, s, tag, nt=64, width=1024):
    ld, st, ve = s["ld"], s["st"], s["ve"]
    with ExitStack() as pctx:
        xin = [pctx.enter_context(nc.sbuf_tensor(f"{tag}_ci{i}", [128, width], F32))
               for i in range(2)]
        xbs = [pctx.enter_context(nc.sbuf_tensor(f"{tag}_cb{i}", [128, width], BF16))
               for i in range(2)]
        ve0, st0 = ve.n, st.n
        for i in range(nt):
            if i >= 2:
                p.wait("sync", ve, ve0 + i - 1)
            p.op("sync", "dma_start", out=xin[i % 2][:],
                 in_=d_src[i * 128:(i + 1) * 128, :], inc=ld, inck=16)
            p.wait("vector", ld)
            if i >= 2:
                p.wait("vector", st, st0 + (i - 1) * 16)
            p.op("vector", "tensor_copy", xbs[i % 2][:], xin[i % 2][:], inc=ve)
            p.wait("scalar", ve)
            p.op("scalar", "dma_start", out=d_dst[i * 128:(i + 1) * 128, :],
                 in_=xbs[i % 2][:], inc=st, inck=16)
    return st.n


def emit_wT_load(p, nc, w_dram, wT_sb, wtmp, ptw, identb, s, nchunk_r=8, nchunk_c=8,
                 ident_ready=0):
    gd, pe, ve = s["gd"], s["pe"], s["ve"]
    R = nchunk_r * 128
    C = nchunk_c * 128
    ntr = 0
    p.wait("tensor", ve, ident_ready)
    for rc in range(nchunk_r):
        p.op("gpsimd", "dma_start", out=wtmp[:],
             in_=w_dram[rc * 128:(rc + 1) * 128, :], inc=gd, inck=16)
        p.wait("tensor", gd)
        for cc in range(nchunk_c):
            k = ntr % 2
            if ntr >= 2:
                p.wait("tensor", ve)
            p.op("tensor", "transpose", ptw[k][:],
                 wtmp[:, cc * 128:(cc + 1) * 128], identb[:], inc=pe)
            p.wait("vector", pe)
            p.op("vector", "tensor_copy",
                 wT_sb[:, cc * R + rc * 128: cc * R + (rc + 1) * 128], ptw[k][:],
                 inc=ve)
            ntr += 1
        p.wait("gpsimd", ve)
    return



def build_stageC(nc, ctx, p, d, s):
    sb = lambda n_, s_, t_=F32: ctx.enter_context(nc.sbuf_tensor(n_, s_, t_))
    ld, st, gd = s["ld"], s["st"], s["gd"]
    pe, ve, se, gp = s["pe"], s["ve"], s["se"], s["gp"]
    NT, HC, TC = 64, 8, 16

    identf = sb("C_identf", [128, 128])
    identb = sb("C_identb", [128, 128], BF16)
    pmf = sb("C_pmf", [128, 128])
    ltm = sb("C_ltm", [128, 128])
    mskC = [sb(f"C_msk{r}", [128, 512]) for r in range(4)]
    scbuf = sb("C_scbuf", [128, 64])
    scmb = sb("C_scmb", [128, 64])
    biasDall = sb("C_biasDall", [128, 64])
    dpc = sb("C_dpc", [128, 18])
    dbufC = sb("C_dbufC", [128, 64])
    dinvC = sb("C_dinvC", [128, 64])
    smaxc = sb("C_smaxc", [128, 1])
    nsmaxc = sb("C_nsmaxc", [128, 1])
    ones1r = sb("C_ones1r", [1, 128])
    sm1sb = sb("C_sm1sb", [1, 1])
    scT = sb("C_scT", [64, 128])
    scrow = sb("C_scrow", [1, 8192])
    screp = sb("C_screp", [128, 8192])
    woT = sb("C_woT", [128, HC * 1024], BF16)
    kit = LNKit(sb, "C")
    ctiny = sb("C_ctiny", [128, 1], BF16)

    p.op("gpsimd", "iota", pmf[:], pattern=[[-1, 128]], base=0, channel_multiplier=1,
         allow_small_or_imprecise_dtypes=True)
    for r in range(4):
        p.op("gpsimd", "iota", mskC[r][:], pattern=[[1, 512]], base=-128 * r,
             channel_multiplier=-1, allow_small_or_imprecise_dtypes=True)
    p.op("gpsimd", "memset", ones1r[:], 1.0, inc=gp)
    p.wait("vector", gp)
    p.op("vector", "tensor_scalar", identf[:], pmf[:], 0.0, None, ALU.is_equal, inc=ve)
    p.wait("vector", ve)
    p.op("vector", "tensor_copy", identb[:], identf[:])
    p.op("vector", "tensor_scalar", ltm[:], pmf[:], 0.0, None, ALU.is_ge)
    for r in range(4):
        p.op("vector", "tensor_scalar", mskC[r][:], mskC[r][:], 0.0, None, ALU.is_ge)
    p.op("vector", "nop", inc=ve)
    ident_ready = ve.n

    x2bf_done = emit_cast_pass(p, nc, ctx, d["x2"], d["x2bf"], s, "C")

    with ExitStack() as cctx:
        csb = lambda n_, s_, t_=F32: cctx.enter_context(nc.sbuf_tensor(n_, s_, t_))
        cps = lambda n_, s_, t_=F32: cctx.enter_context(nc.psum_tensor(n_, s_, t_))
        wjT = csb("C_wjT", [128, HC * 1024], BF16)
        wjvT = csb("C_wjvT", [128, HC * 1024], BF16)
        wtmp = csb("C_wtmp", [128, 1024], BF16)
        xts = [csb(f"C_xts{g}", [128, HC * 512], BF16) for g in range(2)]
        vstg = [csb(f"C_vstg{i}", [128, 1024], BF16) for i in range(2)]
        scp = csb("C_scp", [128, 2])
        wctx = ExitStack()
        ptw = [wctx.enter_context(nc.psum_tensor(f"C_ptw{k}", [128, 128], BF16))
               for k in range(2)]
        p.wait("gpsimd", st, x2bf_done)
        emit_wT_load(p, nc, d["wj"], wjT, wtmp, ptw, identb, s, ident_ready=ident_ready)
        emit_wT_load(p, nc, d["wjv"], wjvT, wtmp, ptw, identb, s, ident_ready=ident_ready)
        emit_wT_load(p, nc, d["wo"], woT, wtmp, ptw, identb, s, ident_ready=ident_ready)
        wT_ready_ve = ve.n
        wctx.close()

        py = [cps(f"C_py{m}", [128, 512]) for m in range(8)]
        p.wait("sync", st, x2bf_done)
        st_i = {}
        pe_tc = {}
        se_tile = {}
        ve_tile = {}
        for tc in range(TC):
            g = tc % 2
            if tc >= 2:
                p.wait("sync", pe, pe_tc[tc - 2])
            for hc in range(HC):
                p.op("sync", "dma_start",
                     out=xts[g][:, hc * 512:(hc + 1) * 512],
                     in_=d["x2bf"][tc * 512:(tc + 1) * 512, hc * 128:(hc + 1) * 128],
                     transpose=True, inc=ld, inck=16)
            p.wait("tensor", ld)
            if tc == 0:
                p.wait("tensor", ve, wT_ready_ve)
            for i4 in range(4):
                i = tc * 4 + i4
                bg = 4 * (i % 2)
                if i >= 2:
                    p.wait("tensor", se, se_tile[i - 2])
                    p.wait("tensor", ve, ve_tile[i - 2])
                for hc in range(HC):
                    lhs = xts[g][:, hc * 512 + i4 * 128: hc * 512 + (i4 + 1) * 128]
                    for half in range(2):
                        p.op("tensor", "matmul", py[bg + half][:],
                             lhs, wjT[:, hc * 1024 + half * 512: hc * 1024 + (half + 1) * 512],
                             start=(hc == 0), stop=(hc == HC - 1))
                        p.op("tensor", "matmul", py[bg + 2 + half][:],
                             lhs, wjvT[:, hc * 1024 + half * 512: hc * 1024 + (half + 1) * 512],
                             start=(hc == 0), stop=(hc == HC - 1),
                             inc=(pe if (hc == HC - 1 and half == 1) else None))
                pe_tc[tc] = pe.n
                p.wait("scalar", pe)
                p.op("scalar", "activation", kit.scr[:, 0:512], py[bg][:], AF.Square,
                     accum_out=scp[:, 0:1])
                p.op("scalar", "activation", kit.scr[:, 0:512], py[bg + 1][:], AF.Square,
                     accum_out=scp[:, 1:2])
                p.op("scalar", "activation", ctiny[:], scp[:, 1:2], AF.Copy,
                     bias=0.0, scale=1.0, inc=se)
                se_tile[i] = se.n
                p.wait("vector", se)
                if i >= 2:
                    p.wait("vector", st, st_i[i - 2])
                p.op("vector", "tensor_scalar", scbuf[:, i:i + 1], scp[:, 0:1],
                     scp[:, 1:2], 1.0 / 32.0, ALU.add, ALU.mult)
                p.op("vector", "tensor_copy", vstg[i % 2][:, 0:512], py[bg + 2][:])
                p.op("vector", "tensor_copy", vstg[i % 2][:, 512:1024], py[bg + 3][:],
                     inc=ve)
                ve_tile[i] = ve.n
                p.wait("scalar", ve)
                p.op("scalar", "dma_start", out=d["vtokC"][i * 128:(i + 1) * 128, :],
                     in_=vstg[i % 2][:], inc=st, inck=16)
                st_i[i] = st.n
        vtokC_done = st.n

    with ExitStack() as cctx:
        csb = lambda n_, s_, t_=F32: cctx.enter_context(nc.sbuf_tensor(n_, s_, t_))
        cps = lambda n_, s_, t_=F32: cctx.enter_context(nc.psum_tensor(n_, s_, t_))
        pctx = ExitStack()
        ptf = pctx.enter_context(nc.psum_tensor("C_ptf", [128, 512], F32))
        p.wait("vector", ve)
        p.op("vector", "tensor_reduce", smaxc[:], scbuf[:], AX.X, ALU.max, inc=ve)
        p.wait("tensor", ve)
        p.op("tensor", "transpose", ptf[0:1, 0:128], smaxc[:], identf[:], inc=pe)
        p.wait("vector", pe)
        p.op("vector", "tensor_reduce", sm1sb[:], ptf[0:1, 0:128], AX.X, ALU.max,
             inc=ve)
        p.wait("tensor", ve)
        p.op("tensor", "transpose", ptf[0:64, 0:128], scbuf[:, 0:64], identf[:], inc=pe)
        p.wait("vector", pe)
        p.op("vector", "tensor_copy", scT[:], ptf[0:64, 0:128], inc=ve)
        p.wait("scalar", ve)
        p.op("scalar", "dma_start", out=d["scd"][:, :], in_=scT[:], inc=st, inck=16)
        p.wait("sync", st)
        p.op("sync", "dma_start", out=scrow[:],
             in_=d["scd"].rearrange("(o a) b -> o (a b)", o=1), inc=ld, inck=16)
        p.wait("tensor", ld)
        p.wait("tensor", ve)
        for c in range(16):
            p.op("tensor", "matmul", ptf[:],
                 ones1r[:], scrow[0:1, c * 512:(c + 1) * 512],
                 start=True, stop=True, inc=pe)
            p.wait("vector", pe)
            p.op("vector", "tensor_copy", screp[:, c * 512:(c + 1) * 512], ptf[:],
                 inc=ve)
            p.wait("tensor", ve)
        p.op("tensor", "matmul", ptf[:, 0:1], ones1r[:], sm1sb[:],
             start=True, stop=True, inc=pe)
        p.wait("vector", pe)
        p.op("vector", "tensor_copy", smaxc[:], ptf[:, 0:1], inc=ve)
        p.wait("vector", ve)
        p.op("vector", "tensor_scalar", nsmaxc[:], smaxc[:], -1.0, None, ALU.mult)
        p.op("vector", "tensor_scalar", scmb[:], scbuf[:], smaxc[:, 0:1], None,
             ALU.subtract)
        p.op("vector", "tensor_scalar", biasDall[:], scbuf[:], nsmaxc[:, 0:1], None,
             ALU.mult, inc=ve)
        prep_ve = ve.n

        edia = csb("C_edia", [128, 128])
        dscr = kit.scr
        dve_i = 0
        p.wait("scalar", ve, prep_ve)
        for i in range(NT):
            fc = (i * 128) // 512
            rm = i * 128 - fc * 512
            if i >= 1:
                p.wait("scalar", ve)
            p.op("vector", "memset", dpc[:], 0.0, inc=ve)
            p.wait("scalar", ve)
            ncol = 0
            for c in range(fc):
                p.op("scalar", "activation", dscr[:, 0:512],
                     screp[:, c * 512:(c + 1) * 512], AF.Exp,
                     scale=scbuf[:, i:i + 1], bias=biasDall[:, i:i + 1],
                     accum_out=dpc[:, ncol:ncol + 1])
                ncol += 1
            if rm > 0:
                p.op("scalar", "activation", dscr[:, 0:rm],
                     screp[:, fc * 512:fc * 512 + rm], AF.Exp,
                     scale=scbuf[:, i:i + 1], bias=biasDall[:, i:i + 1],
                     accum_out=dpc[:, ncol:ncol + 1])
                ncol += 1
            p.op("scalar", "activation", edia[:],
                 screp[:, i * 128:(i + 1) * 128], AF.Exp,
                 scale=scbuf[:, i:i + 1], bias=biasDall[:, i:i + 1])
            p.op("scalar", "activation", ctiny[:], edia[:, 127:128], AF.Copy,
                 bias=0.0, scale=1.0, inc=se)
            p.wait("vector", se)
            p.op("vector", "tensor_tensor", edia[:], edia[:], ltm[:], ALU.mult, inc=ve)
            p.wait("vector", ve)
            p.op("vector", "tensor_reduce", kit.s1[:], edia[:], AX.X, ALU.add, inc=ve)
            p.wait("vector", ve)
            p.op("vector", "tensor_reduce", kit.s2[:], dpc[:], AX.X, ALU.add, inc=ve)
            p.wait("vector", ve)
            p.op("vector", "tensor_tensor", dbufC[:, i:i + 1], kit.s1[:], kit.s2[:],
                 ALU.add, inc=ve)
        p.wait("vector", ve)
        p.op("vector", "reciprocal", dinvC[:], dbufC[:], inc=ve)
        dinv_ready = ve.n

        pctx.close()
        esbC = [csb(f"C_esb{k}", [128, 512], BF16) for k in range(2)]
        vstr = [csb(f"C_vstr{k}", [128, 1024], BF16) for k in range(4)]
        oTC = csb("C_oTC", [128, 8 * 512], BF16)
        xpoC = csb("C_xpoC", [128, 1024])
        xpo2C = csb("C_xpo2C", [128, 1024])
        xresC = [csb(f"C_xresC{k}", [128, 1024]) for k in range(2)]
        xoutC = [csb(f"C_xoutC{k}", [128, 1024]) for k in range(2)]
        poC = [cps(f"C_poC{m}", [128, 512]) for m in range(8)]
        p.wait("sync", st, vtokC_done)
        nv = 0
        pe_v = {}
        se_et = {}
        c2_st0 = st.n
        epi_ve = 0
        xres_ve = {}
        for qc in range(TC):
            jmax = qc * 4 + 4
            for j in range(jmax):
                slv = nv % 4
                if nv >= 4:
                    p.wait("sync", pe, pe_v[nv - 4])
                p.op("sync", "dma_start", out=vstr[slv][:],
                     in_=d["vtokC"][j * 128:(j + 1) * 128, :], inc=ld, inck=16)
                ke = j % 2
                if j >= 2 or qc > 0:
                    p.wait("scalar", pe)
                p.wait("scalar", ve, dinv_ready)
                p.op("scalar", "activation", esbC[ke][:],
                     screp[:, qc * 512:(qc + 1) * 512], AF.Exp,
                     scale=scmb[:, j:j + 1])
                p.op("scalar", "activation", ctiny[:], esbC[ke][:, 511:512], AF.Copy,
                     bias=0.0, scale=1.0, inc=se)
                se_et[(qc, j)] = se.n
                if j >= qc * 4:
                    r = j - qc * 4
                    p.wait("vector", se)
                    p.op("vector", "tensor_tensor", esbC[ke][:], esbC[ke][:],
                         mskC[r][:], ALU.mult, inc=ve)
                    p.wait("tensor", ve)
                else:
                    p.wait("tensor", se)
                p.wait("tensor", ld)
                if qc == 0 and j == 0:
                    pass
                if j == 0 and qc > 0:
                    p.wait("tensor", ve, epi_ve)
                for dc in range(8):
                    p.op("tensor", "matmul", poC[dc][:],
                         vstr[slv][:, dc * 128:(dc + 1) * 128],
                         esbC[ke][:],
                         start=(j == 0), stop=(j == jmax - 1),
                         inc=(pe if dc == 7 else None))
                pe_v[nv] = pe.n
                nv += 1
            p.wait("vector", pe)
            for dc in range(8):
                p.op("vector", "tensor_copy", oTC[:, dc * 512:(dc + 1) * 512],
                     poC[dc][:], inc=ve)
            evac_done = ve.n
            for tsub in range(4):
                tq = qc * 4 + tsub
                if tq >= 2:
                    p.wait("sync", ve, xres_ve[tq - 2])
                p.op("sync", "dma_start", out=xresC[tq % 2][:],
                     in_=d["x2"][tq * 128:(tq + 1) * 128, :], inc=ld, inck=16)
                xres_ld = ld.n
                p.wait("tensor", ve, evac_done if tsub == 0 else xres_ve[tq - 1])
                for hc2 in range(HC):
                    for half in range(2):
                        p.op("tensor", "matmul", poC[half][:],
                             oTC[:, hc2 * 512 + tsub * 128: hc2 * 512 + (tsub + 1) * 128],
                             woT[:, hc2 * 1024 + half * 512: hc2 * 1024 + (half + 1) * 512],
                             start=(hc2 == 0), stop=(hc2 == HC - 1),
                             inc=(pe if (hc2 == HC - 1 and half == 1) else None))
                p.wait("vector", pe)
                for half in range(2):
                    p.op("vector", "tensor_scalar",
                         xpoC[:, half * 512:(half + 1) * 512], poC[half][:],
                         dinvC[:, tq:tq + 1], None, ALU.mult,
                         inc=(ve if half == 1 else None))
                p.wait("vector", ld, xres_ld)
                p.wait("vector", ve)
                p.op("vector", "scalar_tensor_tensor", xpo2C[:], xresC[tq % 2][:], 1.0,
                     xpoC[:], ALU.mult, ALU.add, accum_out=kit.s1[:], inc=ve)
                xres_ve[tq] = ve.n
                if tq >= 2:
                    p.wait("vector", st, c2_st0 + (tq - 1) * 16)
                emit_ln(p, kit, xpo2C[:], xoutC[tq % 2][:], se, ve)
                epi_ve = ve.n
                p.wait("scalar", ve)
                p.op("scalar", "dma_start", out=d["xs3"][tq * 128:(tq + 1) * 128, :],
                     in_=xoutC[tq % 2][:], inc=st, inck=16)
        stageC_done = st.n
    return stageC_done


def emit_ln_affine(p, kit, xpo, out_ap, se, ve, lgrep=None, lbrep=None,
                   eps=1e-5, n=1024.0):
    p.wait("vector", ve)
    p.op("vector", "tensor_scalar", kit.mu[:], kit.s1[:], 1.0 / n, None, ALU.mult,
         inc=ve)
    p.wait("vector", ve)
    p.op("vector", "tensor_scalar", kit.xc[:], xpo, kit.mu[:], None, ALU.subtract,
         inc=ve)
    p.wait("scalar", ve)
    p.op("scalar", "activation", kit.scr[:], kit.xc[:], AF.Square, accum_out=kit.s2[:])
    p.op("scalar", "activation", kit.tiny[:], kit.s2[:], AF.Copy, bias=0.0, scale=1.0,
         inc=se)
    p.wait("vector", se)
    p.op("vector", "tensor_scalar", kit.var[:], kit.s2[:], 1.0 / n, eps, ALU.mult,
         ALU.add, inc=ve)
    p.wait("scalar", ve)
    p.op("scalar", "activation", kit.sd[:], kit.var[:], AF.Sqrt)
    p.op("scalar", "activation", kit.tiny[:], kit.sd[:], AF.Copy, bias=0.0, scale=1.0,
         inc=se)
    p.wait("vector", se)
    p.op("vector", "reciprocal", kit.rstd[:], kit.sd[:], inc=ve)
    p.wait("vector", ve)
    if lgrep is None:
        p.op("vector", "tensor_scalar", out_ap, kit.xc[:], kit.rstd[:], None, ALU.mult,
             inc=ve)
    else:
        p.op("vector", "scalar_tensor_tensor", kit.scr[:], kit.xc[:], kit.rstd[:],
             lgrep, ALU.mult, ALU.mult, inc=ve)
        p.wait("vector", ve)
        p.op("vector", "tensor_tensor", out_ap, kit.scr[:], lbrep, ALU.add, inc=ve)


def emit_replicate_row(p, nc, row_ap, out_sb, ptf, ones1r, s, width):
    pe, ve = s["pe"], s["ve"]
    p.wait("tensor", ve)
    p.op("tensor", "matmul", ptf[:, 0:width], ones1r[:], row_ap, start=True, stop=True,
         inc=pe)
    p.wait("vector", pe)
    p.op("vector", "tensor_copy", out_sb, ptf[:, 0:width], inc=ve)


def build_stageD(nc, ctx, p, d, s):
    sb = lambda n_, s_, t_=F32: ctx.enter_context(nc.sbuf_tensor(n_, s_, t_))
    ld, st, gd = s["ld"], s["st"], s["gd"]
    pe, ve, se, gp = s["pe"], s["ve"], s["se"], s["gp"]
    NT, HC, TC = 64, 8, 16

    identf = sb("D_identf", [128, 128])
    identb = sb("D_identb", [128, 128], BF16)
    pmf = sb("D_pmf", [128, 128])
    mskD = [sb(f"D_msk{r}", [128, 512]) for r in range(8)]
    ones1r = sb("D_ones1r", [1, 128])
    onescol = sb("D_onescol", [128, 1], BF16)
    kit = LNKit(sb, "D")
    dtiny = sb("D_tiny", [128, 1], BF16)

    p.op("gpsimd", "iota", pmf[:], pattern=[[-1, 128]], base=0, channel_multiplier=1,
         allow_small_or_imprecise_dtypes=True)
    p.op("gpsimd", "memset", ones1r[:], 1.0)
    p.op("gpsimd", "memset", onescol[:], 1.0, inc=gp)
    p.wait("vector", gp)
    p.op("vector", "tensor_scalar", identf[:], pmf[:], 0.0, None, ALU.is_equal, inc=ve)
    p.wait("vector", ve)
    p.op("vector", "tensor_copy", identb[:], identf[:])
    for r in range(8):
        p.op("gpsimd", "iota", kit.scr[:, 0:512], pattern=[[1, 512]],
             base=512 - 128 * r, channel_multiplier=-1,
             allow_small_or_imprecise_dtypes=True, inc=gp)
        p.wait("vector", gp)
        p.op("vector", "tensor_scalar", kit.xc[:, 0:512], kit.scr[:, 0:512], 0.0, None,
             ALU.is_ge, inc=ve)
        p.wait("vector", ve)
        p.op("vector", "tensor_scalar", kit.scr[:, 0:512], kit.scr[:, 0:512], 512.0,
             None, ALU.is_le, inc=ve)
        p.wait("vector", ve)
        p.op("vector", "tensor_tensor", mskD[r][:], kit.xc[:, 0:512],
             kit.scr[:, 0:512], ALU.mult, inc=ve)
        p.wait("gpsimd", ve)
    ident_ready = ve.n

    import os as _os
    _dph = set(_os.environ.get("DPH", "d12,d3a,d3b,d3c,fin").split(","))
    _dnl = int(_os.environ.get("DNL", 2))
    zin = d["x3"]
    for l in range(_dnl):
        p.wait("sync", st)
        zbf_done = emit_cast_pass(p, nc, ctx, zin, d["zbf"], s, f"D{l}")
        if "d12" not in _dph:
            return st.n
        with ExitStack() as dctx:
            dsb = lambda n_, s_, t_=F32: dctx.enter_context(nc.sbuf_tensor(f"{n_}_{l}", s_, t_))
            dps = lambda n_, s_, t_=F32: dctx.enter_context(nc.psum_tensor(f"{n_}_{l}", s_, t_))
            wqkvsb = dsb("D_wqkv", [128, HC * 1536], BF16)
            wosb = dsb("D_wosb", [128, 4 * 1024], BF16)
            qsrow = dsb("D_qsrow", [1, 512])
            ksrow = dsb("D_ksrow", [1, 512])
            qsrep = dsb("D_qsrep", [128, 512])
            ksrep = dsb("D_ksrep", [128, 512])
            ush = [[dsb(f"D_ush{g}_{hc}", [128, 512], BF16) for hc in range(HC)]
                   for g in range(2)]
            shs = [[dsb(f"D_shs{g}_{hc}", [128, 512], BF16) for hc in range(4)]
                   for g in range(2)]
            qwork = dsb("D_qwork", [128, 512])
            kwork = dsb("D_kwork", [128, 512])
            sqs = dsb("D_sqs", [128, 8])
            sqs2 = dsb("D_sqs2", [128, 8])
            rwork = dsb("D_rwork", [128, 512])
            t1w = dsb("D_t1w", [128, 512])
            t2w = dsb("D_t2w", [128, 512])
            qfb = dsb("D_qfb", [128, 512], BF16)
            kfb = dsb("D_kfb", [128, 512], BF16)
            tblq = [dsb(f"D_tblq{i}", [128, 64 * 4]) for i in range(2)]
            vring = [dsb(f"D_vring{i}", [128, 512], BF16) for i in range(8)]
            kTr = dsb("D_kTr", [64, 8 * 1024], BF16)
            qTw = dsb("D_qTw", [64, 8 * 512], BF16)
            esbD = [dsb(f"D_esbD{i}", [128, 512], BF16) for i in range(2)]
            invDsb = dsb("D_invDsb", [1, 512])
            rinvsb = dsb("D_rinvsb", [64, 512])
            aT = dsb("D_aT", [128, 4 * 512], BF16)
            xw = dsb("D_xw", [128, 1024])
            zres = [dsb(f"D_zres{i}", [128, 1024]) for i in range(2)]
            z2t = [dsb(f"D_z2t{i}", [128, 1024]) for i in range(2)]
            pq = [dps(f"D_pq{m}", [128, 512]) for m in range(6)]
            ptb = [dps(f"D_ptb{k}", [128, 128], BF16) for k in range(2)]

            p.wait("gpsimd", st, zbf_done)
            p.wait("sync", st, zbf_done)
            p.wait("vector", st, zbf_done)
            for hc in range(HC):
                p.op("gpsimd", "dma_start", out=wqkvsb[:, hc * 1536:(hc + 1) * 1536],
                     in_=d["wqkv"][l, hc * 128:(hc + 1) * 128, :], inc=gd, inck=16)
            for b4 in range(4):
                p.op("gpsimd", "dma_start", out=wosb[:, b4 * 1024:(b4 + 1) * 1024],
                     in_=d["wo_l"][l, b4 * 128:(b4 + 1) * 128, :], inc=gd, inck=16)
            w_ready = gd.n
            p.op("sync", "dma_start", out=qsrow[:], in_=d["qs_t"][l:l + 1, :],
                 inc=ld, inck=16)
            p.op("sync", "dma_start", out=ksrow[:], in_=d["ks_t"][l:l + 1, :],
                 inc=ld, inck=16)
            p.wait("tensor", ld)
            emit_replicate_row(p, nc, qsrow[:], qsrep[:], pq[0], ones1r, s, 512)
            emit_replicate_row(p, nc, ksrow[:], ksrep[:], pq[0], ones1r, s, 512)
            rep_ready = ve.n

            p.wait("sync", st, zbf_done)
            ve_slab = {}
            ve_rot = {}
            pe_tile = {}
            ve_evac = {}
            se_l2 = {}
            ve_attn_done = 0
            st_z2 = {}
            _d12tc = int(_os.environ.get("D12TC", TC))
            for tc in range(_d12tc):
                g = tc % 2
                if tc >= 2:
                    p.wait("sync", pe, pe_tile[(tc - 2, 3)])
                for hc in range(HC):
                    p.op("sync", "dma_start", out=ush[g][hc][:],
                         in_=d["zbf"][tc * 512:(tc + 1) * 512,
                                      hc * 128:(hc + 1) * 128],
                         transpose=True, inc=ld, inck=16)
                slab_ld = ld.n
                p.wait("vector", ld, slab_ld)
                if tc >= 2:
                    p.wait("vector", pe, pe_tile[(tc - 2, 3)])
                for hc in range(4):
                    if tc == 0:
                        p.op("vector", "memset", shs[g][hc][:, 0:1], 0.0)
                    else:
                        p.op("vector", "tensor_copy", shs[g][hc][:, 0:1],
                             ush[g ^ 1][hc + 4][:, 511:512])
                    p.op("vector", "tensor_copy", shs[g][hc][:, 1:512],
                         ush[g][hc + 4][:, 0:511],
                         inc=(ve if hc == 3 else None))
                ve_slab[tc] = ve.n

                for i4 in range(4):
                    i = tc * 4 + i4
                    bq3 = 3 * (i % 2)
                    if i >= 2:
                        p.wait("sync", ve, ve_rot.get(i - 2, 0))
                    p.op("sync", "dma_start", out=tblq[i % 2][:, 0:64],
                         in_=d["cq"][i * 128:(i + 1) * 128, :], inc=ld, inck=16)
                    p.op("sync", "dma_start", out=tblq[i % 2][:, 64:128],
                         in_=d["sq"][i * 128:(i + 1) * 128, :], inc=ld, inck=16)
                    p.op("sync", "dma_start", out=tblq[i % 2][:, 128:192],
                         in_=d["ck"][i * 128:(i + 1) * 128, :], inc=ld, inck=16)
                    p.op("sync", "dma_start", out=tblq[i % 2][:, 192:256],
                         in_=d["sk"][i * 128:(i + 1) * 128, :], inc=ld, inck=16)
                    tbl_ld = ld.n
                    p.wait("tensor", ve, ve_slab[tc])
                    if i == 0:
                        p.wait("tensor", gd, w_ready)
                        p.wait("tensor", ve, max(ident_ready, rep_ready))
                    if i >= 2:
                        p.wait("tensor", ve, ve_evac.get(i - 2, 0))
                        p.wait("tensor", se, se_l2.get(i - 2, 0))
                    if i4 == 0 and ve_attn_done:
                        p.wait("tensor", ve, ve_attn_done)
                        p.wait("tensor", se)
                    for hc in range(HC):
                        lhs = (ush[g][hc] if hc < 4 else shs[g][hc - 4])
                        lhs = lhs[:, i4 * 128:(i4 + 1) * 128]
                        for part in range(3):
                            p.op("tensor", "matmul", pq[bq3 + part][:],
                                 lhs,
                                 wqkvsb[:, hc * 1536 + part * 512:
                                        hc * 1536 + (part + 1) * 512],
                                 start=(hc == 0), stop=(hc == HC - 1),
                                 inc=(pe if (hc == HC - 1 and part == 2) else None))
                    pe_tile[(tc, i4)] = pe.n
                    p.wait("vector", pe)
                    p.op("vector", "tensor_copy", vring[i % 8][:], pq[bq3 + 2][:])
                    p.op("vector", "tensor_copy", qwork[:], pq[bq3][:])
                    p.op("vector", "tensor_copy", kwork[:], pq[bq3 + 1][:], inc=ve)
                    ve_evac[i] = ve.n
                    for which, work, srep, ct_off, st_off, outb in (
                            ("q", qwork, qsrep, 0, 64, qfb),
                            ("k", kwork, ksrep, 128, 192, kfb)):
                        w3 = work[:].rearrange("p (h e) -> p h e", e=64)
                        p.wait("vector", ve)
                        p.op("vector", "tensor_tensor",
                             rwork[:].rearrange("p (h e) -> p h e", e=64), w3, w3,
                             ALU.mult, inc=ve)
                        p.wait("vector", ve)
                        p.op("vector", "tensor_reduce", sqs[:],
                             rwork[:].rearrange("p (h e) -> p h e", e=64), AX.X,
                             ALU.add, inc=ve)
                        p.wait("scalar", ve)
                        p.op("scalar", "activation", sqs2[:], sqs[:], AF.Sqrt)
                        p.op("scalar", "activation", dtiny[:], sqs2[:, 7:8], AF.Copy,
                             bias=0.0, scale=1.0, inc=se)
                        p.wait("vector", se)
                        p.op("vector", "tensor_scalar", sqs2[:], sqs2[:], 1e-12, None,
                             ALU.max, inc=ve)
                        p.wait("vector", ve)
                        p.op("vector", "reciprocal", sqs2[:], sqs2[:], inc=ve)
                        p.wait("vector", ve)
                        sqb = sqs2[:].rearrange("p (h one) -> p h one", one=1) \
                            .broadcast_to([128, 8, 64])
                        p.op("vector", "tensor_tensor",
                             work[:].rearrange("p (h e) -> p h e", e=64), w3, sqb,
                             ALU.mult, inc=ve)
                        p.wait("vector", ve)
                        p.op("vector", "tensor_tensor", work[:], work[:], srep[:],
                             ALU.mult, inc=ve)
                        p.wait("vector", ve)
                        w4 = work[:].rearrange("p (h two e) -> p h two e", two=2, e=32)
                        r4 = rwork[:].rearrange("p (h two e) -> p h two e", two=2, e=32)
                        p.op("vector", "tensor_scalar", r4[:, :, 0:1, :],
                             w4[:, :, 1:2, :], -1.0, None, ALU.mult)
                        p.op("vector", "tensor_scalar", r4[:, :, 1:2, :],
                             w4[:, :, 0:1, :], 1.0, None, ALU.mult, inc=ve)
                        p.wait("vector", ld, tbl_ld)
                        p.wait("vector", ve)
                        ctb = tblq[i % 2][:, ct_off:ct_off + 64] \
                            .rearrange("p (one e) -> p one e", one=1) \
                            .broadcast_to([128, 8, 64])
                        stb = tblq[i % 2][:, st_off:st_off + 64] \
                            .rearrange("p (one e) -> p one e", one=1) \
                            .broadcast_to([128, 8, 64])
                        p.op("vector", "tensor_tensor",
                             t1w[:].rearrange("p (h e) -> p h e", e=64),
                             work[:].rearrange("p (h e) -> p h e", e=64), ctb,
                             ALU.mult)
                        p.op("vector", "tensor_tensor",
                             t2w[:].rearrange("p (h e) -> p h e", e=64),
                             rwork[:].rearrange("p (h e) -> p h e", e=64), stb,
                             ALU.mult, inc=ve)
                        p.wait("vector", ve)
                        p.op("vector", "tensor_tensor", outb[:], t1w[:], t2w[:],
                             ALU.add, inc=ve)
                    se_l2[i] = se.n
                    ve_rot[i] = ve.n
                    if "dbgD" in d and i == int(_os.environ.get("DDUMPI", -1)):
                        p.wait("scalar", ve)
                        p.op("scalar", "dma_start", out=d["dbgD"][0:128, 0:512],
                             in_=ush[g][0][:], inc=st, inck=16)
                        p.op("scalar", "dma_start", out=d["dbgD"][0:128, 512:1024],
                             in_=shs[g][0][:], inc=st, inck=16)
                        p.op("scalar", "dma_start", out=d["dbgD"][0:128, 1024:1536],
                             in_=wqkvsb[:, 0:512], inc=st, inck=16)
                        p.op("scalar", "dma_start", out=d["dbgD"][0:128, 1536:2048],
                             in_=vring[i % 8][:], inc=st, inck=16)
                    ntr0 = 0
                    for which, src, dstbase in (("q", qfb, None), ("k", kfb, None)):
                        for hh in range(8):
                            kk = ntr0 % 2
                            p.wait("tensor", ve)
                            p.op("tensor", "transpose", ptb[kk][0:64, 0:128],
                                 src[:, hh * 64:(hh + 1) * 64], identb[:], inc=pe)
                            p.wait("vector", pe)
                            if which == "q":
                                dst = qTw[0:64, hh * 512 + i4 * 128:
                                          hh * 512 + (i4 + 1) * 128]
                            else:
                                dst = kTr[0:64, hh * 1024 + (i % 8) * 128:
                                          hh * 1024 + ((i % 8) + 1) * 128]
                            p.op("vector", "tensor_copy", dst, ptb[kk][0:64, 0:128],
                                 inc=ve)
                            ntr0 += 1
                if _os.environ.get("D12SUB") == "qkv":
                    continue
                r0 = 4 if tc == 0 else 0
                qkv_done_ve = ve.n
                _skipattn = _os.environ.get("D12SUB") == "skipattn"
                psT = [pq[0], pq[1]]
                poW = pq[2]
                pD = pq[3]
                pR = pq[4]
                for hh in range(0 if _skipattn else 8):
                    if hh == 0:
                        p.wait("tensor", ve, qkv_done_ve)
                    first = True
                    se_r = {}
                    for r in range(r0, 8):
                        gt = (tc - 1) * 4 + r
                        rp = gt % 8
                        kk = r % 2
                        p.wait("tensor", se)
                        p.op("tensor", "matmul", psT[kk][:],
                             kTr[0:64, hh * 1024 + rp * 128:hh * 1024 + (rp + 1) * 128],
                             qTw[0:64, hh * 512:(hh + 1) * 512],
                             start=True, stop=True, inc=pe)
                        p.wait("scalar", pe)
                        p.op("scalar", "activation", esbD[kk][:], psT[kk][:], AF.Exp,
                             scale=8.0)
                        p.op("scalar", "activation", dtiny[:], esbD[kk][:, 511:512],
                             AF.Copy, bias=0.0, scale=1.0, inc=se)
                        se_r[r] = se.n
                        p.wait("vector", se)
                        p.op("vector", "tensor_tensor", esbD[kk][:], esbD[kk][:],
                             mskD[r][:], ALU.mult, inc=ve)
                        p.wait("tensor", ve)
                        p.op("tensor", "matmul", poW[0:64, :],
                             vring[rp][:, hh * 64:(hh + 1) * 64], esbD[kk][:],
                             start=first, stop=(r == 7))
                        p.op("tensor", "matmul", pD[0:1, :],
                             onescol[:], esbD[kk][:],
                             start=first, stop=(r == 7),
                             inc=(pe if r == 7 else None))
                        first = False
                    p.wait("vector", pe)
                    p.op("vector", "reciprocal", invDsb[:], pD[0:1, :], inc=ve)
                    p.wait("tensor", ve)
                    p.op("tensor", "matmul", pR[0:64, :], ones1r[0:1, 0:64],
                         invDsb[:], start=True, stop=True, inc=pe)
                    p.wait("vector", pe)
                    p.op("vector", "tensor_copy", rinvsb[:], pR[0:64, :], inc=ve)
                    p.wait("vector", ve)
                    p.op("vector", "tensor_tensor",
                         aT[(hh % 2) * 64:(hh % 2) * 64 + 64,
                            (hh // 2) * 512:(hh // 2 + 1) * 512],
                         poW[0:64, :], rinvsb[:], ALU.mult, inc=ve)
                ve_attn = ve.n if not _skipattn else qkv_done_ve
                if _os.environ.get("D12SUB") == "attn":
                    continue
                pwo = [pq[4], pq[5]]
                _dwo = _os.environ.get("D12WO", "")
                for i4 in range(4):
                    i = tc * 4 + i4
                    if i >= 2:
                        p.wait("sync", ve)
                    p.op("sync", "dma_start", out=zres[i % 2][:],
                         in_=zin[i * 128:(i + 1) * 128, :], inc=ld, inck=16)
                    zld = ld.n
                    p.wait("tensor", ve, max(ve_attn, ve_attn_done))
                    if _dwo != "nomm":
                        for hp in range(4):
                            for half in range(2):
                                p.op("tensor", "matmul", pwo[half][:],
                                     aT[:, hp * 512 + i4 * 128:hp * 512 + (i4 + 1) * 128],
                                     wosb[:, hp * 1024 + half * 512:
                                          hp * 1024 + (half + 1) * 512],
                                     start=(hp == 0), stop=(hp == 3),
                                     inc=(pe if (hp == 3 and half == 1) else None))
                        p.wait("vector", pe)
                    p.wait("vector", ld, zld)
                    if i >= 2 and _dwo != "nostore":
                        p.wait("vector", st, st_z2[i - 2])
                    for half in range(2):
                        if _dwo == "nostt":
                            p.op("vector", "tensor_copy",
                                 z2t[i % 2][:, half * 512:(half + 1) * 512],
                                 zres[i % 2][:, half * 512:(half + 1) * 512],
                                 inc=(ve if half == 1 else None))
                        else:
                            p.op("vector", "scalar_tensor_tensor",
                                 z2t[i % 2][:, half * 512:(half + 1) * 512],
                                 zres[i % 2][:, half * 512:(half + 1) * 512], 1.0,
                                 pwo[half][:], ALU.mult, ALU.add,
                                 inc=(ve if half == 1 else None))
                    ve_attn_done = ve.n
                    p.wait("scalar", ve)
                    if _dwo != "nostore":
                        p.op("scalar", "dma_start",
                             out=d["z2"][i * 128:(i + 1) * 128, :], in_=z2t[i % 2][:],
                             inc=st, inck=16)
                        st_z2[i] = st.n
            d12_done = st.n

        if "d3a" not in _dph:
            return st.n
        with ExitStack() as dctx:
            dsb = lambda n_, s_, t_=F32: dctx.enter_context(nc.sbuf_tensor(f"{n_}_{l}", s_, t_))
            dps = lambda n_, s_, t_=F32: dctx.enter_context(nc.psum_tensor(f"{n_}_{l}", s_, t_))
            lgrow = dsb("D_lgrow", [1, 1024])
            lbrow = dsb("D_lbrow", [1, 1024])
            lgrep = dsb("D_lgrep", [128, 1024])
            lbrep = dsb("D_lbrep", [128, 1024])
            htile = [dsb(f"D_ht{i}", [128, 1024]) for i in range(2)]
            hnstage = [dsb(f"D_hn{i}", [128, 1152], BF16) for i in range(2)]
            ptf2 = dps("D_ptf2", [128, 512])
            p.wait("sync", st, d12_done)
            p.wait("vector", st, d12_done)
            p.op("sync", "dma_start", out=lgrow[:], in_=d["lng"][l:l + 1, :],
                 inc=ld, inck=16)
            p.op("sync", "dma_start", out=lbrow[:], in_=d["lnb"][l:l + 1, :],
                 inc=ld, inck=16)
            p.wait("tensor", ld)
            for half in range(2):
                emit_replicate_row(p, nc, lgrow[0:1, half * 512:(half + 1) * 512],
                                   lgrep[:, half * 512:(half + 1) * 512], ptf2,
                                   ones1r, s, 512)
                emit_replicate_row(p, nc, lbrow[0:1, half * 512:(half + 1) * 512],
                                   lbrep[:, half * 512:(half + 1) * 512], ptf2,
                                   ones1r, s, 512)
            p.op("vector", "memset", hnstage[0][:, 1024:1152], 0.0)
            p.op("vector", "memset", hnstage[1][:, 1024:1152], 0.0)
            p.op("vector", "memset", hnstage[0][:, 1024:1025], 1.0)
            p.op("vector", "memset", hnstage[1][:, 1024:1025], 1.0, inc=ve)
            p.wait("sync", st, d12_done)
            st_hn = {}
            ve_h = {}
            for i in range(NT):
                if i >= 2:
                    p.wait("sync", ve, ve_h[i - 2])
                p.op("sync", "dma_start", out=htile[i % 2][:, 0:512],
                     in_=d["z2"][i * 128:(i + 1) * 128, 0:512], inc=ld, inck=16)
                if i == 0:
                    p.op("sync", "dma_start", out=htile[0][1:128, 512:1024],
                         in_=d["z2"][0:127, 512:1024], inc=ld, inck=16)
                else:
                    p.op("sync", "dma_start", out=htile[i % 2][:, 512:1024],
                         in_=d["z2"][i * 128 - 1:(i + 1) * 128 - 1, 512:1024],
                         inc=ld, inck=16)
                p.wait("vector", ld)
                if i == 0:
                    p.op("vector", "memset", htile[0][0:1, 512:1024], 0.0)
                p.op("vector", "tensor_reduce", kit.s1[:], htile[i % 2][:], AX.X,
                     ALU.add, inc=ve)
                ve_h[i] = ve.n
                if i >= 2:
                    p.wait("vector", st, st_hn[i - 2])
                emit_ln_affine(p, kit, htile[i % 2][:], hnstage[i % 2][:, 0:1024],
                               se, ve, lgrep=lgrep[:], lbrep=lbrep[:])
                ve_h[i] = ve.n
                p.wait("scalar", ve)
                p.op("scalar", "dma_start", out=d["hnbf"][i * 128:(i + 1) * 128, :],
                     in_=hnstage[i % 2][:], inc=st, inck=16)
                st_hn[i] = st.n
            d3a_done = st.n

        if "d3b" not in _dph:
            return st.n
        with ExitStack() as dctx:
            dsb = lambda n_, s_, t_=F32: dctx.enter_context(nc.sbuf_tensor(f"{n_}_{l}", s_, t_))
            dps = lambda n_, s_, t_=F32: dctx.enter_context(nc.psum_tensor(f"{n_}_{l}", s_, t_))
            w1sb = dsb("D_w1sb", [128, 9 * 5632], BF16)
            hnT = [[dsb(f"D_hnT{g}_{c}", [128, 512], BF16) for c in range(9)]
                   for g in range(2)]
            gsc = dsb("D_gsc", [128, 256])
            h2st = [dsb(f"D_h2st{i}", [128, 2944], BF16) for i in range(2)]
            psW = [dps(f"D_psW{k}", [128, 512]) for k in range(2)]
            p.wait("gpsimd", st, d3a_done)
            p.wait("vector", st, d3a_done)
            for c in range(9):
                p.op("gpsimd", "dma_start", out=w1sb[:, c * 5632:(c + 1) * 5632],
                     in_=d["w1e"][l, c * 128:(c + 1) * 128, :], inc=gd, inck=16)
            w1_ready = gd.n
            p.op("vector", "memset", h2st[0][:, 2816:2944], 0.0)
            p.op("vector", "memset", h2st[1][:, 2816:2944], 0.0)
            p.op("vector", "memset", h2st[0][:, 2816:2817], 1.0)
            p.op("vector", "memset", h2st[1][:, 2816:2817], 1.0, inc=ve)
            h2const_ve = ve.n
            p.wait("sync", st, d3a_done)
            pe_g = {}
            st_h2 = {}
            ve_h2 = {}
            se_h2 = {}
            ngrp = 0
            for tc in range(TC):
                g = tc % 2
                if tc >= 2:
                    p.wait("sync", pe, pe_g[(tc - 2, 3, 10)])
                for c in range(9):
                    p.op("sync", "dma_start", out=hnT[g][c][:],
                         in_=d["hnbf"][tc * 512:(tc + 1) * 512,
                                       c * 128:(c + 1) * 128],
                         transpose=True, inc=ld, inck=16)
                p.wait("tensor", ld)
                if tc == 0:
                    p.wait("tensor", gd, w1_ready)
                    p.wait("tensor", ve, h2const_ve)
                for i4 in range(4):
                    i = tc * 4 + i4
                    for grp in range(11):
                        kk = ngrp % 2
                        if ngrp >= 2:
                            p.wait("tensor", se, se_h2.get(ngrp - 2, 0))
                            p.wait("tensor", ve, ve_h2.get(ngrp - 2, 0))
                        for c in range(9):
                            p.op("tensor", "matmul", psW[kk][:],
                                 hnT[g][c][:, i4 * 128:(i4 + 1) * 128],
                                 w1sb[:, c * 5632 + grp * 512:
                                      c * 5632 + (grp + 1) * 512],
                                 start=(c == 0), stop=(c == 8),
                                 inc=(pe if c == 8 else None))
                        pe_g[(tc, i4, grp)] = pe.n
                        p.wait("scalar", pe)
                        p.wait("scalar", ve, ve_h2.get(ngrp - 1, 0))
                        pg = psW[kk][:].rearrange("p (a b) -> p a b", b=256)
                        p.op("scalar", "activation",
                             gsc[:].rearrange("p (a b) -> p a b", a=2),
                             pg[:, :, 128:256], AF.Gelu)
                        p.op("scalar", "activation", dtiny[:], gsc[:, 255:256],
                             AF.Copy, bias=0.0, scale=1.0, inc=se)
                        se_h2[ngrp] = se.n
                        p.wait("vector", se)
                        if i >= 2 and grp == 0:
                            p.wait("vector", st, st_h2[i - 2])
                        p.op("vector", "tensor_tensor",
                             h2st[i % 2][:, grp * 256:(grp + 1) * 256]
                             .rearrange("p (a b) -> p a b", a=2),
                             pg[:, :, 0:128],
                             gsc[:].rearrange("p (a b) -> p a b", a=2), ALU.mult,
                             inc=ve)
                        ve_h2[ngrp] = ve.n
                        ngrp += 1
                    p.wait("scalar", ve)
                    p.op("scalar", "dma_start",
                         out=d["h2bf"][i * 128:(i + 1) * 128, :], in_=h2st[i % 2][:],
                         inc=st, inck=16)
                    st_h2[i] = st.n
                pe_g[(tc, 3, 10)] = pe.n
            d3b_done = st.n

        if "d3c" not in _dph:
            return st.n
        zout = d["z3a"] if l == 0 else d["z3b"]
        with ExitStack() as dctx:
            dsb = lambda n_, s_, t_=F32: dctx.enter_context(nc.sbuf_tensor(f"{n_}_{l}", s_, t_))
            dps = lambda n_, s_, t_=F32: dctx.enter_context(nc.psum_tensor(f"{n_}_{l}", s_, t_))
            w2sb = dsb("D_w2sb", [128, 23 * 1024], BF16)
            h2T = [[dsb(f"D_h2T{g}_{c}", [128, 512], BF16) for c in range(23)]
                   for g in range(2)]
            z2res = [dsb(f"D_z2res{i}", [128, 1024]) for i in range(2)]
            z3t = [dsb(f"D_z3t{i}", [128, 1024]) for i in range(2)]
            psW2 = [dps(f"D_psW2{k}", [128, 512]) for k in range(2)]
            p.wait("gpsimd", st, d3b_done)
            for c in range(23):
                p.op("gpsimd", "dma_start", out=w2sb[:, c * 1024:(c + 1) * 1024],
                     in_=d["w2e"][l, c * 128:(c + 1) * 128, :], inc=gd, inck=16)
            w2_ready = gd.n
            p.wait("sync", st, d3b_done)
            pe_t = {}
            st_z3 = {}
            ve_z3 = {}
            for tc in range(TC):
                g = tc % 2
                if tc >= 2:
                    p.wait("sync", pe, pe_t[(tc - 2, 3)])
                for c in range(23):
                    p.op("sync", "dma_start", out=h2T[g][c][:],
                         in_=d["h2bf"][tc * 512:(tc + 1) * 512,
                                       c * 128:(c + 1) * 128],
                         transpose=True, inc=ld, inck=16)
                p.wait("tensor", ld)
                if tc == 0:
                    p.wait("tensor", gd, w2_ready)
                for i4 in range(4):
                    i = tc * 4 + i4
                    if i >= 2:
                        p.wait("sync", ve, ve_z3.get(i - 2, 0))
                    p.op("sync", "dma_start", out=z2res[i % 2][:],
                         in_=d["z2"][i * 128:(i + 1) * 128, :], inc=ld, inck=16)
                    zld = ld.n
                    if i >= 1:
                        p.wait("tensor", ve, ve_z3.get(i - 1, 0))
                    for half in range(2):
                        for c in range(23):
                            p.op("tensor", "matmul", psW2[half][:],
                                 h2T[g][c][:, i4 * 128:(i4 + 1) * 128],
                                 w2sb[:, c * 1024 + half * 512:
                                      c * 1024 + (half + 1) * 512],
                                 start=(c == 0), stop=(c == 22),
                                 inc=(pe if (c == 22 and half == 1) else None))
                    pe_t[(tc, i4)] = pe.n
                    p.wait("vector", pe)
                    p.wait("vector", ld, zld)
                    if i >= 2:
                        p.wait("vector", st, st_z3[i - 2])
                    for half in range(2):
                        p.op("vector", "scalar_tensor_tensor",
                             z3t[i % 2][:, half * 512:(half + 1) * 512],
                             z2res[i % 2][:, half * 512:(half + 1) * 512], 1.0,
                             psW2[half][:], ALU.mult, ALU.add,
                             inc=(ve if half == 1 else None))
                    ve_z3[i] = ve.n
                    p.wait("scalar", ve)
                    p.op("scalar", "dma_start",
                         out=zout[i * 128:(i + 1) * 128, :], in_=z3t[i % 2][:],
                         inc=st, inck=16)
                    st_z3[i] = st.n
            d3c_done = st.n
        zin = zout

    if "fin" not in _dph:
        return st.n
    with ExitStack() as dctx:
        dsb = lambda n_, s_, t_=F32: dctx.enter_context(nc.sbuf_tensor(f"{n_}_{l}", s_, t_))
        xa = [dsb(f"D_fxa{i}", [128, 1024]) for i in range(2)]
        xb = [dsb(f"D_fxb{i}", [128, 1024]) for i in range(2)]
        xo = [dsb(f"D_fxo{i}", [128, 1024]) for i in range(2)]
        xsum = dsb("D_fxsum", [128, 1024])
        p.wait("sync", st, d3c_done)
        st_f = {}
        ve_f = {}
        for i in range(NT):
            if i >= 2:
                p.wait("sync", ve, ve_f[i - 2])
            p.op("sync", "dma_start", out=xa[i % 2][:],
                 in_=d["x3"][i * 128:(i + 1) * 128, :], inc=ld, inck=16)
            p.op("sync", "dma_start", out=xb[i % 2][:],
                 in_=zin[i * 128:(i + 1) * 128, :], inc=ld, inck=16)
            p.wait("vector", ld)
            p.op("vector", "scalar_tensor_tensor", xsum[:], xa[i % 2][:], 1.0,
                 xb[i % 2][:], ALU.mult, ALU.add, accum_out=kit.s1[:], inc=ve)
            ve_f[i] = ve.n
            if i >= 2:
                p.wait("vector", st, st_f[i - 2])
            emit_ln(p, kit, xsum[:], xo[i % 2][:], se, ve)
            ve_f[i] = ve.n
            p.wait("scalar", ve)
            p.op("scalar", "dma_start", out=d["out4"][i * 128:(i + 1) * 128, :],
                 in_=xo[i % 2][:], inc=st, inck=16)
            st_f[i] = st.n
    return st.n


def build_stageB(nc, ctx, p, d, s):
    sb = lambda n_, s_, t_=F32: ctx.enter_context(nc.sbuf_tensor(n_, s_, t_))
    ld, st, gd = s["ld"], s["st"], s["gd"]
    pe, ve, se, gp, ig = s["pe"], s["ve"], s["se"], s["gp"], s["ig"]
    NT, HC, TC = 64, 8, 16
    import os as _os

    identf = sb("B_identf", [128, 128])
    identb = sb("B_identb", [128, 128], BF16)
    pmf = sb("B_pmf", [128, 128])
    ones1r = sb("B_ones1r", [1, 128])
    onescol = sb("B_onescol", [128, 1], BF16)
    onescf = sb("B_onescf", [128, 1])
    glrep = sb("B_glrep", [128, 1024])
    ghrep = sb("B_ghrep", [128, 1024])
    qtokr = sb("B_qtokr", [128, 1024])
    ktokr = sb("B_ktokr", [128, 1024])
    sbq = sb("B_sbq", [128, 64])
    sbkv = sb("B_sbkv", [128, 64])
    idxf = sb("B_idxf", [128, 64])
    kit = LNKit(sb, "B")
    btiny = sb("B_tiny", [128, 1], BF16)

    p.op("gpsimd", "iota", pmf[:], pattern=[[-1, 128]], base=0, channel_multiplier=1,
         allow_small_or_imprecise_dtypes=True)
    p.op("gpsimd", "iota", idxf[:], pattern=[[128, 64]], base=0, channel_multiplier=1,
         allow_small_or_imprecise_dtypes=True)
    p.op("gpsimd", "memset", ones1r[:], 1.0)
    p.op("gpsimd", "memset", onescol[:], 1.0)
    p.op("gpsimd", "memset", onescf[:], 1.0, inc=gp)
    p.wait("vector", gp)
    p.op("vector", "tensor_scalar", identf[:], pmf[:], 0.0, None, ALU.is_equal, inc=ve)
    p.wait("vector", ve)
    p.op("vector", "tensor_copy", identb[:], identf[:], inc=ve)
    ident_ready = ve.n

    with ExitStack() as bctx:
        rps = bctx.enter_context(nc.psum_tensor("B_rps", [128, 512], F32))
        grow = bctx.enter_context(nc.sbuf_tensor("B_grow", [1, 1024], F32))
        for src_name, dst in (("gamma_light", glrep), ("gamma_heavy", ghrep),
                              ("q_tok", qtokr), ("kv_tok", ktokr)):
            p.op("sync", "dma_start", out=grow[:],
                 in_=d[src_name].rearrange("(o f) -> o f", o=1), inc=ld, inck=16)
            p.wait("tensor", ld)
            for half in range(2):
                emit_replicate_row(p, nc, grow[0:1, half * 512:(half + 1) * 512],
                                   dst[:, half * 512:(half + 1) * 512], rps,
                                   ones1r, s, 512)
            p.wait("sync", ve)
        rep_done = ve.n

    with ExitStack() as bctx:
        bsb = lambda n_, s_, t_=F32: bctx.enter_context(nc.sbuf_tensor(n_, s_, t_))
        xin = [bsb(f"B_xin{i}", [128, 1024]) for i in range(2)]
        xnb = [bsb(f"B_xnb{i}", [128, 1024], BF16) for i in range(2)]
        rms = bsb("B_rms", [128, 1])
        scr2 = bsb("B_scr2", [128, 1024])
        b0_st0 = st.n
        ve_t = {}
        for i in range(NT):
            if i >= 2:
                p.wait("sync", ve, ve_t[i - 2])
            p.op("sync", "dma_start", out=xin[i % 2][:],
                 in_=d["x1"][i * 128:(i + 1) * 128, :], inc=ld, inck=16)
            p.wait("scalar", ld)
            p.op("scalar", "activation", kit.scr[:], xin[i % 2][:], AF.Square,
                 accum_out=kit.s2[:])
            p.op("scalar", "activation", btiny[:], kit.s2[:], AF.Copy, bias=0.0,
                 scale=1.0, inc=se)
            p.wait("vector", se)
            if i == 0:
                p.wait("vector", ve, rep_done)
            p.op("vector", "scalar_tensor_tensor", scr2[:], xin[i % 2][:], 1.0,
                 qtokr[:], ALU.mult, ALU.mult, accum_out=sbq[:, i:i + 1])
            p.op("vector", "scalar_tensor_tensor", scr2[:], xin[i % 2][:], 1.0,
                 ktokr[:], ALU.mult, ALU.mult, accum_out=sbkv[:, i:i + 1], inc=ve)
            p.wait("scalar", ve)
            p.op("scalar", "activation", kit.sd[:], kit.s2[:], AF.Sqrt)
            p.op("scalar", "activation", btiny[:], kit.sd[:], AF.Copy, bias=0.0,
                 scale=1.0, inc=se)
            p.wait("vector", se)
            p.op("vector", "tensor_scalar", kit.sd[:], kit.sd[:], 1e-12, None, ALU.max,
                 inc=ve)
            p.wait("vector", ve)
            p.op("vector", "reciprocal", rms[:], kit.sd[:], inc=ve)
            p.wait("vector", ve)
            p.op("vector", "tensor_scalar", scr2[:], xin[i % 2][:], rms[:], 32.0,
                 ALU.mult, ALU.mult, inc=ve)
            p.wait("vector", ve)
            if i >= 2:
                p.wait("vector", st, b0_st0 + (i - 1) * 16)
            p.op("vector", "tensor_tensor", xnb[i % 2][:], scr2[:], glrep[:], ALU.mult,
                 inc=ve)
            ve_t[i] = ve.n
            p.wait("scalar", ve)
            p.op("scalar", "dma_start", out=d["xnbf"][i * 128:(i + 1) * 128, :],
                 in_=xnb[i % 2][:], inc=st, inck=16)
        xnbf_done = st.n
        s_ready = ve.n

    if "b1" in _os.environ.get("BPH", "b1,b2"):
        with ExitStack() as bctx:
            bsb = lambda n_, s_, t_=F32: bctx.enter_context(nc.sbuf_tensor(n_, s_, t_))
            bps = lambda n_, s_, t_=F32: bctx.enter_context(nc.psum_tensor(n_, s_, t_))
            wcat = bsb("B_wcat", [128, HC * 1536], BF16)
            wol = bsb("B_wol", [128, 4 * 1024], BF16)
            xts = [bsb(f"B_xts{g}", [128, HC * 512], BF16) for g in range(2)]
            strip = [bsb(f"B_strip{g}", [128, 12 * 512], BF16) for g in range(2)]
            vtok = bsb("B_vtok", [128, 64], BF16)
            esbB = bsb("B_esbB", [128, 128], BF16)
            invD = bsb("B_invD", [1, 128])
            rinv = bsb("B_rinv", [64, 128])
            aTB = bsb("B_aTB", [128, 512], BF16)
            xres = [bsb(f"B_xres{i}", [128, 1024]) for i in range(2)]
            btt = [bsb(f"B_btt{i}", [128, 1024]) for i in range(2)]
            psQ = [bps(f"B_psQ{k}", [128, 512]) for k in range(2)]
            poB = bps("B_poB", [128, 512])
            pDb = bps("B_pDb", [128, 512])
            pRb = bps("B_pRb", [128, 512])
            pwo = [bps(f"B_pwo{k}", [128, 512]) for k in range(2)]
            ptbB = bps("B_ptbB", [128, 128], BF16)

            p.wait("gpsimd", st, xnbf_done)
            for hc in range(HC):
                p.op("gpsimd", "dma_start", out=wcat[:, hc * 1536:hc * 1536 + 512],
                     in_=d["wq_light"][hc * 128:(hc + 1) * 128, :], inc=gd, inck=16)
                p.op("gpsimd", "dma_start", out=wcat[:, hc * 1536 + 512:(hc + 1) * 1536],
                     in_=d["wkv_light"][hc * 128:(hc + 1) * 128, :], inc=gd, inck=16)
            for b4 in range(4):
                p.op("gpsimd", "dma_start", out=wol[:, b4 * 1024:(b4 + 1) * 1024],
                     in_=d["wo_light"][b4 * 128:(b4 + 1) * 128, :], inc=gd, inck=16)
            w_ready = gd.n
            p.wait("sync", st, xnbf_done)
            pe_tc = {}
            ve_strip = {}
            ve_attn_end = 0
            st_bt = {}
            for tc in range(TC):
                g = tc % 2
                if tc >= 2:
                    p.wait("sync", pe, pe_tc[tc - 2])
                for hc in range(HC):
                    p.op("sync", "dma_start",
                         out=xts[g][:, hc * 512:(hc + 1) * 512],
                         in_=d["xnbf"][tc * 512:(tc + 1) * 512,
                                       hc * 128:(hc + 1) * 128],
                         transpose=True, inc=ld, inck=16)
                p.wait("tensor", ld)
                if tc == 0:
                    p.wait("tensor", gd, w_ready)
                    p.wait("tensor", ve, ident_ready)
                if tc >= 2:
                    p.wait("tensor", ve, ve_strip[tc - 2])
                str_ve = {}
                for m in range(12):
                    kk = m % 2
                    if m >= 2:
                        p.wait("tensor", ve, str_ve[m - 2])
                    if m < 2 and ve_attn_end:
                        p.wait("tensor", ve, ve_attn_end)
                    for hc in range(HC):
                        p.op("tensor", "matmul", psQ[kk][:],
                             wcat[:, hc * 1536 + m * 128: hc * 1536 + (m + 1) * 128],
                             xts[g][:, hc * 512:(hc + 1) * 512],
                             start=(hc == 0), stop=(hc == HC - 1),
                             inc=(pe if hc == HC - 1 else None))
                    p.wait("vector", pe)
                    p.op("vector", "tensor_copy", strip[g][:, m * 512:(m + 1) * 512],
                         psQ[kk][:], inc=ve)
                    str_ve[m] = ve.n
                ve_strip[tc] = ve.n
                for w4 in range(4):
                    i = tc * 4 + w4
                    if i >= 2:
                        p.wait("sync", ve)
                    p.op("sync", "dma_start", out=xres[i % 2][:],
                         in_=d["x1"][i * 128:(i + 1) * 128, :], inc=ld, inck=16)
                    xres_ld = ld.n
                    for hh in range(8):
                        ch_q = hh // 2
                        ch_k = 4 + hh // 2
                        ch_v = 8 + hh // 2
                        pb = (hh % 2) * 64
                        p.wait("tensor", ve, max(ve_strip[tc], ve_attn_end))
                        p.wait("tensor", ve)
                        p.op("tensor", "transpose", ptbB[0:128, 0:64],
                             strip[g][pb:pb + 64,
                                      ch_v * 512 + w4 * 128:ch_v * 512 + (w4 + 1) * 128],
                             identb[pb:pb + 64, pb:pb + 64], inc=pe)
                        p.wait("vector", pe)
                        p.op("vector", "tensor_copy", vtok[:], ptbB[0:128, 0:64],
                             inc=ve)
                        p.wait("tensor", ve)
                        p.op("tensor", "matmul", psQ[0][:, 0:128],
                             strip[g][pb:pb + 64,
                                      ch_k * 512 + w4 * 128:ch_k * 512 + (w4 + 1) * 128],
                             strip[g][pb:pb + 64,
                                      ch_q * 512 + w4 * 128:ch_q * 512 + (w4 + 1) * 128],
                             start=True, stop=True, inc=pe)
                        p.wait("scalar", pe)
                        p.op("scalar", "activation", esbB[:], psQ[0][:, 0:128], AF.Exp,
                             scale=0.125)
                        p.op("scalar", "activation", btiny[:], esbB[:, 127:128],
                             AF.Copy, bias=0.0, scale=1.0, inc=se)
                        p.wait("tensor", se)
                        p.op("tensor", "matmul", poB[0:64, 0:128], vtok[:, 0:64],
                             esbB[:], start=True, stop=True)
                        p.op("tensor", "matmul", pDb[0:1, 0:128], onescol[:],
                             esbB[:], start=True, stop=True, inc=pe)
                        p.wait("vector", pe)
                        p.op("vector", "reciprocal", invD[:], pDb[0:1, 0:128],
                             inc=ve)
                        p.wait("tensor", ve)
                        p.op("tensor", "matmul", pRb[0:64, 0:128], ones1r[0:1, 0:64],
                             invD[:], start=True, stop=True, inc=pe)
                        p.wait("vector", pe)
                        p.op("vector", "tensor_copy", rinv[:], pRb[0:64, 0:128],
                             inc=ve)
                        p.wait("vector", ve)
                        p.op("vector", "tensor_tensor",
                             aTB[pb:pb + 64, (hh // 2) * 128:(hh // 2 + 1) * 128],
                             poB[0:64, 0:128], rinv[:], ALU.mult, inc=ve)
                    attn_ve = ve.n
                    p.wait("tensor", ve, attn_ve)
                    for b4 in range(4):
                        for half in range(2):
                            p.op("tensor", "matmul", pwo[half][:],
                                 aTB[:, b4 * 128:(b4 + 1) * 128],
                                 wol[:, b4 * 1024 + half * 512:
                                     b4 * 1024 + (half + 1) * 512],
                                 start=(b4 == 0), stop=(b4 == 3),
                                 inc=(pe if (b4 == 3 and half == 1) else None))
                    pe_tc[tc] = pe.n
                    p.wait("vector", pe)
                    p.wait("vector", ld, xres_ld)
                    if i >= 2:
                        p.wait("vector", st, st_bt[i - 2])
                    for half in range(2):
                        p.op("vector", "scalar_tensor_tensor",
                             btt[i % 2][:, half * 512:(half + 1) * 512],
                             xres[i % 2][:, half * 512:(half + 1) * 512], 1.0,
                             pwo[half][:], ALU.mult, ALU.add,
                             inc=(ve if half == 1 else None))
                    ve_attn_end = ve.n
                    p.wait("scalar", ve)
                    p.op("scalar", "dma_start",
                         out=d["btmp"][i * 128:(i + 1) * 128, :], in_=btt[i % 2][:],
                         inc=st, inck=16)
                    st_bt[i] = st.n
            b1_done = st.n
    else:
        b1_done = st.n

    if "b2" in _os.environ.get("BPH", "b1,b2"):
        with ExitStack() as bctx:
            bsb = lambda n_, s_, t_=F32: bctx.enter_context(nc.sbuf_tensor(n_, s_, t_))
            bps = lambda n_, s_, t_=F32: bctx.enter_context(nc.psum_tensor(n_, s_, t_))
            KSEL = {"q": 1024, "kv": 2048}
            LOGK = {"q": float(np.log(np.float32(1152.0))),
                    "kv": float(np.log(np.float32(2304.0)))}
            svec = {"q": sbq, "kv": sbkv}
            bvec = {r: bsb(f"B_bv_{r}", [128, 64]) for r in ("q", "kv")}
            tsb = {r: bsb(f"B_tsb_{r}", [128, 64]) for r in ("q", "kv")}
            rmaxc = {r: bsb(f"B_rmax_{r}", [128, 1]) for r in ("q", "kv")}
            scal1 = {r: bsb(f"B_sc1_{r}", [1, 1]) for r in ("q", "kv")}
            scal2 = {r: bsb(f"B_sc2_{r}", [1, 1]) for r in ("q", "kv")}
            mrep = {r: bsb(f"B_mrep_{r}", [128, 1]) for r in ("q", "kv")}
            scores = {r: bsb(f"B_scores_{r}", [128, 64]) for r in ("q", "kv")}
            etile = {r: bsb(f"B_etile_{r}", [128, 64]) for r in ("q", "kv")}
            dbgst = bsb("B_dbgst", [1, 64])
            lo = {r: bsb(f"B_lo_{r}", [1, 1]) for r in ("q", "kv")}
            hi = {r: bsb(f"B_hi_{r}", [1, 1]) for r in ("q", "kv")}
            mid = {r: bsb(f"B_mid_{r}", [1, 1]) for r in ("q", "kv")}
            cnt = {r: bsb(f"B_cnt_{r}", [1, 1]) for r in ("q", "kv")}
            cond = {r: bsb(f"B_cond_{r}", [1, 1]) for r in ("q", "kv")}
            midrep = {r: bsb(f"B_midrep_{r}", [128, 1]) for r in ("q", "kv")}
            condi = {r: bsb(f"B_condi_{r}", [1, 1], I32) for r in ("q", "kv")}
            selmi = {r: bsb(f"B_selmi_{r}", [128, 64], I32) for r in ("q", "kv")}
            msk = {r: bsb(f"B_msk_{r}", [128, 64]) for r in ("q", "kv")}
            selm = {r: bsb(f"B_selm_{r}", [128, 64]) for r in ("q", "kv")}
            eqm = {r: bsb(f"B_eqm_{r}", [128, 64]) for r in ("q", "kv")}
            rank2 = {r: bsb(f"B_rank2_{r}", [128, 64]) for r in ("q", "kv")}
            idxi32 = {r: bsb(f"B_idxi_{r}", [128, 64], I32) for r in ("q", "kv")}
            toki2 = bsb("B_toki2", [128, 128], I32)
            mT = bsb("B_mT", [64, 128])
            prefA = bsb("B_prefA", [64, 128])
            prefB = bsb("B_prefB", [64, 128])
            rowtot = bsb("B_rowtot", [64, 1])
            rowA = bsb("B_rowA", [1, 64])
            rowB = bsb("B_rowB", [1, 64])
            roffc = bsb("B_roffc", [64, 1])
            rankT = bsb("B_rankT", [64, 128])

            pctxB = ExitStack()
            ptB = pctxB.enter_context(nc.psum_tensor("B_ptB", [128, 512], F32))

            p.wait("tensor", st, b1_done)
            p.wait("vector", st, b1_done)
            p.wait("scalar", st, b1_done)
            p.wait("gpsimd", st, b1_done)
            p.wait("sync", st, b1_done)
            p.wait("vector", ve, s_ready)
            for r in ("q", "kv"):
                p.op("vector", "tensor_scalar", bvec[r][:], svec[r][:], -1.0, None,
                     ALU.mult, inc=ve)
            for it in range(int(_os.environ.get("B2IT", 50))):
                for r in ("q", "kv"):
                    p.wait("vector", ve)
                    p.op("vector", "tensor_tensor", tsb[r][:], svec[r][:], bvec[r][:],
                         ALU.add, inc=ve)
                    p.wait("vector", ve)
                    p.op("vector", "tensor_reduce", rmaxc[r][:], tsb[r][:], AX.X,
                         ALU.max, inc=ve)
                for r in ("q", "kv"):
                    p.wait("tensor", ve)
                    p.op("tensor", "transpose", ptB[0:1, 0:128], rmaxc[r][:],
                         identf[:], inc=pe)
                    p.wait("vector", pe)
                    p.op("vector", "tensor_reduce", scal1[r][:], ptB[0:1, 0:128],
                         AX.X, ALU.max, inc=ve)
                for r in ("q", "kv"):
                    p.wait("tensor", ve)
                    p.op("tensor", "matmul", ptB[:, 128:129], ones1r[:], scal1[r][:],
                         start=True, stop=True, inc=pe)
                    p.wait("vector", pe)
                    p.op("vector", "tensor_scalar", mrep[r][:], ptB[:, 128:129], -1.0,
                         None, ALU.mult, inc=ve)
                for r in ("q", "kv"):
                    p.wait("scalar", ve)
                    p.op("scalar", "activation", etile[r][:], tsb[r][:], AF.Exp,
                         bias=mrep[r][:], scale=1.0)
                    p.op("scalar", "activation", btiny[:], etile[r][:, 63:64],
                         AF.Exp, scale=1e-6, inc=se)
                for r in ("q", "kv"):
                    p.wait("tensor", se)
                    p.wait("tensor", ve)
                    p.op("tensor", "matmul", ptB[0:1, 192:256], onescf[:],
                         etile[r][:], start=True, stop=True, inc=pe)
                    p.wait("vector", pe)
                    p.op("vector", "tensor_reduce", scal2[r][:], ptB[0:1, 192:256],
                         AX.X, ALU.add, inc=ve)
                for r in ("q", "kv"):
                    if r == "q" and it == 1 and "dbgB" in d:
                        p.wait("vector", ve)
                        p.op("vector", "tensor_copy", dbgst[0:1, 0:1], scal2[r][:])
                        p.op("vector", "tensor_copy", dbgst[0:1, 1:2], scal1[r][:],
                             inc=ve)
                        p.wait("sync", ve)
                        p.op("sync", "dma_start", out=d["dbgB"][0:1, 32:96],
                             in_=dbgst[:], inc=ld, inck=16)
                        p.wait("scalar", ld)
                        p.wait("vector", ld)
                    p.wait("scalar", ve)
                    p.op("scalar", "activation", scal2[r][:], scal2[r][:], AF.Ln)
                    p.op("scalar", "activation", btiny[0:1, 0:1], scal2[r][:],
                         AF.Exp, scale=1e-6, inc=se)
                    p.wait("vector", se)
                    p.op("vector", "tensor_tensor", scal2[r][:], scal2[r][:],
                         scal1[r][:], ALU.add, inc=ve)
                    p.wait("vector", ve)
                    p.op("vector", "tensor_scalar", scal2[r][:], scal2[r][:], -1.0,
                         LOGK[r], ALU.mult, ALU.add, inc=ve)
                for r in ("q", "kv"):
                    p.wait("tensor", ve)
                    p.op("tensor", "matmul", ptB[:, 128:129], ones1r[:], scal2[r][:],
                         start=True, stop=True, inc=pe)
                    p.wait("vector", pe)
                    p.op("vector", "tensor_copy", mrep[r][:], ptB[:, 128:129], inc=ve)
                for r in ("q", "kv"):
                    p.wait("vector", ve)
                    p.op("vector", "tensor_scalar", tsb[r][:], svec[r][:],
                         mrep[r][:], None, ALU.add, inc=ve)
                    p.wait("vector", ve)
                    p.op("vector", "tensor_scalar", bvec[r][:], tsb[r][:], 0.0, -1.0,
                         ALU.max, ALU.mult, inc=ve)
            for r in ("q", "kv"):
                p.wait("vector", ve)
                p.op("vector", "tensor_scalar", tsb[r][:], svec[r][:], mrep[r][:],
                     None, ALU.add, inc=ve)
                p.wait("vector", ve)
                p.op("vector", "tensor_tensor", tsb[r][:], tsb[r][:], bvec[r][:],
                     ALU.add, inc=ve)
                p.wait("scalar", ve)
                p.op("scalar", "activation", scores[r][:], tsb[r][:], AF.Exp)
                p.op("scalar", "activation", btiny[:], scores[r][:, 63:64], AF.Copy,
                     bias=0.0, scale=1.0, inc=se)
            if "dbgB" in d:
                p.wait("sync", se)
                p.wait("sync", ve)
                p.op("sync", "dma_start", out=d["dbgB"][:, 0:64],
                     in_=scores["q"][:], inc=ld, inck=16)
                p.op("sync", "dma_start", out=d["dbgB"][:, 64:128],
                     in_=sbq[:], inc=ld, inck=16)
            p.wait("vector", se)
            for r in ("q", "kv"):
                p.op("vector", "memset", lo[r][:], -1.0)
                p.op("vector", "memset", hi[r][:], 2.0, inc=ve)

            def count_gt(r, thr_ap, out_cnt):
                p.wait("tensor", ve)
                p.op("tensor", "matmul", ptB[:, 128:129], ones1r[:], thr_ap,
                     start=True, stop=True, inc=pe)
                p.wait("vector", pe)
                p.op("vector", "tensor_copy", midrep[r][:], ptB[:, 128:129], inc=ve)
                p.wait("vector", ve)
                p.op("vector", "tensor_scalar", msk[r][:], scores[r][:],
                     midrep[r][:], None, ALU.is_gt, inc=ve)
                p.wait("tensor", ve)
                p.op("tensor", "matmul", ptB[0:1, 192:256], onescf[:], msk[r][:],
                     start=True, stop=True, inc=pe)
                p.wait("vector", pe)
                p.op("vector", "tensor_reduce", out_cnt, ptB[0:1, 192:256], AX.X,
                     ALU.add, inc=ve)

            for it in range(45):
                for r in ("q", "kv"):
                    p.wait("vector", ve)
                    p.op("vector", "tensor_tensor", mid[r][:], lo[r][:], hi[r][:],
                         ALU.add, inc=ve)
                    p.wait("vector", ve)
                    p.op("vector", "tensor_scalar", mid[r][:], mid[r][:], 0.5, None,
                         ALU.mult, inc=ve)
                for r in ("q", "kv"):
                    count_gt(r, mid[r][:], cnt[r][:])
                for r in ("q", "kv"):
                    p.wait("vector", ve)
                    p.op("vector", "tensor_scalar", cond[r][:], cnt[r][:],
                         float(KSEL[r]), None, ALU.is_ge, inc=ve)
                    p.wait("vector", ve)
                    p.op("vector", "tensor_copy", condi[r][:], cond[r][:], inc=ve)
                    p.wait("vector", ve)
                    p.op("vector", "select", lo[r][:], condi[r][:], mid[r][:],
                         lo[r][:])
                    p.op("vector", "select", hi[r][:], condi[r][:], hi[r][:],
                         mid[r][:], inc=ve)
            for r in ("q", "kv"):
                count_gt(r, hi[r][:], cnt[r][:])
                p.wait("vector", ve)
                p.op("vector", "tensor_copy", selm[r][:], msk[r][:], inc=ve)
                count_gt(r, lo[r][:], scal1[r][:])
                p.wait("vector", ve)
                p.op("vector", "tensor_tensor", eqm[r][:], msk[r][:], selm[r][:],
                     ALU.subtract, inc=ve)
            for r in ("q", "kv"):
                p.wait("vector", ve)
                p.op("vector", "tensor_scalar", scal2[r][:], cnt[r][:], -1.0,
                     float(KSEL[r]), ALU.mult, ALU.add, inc=ve)
                p.op("vector", "memset", lo[r][:], 0.0)
                p.op("vector", "memset", hi[r][:], 8192.0, inc=ve)
            for it in range(14):
                for r in ("q", "kv"):
                    p.wait("vector", ve)
                    p.op("vector", "tensor_tensor", mid[r][:], lo[r][:], hi[r][:],
                         ALU.add, inc=ve)
                    p.wait("vector", ve)
                    p.op("vector", "tensor_scalar", mid[r][:], mid[r][:], 0.5, None,
                         ALU.mult, inc=ve)
                for r in ("q", "kv"):
                    p.wait("tensor", ve)
                    p.op("tensor", "matmul", ptB[:, 128:129], ones1r[:], mid[r][:],
                         start=True, stop=True, inc=pe)
                    p.wait("vector", pe)
                    p.op("vector", "tensor_copy", midrep[r][:], ptB[:, 128:129],
                         inc=ve)
                    p.wait("vector", ve)
                    p.op("vector", "tensor_scalar", msk[r][:], idxf[:], midrep[r][:],
                         None, ALU.is_lt, inc=ve)
                    p.wait("vector", ve)
                    p.op("vector", "tensor_tensor", msk[r][:], msk[r][:], eqm[r][:],
                         ALU.mult, inc=ve)
                    p.wait("tensor", ve)
                    p.op("tensor", "matmul", ptB[0:1, 192:256], onescf[:], msk[r][:],
                         start=True, stop=True, inc=pe)
                    p.wait("vector", pe)
                    p.op("vector", "tensor_reduce", cnt[r][:], ptB[0:1, 192:256],
                         AX.X, ALU.add, inc=ve)
                for r in ("q", "kv"):
                    p.wait("vector", ve)
                    p.op("vector", "tensor_scalar", cond[r][:], cnt[r][:],
                         scal2[r][0:1, 0:1], None, ALU.is_ge, inc=ve)
                    p.wait("vector", ve)
                    p.op("vector", "tensor_copy", condi[r][:], cond[r][:], inc=ve)
                    p.wait("vector", ve)
                    p.op("vector", "select", hi[r][:], condi[r][:], mid[r][:],
                         hi[r][:])
                    p.op("vector", "select", lo[r][:], condi[r][:], lo[r][:],
                         mid[r][:], inc=ve)
            for r in ("q", "kv"):
                p.wait("tensor", ve)
                p.op("tensor", "matmul", ptB[:, 128:129], ones1r[:], hi[r][:],
                     start=True, stop=True, inc=pe)
                p.wait("vector", pe)
                p.op("vector", "tensor_copy", midrep[r][:], ptB[:, 128:129], inc=ve)
                p.wait("vector", ve)
                p.op("vector", "tensor_scalar", msk[r][:], idxf[:], midrep[r][:],
                     None, ALU.is_lt, inc=ve)
                p.wait("vector", ve)
                p.op("vector", "tensor_tensor", msk[r][:], msk[r][:], eqm[r][:],
                     ALU.mult, inc=ve)
                p.wait("vector", ve)
                p.op("vector", "tensor_tensor", selm[r][:], selm[r][:], msk[r][:],
                     ALU.add, inc=ve)
            sel_ve = ve.n

            p.op("gpsimd", "iota", toki2[:].rearrange("p (f two) -> p f two", two=2),
                 pattern=[[128, 64], [0, 2]], base=0, channel_multiplier=1, inc=gp)
            for r in ("q", "kv"):
                p.wait("tensor", ve, sel_ve)
                p.wait("tensor", ve)
                p.op("tensor", "transpose", ptB[0:64, 0:128], selm[r][:], identf[:],
                     inc=pe)
                p.wait("vector", pe)
                p.op("vector", "tensor_copy", mT[:], ptB[0:64, 0:128], inc=ve)
                src, dst = mT, prefA
                for k in range(7):
                    sh = 1 << k
                    p.wait("vector", ve)
                    p.op("vector", "tensor_copy", dst[:, 0:sh], src[:, 0:sh])
                    p.op("vector", "tensor_tensor", dst[:, sh:128], src[:, sh:128],
                         src[:, 0:128 - sh], ALU.add, inc=ve)
                    src, dst = dst, (prefB if dst is prefA else prefA)
                pref = src
                p.wait("vector", ve)
                p.op("vector", "tensor_copy", rowtot[:], pref[:, 127:128], inc=ve)
                p.wait("tensor", ve)
                p.op("tensor", "transpose", ptB[0:1, 0:64], rowtot[:],
                     identf[0:64, 0:64], inc=pe)
                p.wait("vector", pe)
                p.op("vector", "tensor_copy", rowA[:], ptB[0:1, 0:64], inc=ve)
                srcr, dstr = rowA, rowB
                for k in range(6):
                    sh = 1 << k
                    p.wait("vector", ve)
                    p.op("vector", "tensor_copy", dstr[:, 0:sh], srcr[:, 0:sh])
                    p.op("vector", "tensor_tensor", dstr[:, sh:64], srcr[:, sh:64],
                         srcr[:, 0:64 - sh], ALU.add, inc=ve)
                    srcr, dstr = dstr, srcr
                p.wait("vector", ve)
                p.op("vector", "memset", dstr[:, 0:1], 0.0)
                p.op("vector", "tensor_copy", dstr[:, 1:64], srcr[:, 0:63], inc=ve)
                p.wait("tensor", ve)
                p.op("tensor", "transpose", ptB[0:64, 0:1], dstr[:],
                     identf[0:1, 0:1], inc=pe)
                p.wait("vector", pe)
                p.op("vector", "tensor_copy", roffc[:], ptB[0:64, 0:1], inc=ve)
                p.wait("vector", ve)
                p.op("vector", "tensor_tensor", rankT[:], pref[:], mT[:],
                     ALU.subtract, inc=ve)
                p.wait("vector", ve)
                p.op("vector", "tensor_scalar", rankT[:], rankT[:], roffc[:], None,
                     ALU.add, inc=ve)
                p.wait("tensor", ve)
                p.op("tensor", "transpose", ptB[0:128, 0:64], rankT[:],
                     identf[0:64, 0:64], inc=pe)
                p.wait("vector", pe)
                p.op("vector", "tensor_copy", rank2[r][:], ptB[0:128, 0:64], inc=ve)
                p.wait("vector", ve)
                p.op("vector", "tensor_tensor", msk[r][:], idxf[:], rank2[r][:],
                     ALU.subtract, inc=ve)
                p.wait("vector", ve)
                p.op("vector", "tensor_scalar", msk[r][:], msk[r][:],
                     float(KSEL[r]), None, ALU.add, inc=ve)
                p.wait("vector", ve)
                p.op("vector", "tensor_copy", selmi[r][:], selm[r][:], inc=ve)
                p.wait("vector", ve)
                p.op("vector", "select", msk[r][:], selmi[r][:], rank2[r][:],
                     msk[r][:], inc=ve)
                p.wait("gpsimd", ve)
                p.op("gpsimd", "tensor_copy", idxi32[r][:], msk[r][:], inc=gp)
            breg = {}
            def _mkreg(e, _k=None):
                pass
            for r in ("q", "kv"):
                breg[r] = nc.gpsimd.alloc_register(name=f"bchk_{r}")
                def loadreg(e, _reg=breg[r], _v=KSEL[r] - 1):
                    e.reg_mov(_reg, _v)
                p.emit("gpsimd", loadreg)
            for r, dname in (("q", "qidxd"), ("kv", "kidxd")):
                for f in range(64):
                    p.op("gpsimd", "indirect_dma_start",
                         out=d[dname],
                         out_offset=IOA(idxi32[r][:, f:f + 1], 0),
                         in_=toki2[:, 2 * f:2 * f + 2],
                         in_offset=None, bounds_check=breg[r],
                         oob_is_err=False, inc=ig, inck=16)
            scatter_ig = ig.n
            pctxB.close()

            idxpool = bsb("B_idxpool", [128, 24], I32)
            idxg = bsb("B_idxg", [128, 24], I32)

            xqT = bsb("B_xqT", [128, HC * 1024], BF16)
            ctT = bsb("B_ctT", [128, HC * 2048], BF16)
            rnt = [bsb(f"B_rnt{i}", [128, 1024], BF16) for i in range(2)]
            rmsB = bsb("B_rmsB", [128, 1])
            scrB = bsb("B_scrB", [128, 1024])
            ptH = [bps(f"B_ptH{k}", [128, 128], BF16) for k in range(2)]
            with ExitStack() as gctx:
                xg = gctx.enter_context(nc.sbuf_tensor("B_xg", [128, 8 * 1024], F32))
                cg = gctx.enter_context(nc.sbuf_tensor("B_cg", [128, 16 * 1024], F32))
                p.wait("gpsimd", ig, scatter_ig)
                for bi in range(8):
                    p.op("gpsimd", "dma_start", out=idxpool[:, bi:bi + 1],
                         in_=d["qidxd"][bi * 128:(bi + 1) * 128, 0:1],
                         allow_slow_non_contiguous=True, inc=gd, inck=16)
                for bi in range(16):
                    p.op("gpsimd", "dma_start", out=idxpool[:, 8 + bi:9 + bi],
                         in_=d["kidxd"][bi * 128:(bi + 1) * 128, 0:1],
                         allow_slow_non_contiguous=True, inc=gd, inck=16)
                p.wait("gpsimd", gd)
                p.op("gpsimd", "tensor_copy", idxg[:], idxpool[:], inc=gp)
                for bi in range(8):
                    p.op("gpsimd", "indirect_dma_start",
                         out=xg[:, bi * 1024:(bi + 1) * 1024], out_offset=None,
                         in_=d["x1"], in_offset=IOA(idxg[:, bi:bi + 1], 0),
                         inc=ig, inck=16)
                for bi in range(16):
                    p.op("gpsimd", "indirect_dma_start",
                         out=cg[:, bi * 1024:(bi + 1) * 1024], out_offset=None,
                         in_=d["x1"], in_offset=IOA(idxg[:, 8 + bi:9 + bi], 0),
                         inc=ig, inck=16)
                gather_ig = ig.n
                ntr = 0
                tr_ve = {}
                for t24 in range(24):
                    p.wait("scalar", ig, gather_ig)
                    src = (xg[:, t24 * 1024:(t24 + 1) * 1024] if t24 < 8 else
                           cg[:, (t24 - 8) * 1024:(t24 - 7) * 1024])
                    if t24 >= 2:
                        p.wait("scalar", ve, tr_ve[t24 - 2])
                    p.op("scalar", "activation", kit.scr[:], src, AF.Square,
                         accum_out=kit.s2[:])
                    p.op("scalar", "activation", btiny[:], kit.s2[:], AF.Copy,
                         bias=0.0, scale=1.0, inc=se)
                    p.wait("scalar", se)
                    p.op("scalar", "activation", kit.sd[:], kit.s2[:], AF.Sqrt)
                    p.op("scalar", "activation", btiny[:], kit.sd[:], AF.Copy,
                         bias=0.0, scale=1.0, inc=se)
                    p.wait("vector", se)
                    p.op("vector", "tensor_scalar", kit.sd[:], kit.sd[:], 1e-12,
                         None, ALU.max, inc=ve)
                    p.wait("vector", ve)
                    p.op("vector", "reciprocal", rmsB[:], kit.sd[:], inc=ve)
                    p.wait("vector", ve)
                    p.op("vector", "tensor_scalar", scrB[:], src, rmsB[:], 32.0,
                         ALU.mult, ALU.mult, inc=ve)
                    p.wait("vector", ve)
                    p.op("vector", "tensor_tensor", rnt[t24 % 2][:], scrB[:],
                         ghrep[:], ALU.mult, inc=ve)
                    p.wait("tensor", ve)
                    for hc in range(HC):
                        kk = ntr % 2
                        if ntr >= 2:
                            p.wait("tensor", ve)
                        p.op("tensor", "transpose", ptH[kk][:],
                             rnt[t24 % 2][:, hc * 128:(hc + 1) * 128], identb[:],
                             inc=pe)
                        p.wait("vector", pe)
                        if t24 < 8:
                            dst = xqT[:, hc * 1024 + t24 * 128:
                                      hc * 1024 + (t24 + 1) * 128]
                        else:
                            dst = ctT[:, hc * 2048 + (t24 - 8) * 128:
                                      hc * 2048 + (t24 - 7) * 128]
                        p.op("vector", "tensor_copy", dst, ptH[kk][:], inc=ve)
                        ntr += 1
                    tr_ve[t24] = ve.n
                rn_ve = ve.n
            wkvT = bsb("B_wkvT", [128, HC * 1024], BF16)
            wqh = bsb("B_wqh", [128, HC * 512], BF16)
            woh = bsb("B_woh", [128, 4 * 1024], BF16)
            wtmpB = bsb("B_wtmpB", [128, 1024], BF16)
            qhT = bsb("B_qhT", [128, 4 * 1024], BF16)
            kvT = bsb("B_kvT", [128, 8 * 2048], BF16)
            vtk = bsb("B_vtk", [128, 8 * 16 * 64], BF16)
            eTh = bsb("B_eTh", [128, 1024], BF16)
            invH = bsb("B_invH", [1, 1024])
            rinvH = bsb("B_rinvH", [64, 1024])
            ohT = bsb("B_ohT", [128, 4 * 1024], BF16)
            hvf = [bsb("B_hvf0", [128, 1024])] * 2
            p.wait("gpsimd", ve, rn_ve)
            for hc in range(HC):
                p.op("gpsimd", "dma_start", out=wqh[:, hc * 512:(hc + 1) * 512],
                     in_=d["wq_heavy"][hc * 128:(hc + 1) * 128, :], inc=gd, inck=16)
            for b4 in range(4):
                p.op("gpsimd", "dma_start", out=woh[:, b4 * 1024:(b4 + 1) * 1024],
                     in_=d["wo_heavy"][b4 * 128:(b4 + 1) * 128, :], inc=gd, inck=16)
            wh_ready = gd.n
            with ExitStack() as wctxB:
                ptwB = [wctxB.enter_context(
                    nc.psum_tensor(f"B_ptwB{k}", [128, 128], BF16)) for k in range(2)]
                emit_wT_load(p, nc, d["wkv_heavy"], wkvT, wtmpB, ptwB, identb, s,
                             ident_ready=ident_ready)
            wkvT_ve = ve.n
            pH = [bps(f"B_pH{k}", [128, 512]) for k in range(6)]

            p.wait("tensor", gd, wh_ready)
            p.wait("tensor", ve, rn_ve)
            for m in range(4):
                k2 = (m % 2) * 2
                if m >= 2:
                    p.wait("tensor", ve)
                for hc in range(HC):
                    for half in range(2):
                        p.op("tensor", "matmul", pH[k2 + half][:],
                             wqh[:, hc * 512 + m * 128: hc * 512 + (m + 1) * 128],
                             xqT[:, hc * 1024 + half * 512:
                                 hc * 1024 + (half + 1) * 512],
                             start=(hc == 0), stop=(hc == HC - 1),
                             inc=(pe if (hc == HC - 1 and half == 1) else None))
                p.wait("vector", pe)
                for half in range(2):
                    p.op("vector", "tensor_copy",
                         qhT[:, m * 1024 + half * 512: m * 1024 + (half + 1) * 512],
                         pH[k2 + half][:], inc=(ve if half == 1 else None))
            p.wait("tensor", ve, wkvT_ve)
            for m in range(8):
                p.wait("tensor", ve)
                for hc in range(HC):
                    for q4 in range(4):
                        p.op("tensor", "matmul", pH[q4][:],
                             wkvT[:, hc * 1024 + m * 128: hc * 1024 + (m + 1) * 128],
                             ctT[:, hc * 2048 + q4 * 512: hc * 2048 + (q4 + 1) * 512],
                             start=(hc == 0), stop=(hc == HC - 1),
                             inc=(pe if (hc == HC - 1 and q4 == 3) else None))
                p.wait("vector", pe)
                for q4 in range(4):
                    p.op("vector", "tensor_copy",
                         kvT[:, m * 2048 + q4 * 512: m * 2048 + (q4 + 1) * 512],
                         pH[q4][:], inc=(ve if q4 == 3 else None))
            kv_ve = ve.n
            ntr = 0
            for h in range(8):
                for nt in range(16):
                    kk = ntr % 2
                    p.wait("tensor", ve)
                    p.op("tensor", "transpose", ptH[kk][0:128, 0:64],
                         kvT[64:128, h * 2048 + nt * 128: h * 2048 + (nt + 1) * 128],
                         identb[64:128, 64:128], inc=pe)
                    p.wait("vector", pe)
                    p.op("vector", "tensor_copy",
                         vtk[:, (h * 16 + nt) * 64:(h * 16 + nt + 1) * 64],
                         ptH[kk][0:128, 0:64], inc=ve)
                    ntr += 1
            vtk_ve = ve.n
            qcur = bsb("B_qcur", [64, 1024], BF16)
            for h in range(8):
                pbq = (h % 2) * 64
                chq = h // 2
                p.wait("vector", ve)
                p.op("vector", "tensor_copy", qcur[:],
                     qhT[pbq:pbq + 64, chq * 1024:(chq + 1) * 1024], inc=ve)
                for nt in range(16):
                    p.wait("tensor", ve)
                    p.wait("tensor", se)
                    for half in range(2):
                        p.op("tensor", "matmul", pH[half][:],
                             kvT[0:64, h * 2048 + nt * 128: h * 2048 + (nt + 1) * 128],
                             qcur[0:64, half * 512:(half + 1) * 512],
                             start=True, stop=True,
                             inc=(pe if half == 1 else None))
                    p.wait("scalar", pe)
                    for half in range(2):
                        p.op("scalar", "activation",
                             eTh[:, half * 512:(half + 1) * 512], pH[half][:],
                             AF.Exp, scale=0.125)
                    p.op("scalar", "activation", btiny[:], eTh[:, 1023:1024],
                         AF.Copy, bias=0.0, scale=1.0, inc=se)
                    p.wait("tensor", se)
                    p.op("tensor", "matmul", pH[2][0:64, :],
                         vtk[:, (h * 16 + nt) * 64:(h * 16 + nt + 1) * 64],
                         eTh[:, 0:512], start=(nt == 0), stop=(nt == 15))
                    p.op("tensor", "matmul", pH[3][0:64, :],
                         vtk[:, (h * 16 + nt) * 64:(h * 16 + nt + 1) * 64],
                         eTh[:, 512:1024], start=(nt == 0), stop=(nt == 15))
                    p.op("tensor", "matmul", pH[4][0:1, :], onescol[:],
                         eTh[:, 0:512], start=(nt == 0), stop=(nt == 15))
                    p.op("tensor", "matmul", pH[5][0:1, :], onescol[:],
                         eTh[:, 512:1024], start=(nt == 0), stop=(nt == 15),
                         inc=pe)
                p.wait("vector", pe)
                p.op("vector", "reciprocal", invH[0:1, 0:512], pH[4][0:1, :])
                p.op("vector", "reciprocal", invH[0:1, 512:1024], pH[5][0:1, :],
                     inc=ve)
                p.wait("tensor", ve)
                p.op("tensor", "matmul", pH[0][0:64, :], ones1r[0:1, 0:64],
                     invH[0:1, 0:512], start=True, stop=True)
                p.op("tensor", "matmul", pH[1][0:64, :], ones1r[0:1, 0:64],
                     invH[0:1, 512:1024], start=True, stop=True, inc=pe)
                p.wait("vector", pe)
                p.op("vector", "tensor_copy", rinvH[:, 0:512], pH[0][0:64, :])
                p.op("vector", "tensor_copy", rinvH[:, 512:1024], pH[1][0:64, :],
                     inc=ve)
                p.wait("vector", ve)
                for half in range(2):
                    p.op("vector", "tensor_tensor",
                         ohT[pbq:pbq + 64,
                             chq * 1024 + half * 512: chq * 1024 + (half + 1) * 512],
                         pH[2 + half][0:64, :], rinvH[:, half * 512:(half + 1) * 512],
                         ALU.mult, inc=(ve if half == 1 else None))
            oh_ve = ve.n
            p.wait("gpsimd", st, b1_done)
            hv_ve = {}
            for tqt in range(8):
                p.wait("tensor", ve, oh_ve if tqt == 0 else hv_ve[tqt - 1])
                for b4 in range(4):
                    for half in range(2):
                        p.op("tensor", "matmul", pH[half][:],
                             ohT[:, b4 * 1024 + tqt * 128: b4 * 1024 + (tqt + 1) * 128],
                             woh[:, b4 * 1024 + half * 512:
                                 b4 * 1024 + (half + 1) * 512],
                             start=(b4 == 0), stop=(b4 == 3),
                             inc=(pe if (b4 == 3 and half == 1) else None))
                p.wait("vector", pe)
                if tqt >= 1:
                    p.wait("vector", ig)
                for half in range(2):
                    p.op("vector", "tensor_copy",
                         hvf[tqt % 2][:, half * 512:(half + 1) * 512], pH[half][:],
                         inc=(ve if half == 1 else None))
                hv_ve[tqt] = ve.n
                p.wait("gpsimd", ve, hv_ve[tqt])
                p.op("gpsimd", "indirect_dma_start",
                     out=d["btmp"],
                     out_offset=IOA(idxg[:, tqt:tqt + 1], 0),
                     in_=hvf[tqt % 2][:], in_offset=None,
                     compute_op=ALU.add, inc=ig, inck=16)
            heavy_ig = ig.n
    else:
        heavy_ig = ig.n

    with ExitStack() as bctx:
        bsb2 = lambda n_, s_, t_=F32: bctx.enter_context(nc.sbuf_tensor(n_, s_, t_))
        bt = [bsb2(f"B3_bt{i}", [128, 1024]) for i in range(2)]
        xo = [bsb2(f"B3_xo{i}", [128, 1024]) for i in range(2)]
        p.wait("sync", st, b1_done)
        p.wait("sync", ig, heavy_ig)
        p.wait("vector", ig, heavy_ig)
        st_f = {}
        ve_f = {}
        for i in range(NT):
            if i >= 2:
                p.wait("sync", ve, ve_f[i - 2])
            p.op("sync", "dma_start", out=bt[i % 2][:],
                 in_=d["btmp"][i * 128:(i + 1) * 128, :], inc=ld, inck=16)
            p.wait("vector", ld)
            p.op("vector", "tensor_reduce", kit.s1[:], bt[i % 2][:], AX.X, ALU.add,
                 inc=ve)
            if i >= 2:
                p.wait("vector", st, st_f[i - 2])
            emit_ln(p, kit, bt[i % 2][:], xo[i % 2][:], se, ve)
            ve_f[i] = ve.n
            p.wait("scalar", ve)
            p.op("scalar", "dma_start", out=d["xs2"][i * 128:(i + 1) * 128, :],
                 in_=xo[i % 2][:], inc=st, inck=16)
            st_f[i] = st.n
    return st.n

from concourse.bass_utils import run_bass_kernel_spmd

S, D_MODEL, NB = 8192, 1024, 8


def _mk_tables():
    half = np.arange(0, 64, 2, dtype=np.float64)
    inv_freq = 1.0 / 10000.0 ** (half / 64)
    t = np.arange(S, dtype=np.float64)
    freqs = np.concatenate([t[:, None] * inv_freq[None, :]] * 2, 1)
    base = (half + 0.4 * 64) / (1.4 * 64)
    power = (t - S // 2) / 256.0
    scale = np.concatenate([base[None, :] ** power[:, None]] * 2, 1)
    cos, sin = np.cos(freqs), np.sin(freqs)
    return (np.ascontiguousarray((cos * scale).astype(np.float32)),
            np.ascontiguousarray((sin * scale).astype(np.float32)),
            np.ascontiguousarray((cos / scale).astype(np.float32)),
            np.ascontiguousarray((sin / scale).astype(np.float32)))


def _prep_d_weights(w1, b1, w2, b2):
    L = w1.shape[0]
    w1e = np.zeros((L, 1152, 5632), np.float32)
    w2e = np.zeros((L, 2944, 1024), np.float32)
    for l in range(L):
        for k in range(22):
            a0, a1 = 128 * k, min(128 * (k + 1), 2730)
            n = a1 - a0
            w1e[l, 0:1024, 256 * k:256 * k + n] = w1[l][:, a0:a1]
            w1e[l, 1024, 256 * k:256 * k + n] = b1[l][a0:a1]
            w1e[l, 0:1024, 256 * k + 128:256 * k + 128 + n] = w1[l][:, 2730 + a0:2730 + a1]
            w1e[l, 1024, 256 * k + 128:256 * k + 128 + n] = b1[l][2730 + a0:2730 + a1]
            w2e[l, 128 * k:128 * k + n, :] = w2[l][a0:a1, :]
        w2e[l, 2816, :] = b2[l]
    return w1e, w2e


def _barrier(p, s, done):
    for eng in ENGINES:
        p.wait(eng, s["st"], done)


def _build_fused():
    nc = bass.Bass("TRN2", target_bir_lowering=False)
    d = {}

    def inp(name, shape):
        d[name] = nc.dram_tensor(name, shape, F32, kind="ExternalInput").ap()

    def intern(name, shape, dt=F32):
        d[name] = nc.dram_tensor(name, shape, dt, kind="Internal").ap()

    inp("x0", [S, 1024]); inp("Wq", [1024, 1024]); inp("bq", [1024])
    intern("xbf", [S, 1024], BF16); intern("vtok", [S, 1024], BF16)
    intern("ebuf", [64, 128, S], BF16)
    intern("xs1", [S, 1024])
    inp("gamma_light", [1024]); inp("gamma_heavy", [1024])
    inp("q_tok", [1024]); inp("kv_tok", [1024])
    inp("wq_light", [1024, 512]); inp("wkv_light", [1024, 1024])
    inp("wo_light", [512, 1024])
    inp("wq_heavy", [1024, 512]); inp("wkv_heavy", [1024, 1024])
    inp("wo_heavy", [512, 1024])
    intern("xnbf", [S, 1024], BF16); intern("btmp", [S, 1024])
    intern("qidxd", [S, 2], I32); intern("kidxd", [S, 2], I32)
    intern("xs2", [S, 1024])
    inp("wj", [1024, 1024]); inp("wjv", [1024, 1024]); inp("wo", [1024, 1024])
    intern("x2bf", [S, 1024], BF16); intern("vtokC", [S, 1024], BF16)
    intern("scd", [64, 128])
    intern("xs3", [S, 1024])
    inp("wqkv", [2, 1024, 1536]); inp("wo_l", [2, 512, 1024])
    inp("qs_t", [2, 512]); inp("ks_t", [2, 512])
    inp("lng", [2, 1024]); inp("lnb", [2, 1024])
    inp("w1e", [2, 1152, 5632]); inp("w2e", [2, 2944, 1024])
    inp("cq", [S, 64]); inp("sq", [S, 64]); inp("ck", [S, 64]); inp("sk", [S, 64])
    intern("zbf", [S, 1024], BF16); intern("z2", [S, 1024])
    intern("z3a", [S, 1024]); intern("z3b", [S, 1024])
    intern("hnbf", [S, 1152], BF16); intern("h2bf", [S, 2944], BF16)
    d["out4"] = nc.dram_tensor("out4", [S, 1024], F32, kind="ExternalOutput").ap()

    with ExitStack() as ctx:
        p = Prog(nc, ctx)
        s = make_sems(p)
        with ExitStack() as actx:
            done = build_stageA(nc, actx, p, d, s)
        _barrier(p, s, done)
        with ExitStack() as bctx:
            dB = dict(d); dB["x1"] = d["xs1"]
            done = build_stageB(nc, bctx, p, dB, s)
        _barrier(p, s, done)
        with ExitStack() as cctx:
            dC = dict(d); dC["x2"] = d["xs2"]
            done = build_stageC(nc, cctx, p, dC, s)
        _barrier(p, s, done)
        with ExitStack() as dctx:
            dD = dict(d); dD["x3"] = d["xs3"]
            done = build_stageD(nc, dctx, p, dD, s)
        p.wait("sync", s["st"], done)
        p.finalize()
    return nc


_NC_CACHE = {}


def kernel(L0_attn_inputs, L0_attn_Wq, L0_attn_bq, L0_ffn_gamma_light,
           L0_ffn_wq_light, L0_ffn_wkv_light, L0_ffn_wo_light,
           L0_ffn_q_routing_token, L0_ffn_kv_routing_token, L0_ffn_gamma_heavy,
           L0_ffn_wq_heavy, L0_ffn_wkv_heavy, L0_ffn_wo_heavy,
           L1_attn_wj_w, L1_attn_wjv_w, L1_attn_wo_w, L1_ffn_wqkv,
           L1_ffn_q_scale, L1_ffn_k_scale, L1_ffn_wo, L1_ffn_ln_g, L1_ffn_ln_b,
           L1_ffn_w1, L1_ffn_b1, L1_ffn_w2, L1_ffn_b2):
    f32 = lambda a: np.ascontiguousarray(np.asarray(a, dtype=np.float32))
    x0 = f32(L0_attn_inputs)
    w1e, w2e = _prep_d_weights(f32(L1_ffn_w1), f32(L1_ffn_b1), f32(L1_ffn_w2),
                               f32(L1_ffn_b2))
    qs_t = np.ascontiguousarray(np.tile(f32(L1_ffn_q_scale), (1, 8)))
    ks_t = np.ascontiguousarray(np.tile(f32(L1_ffn_k_scale), (1, 8)))
    cq, sq, ck, sk = _mk_tables()
    nc = _build_fused()
    common = {
        "Wq": f32(L0_attn_Wq), "bq": f32(L0_attn_bq),
        "gamma_light": f32(L0_ffn_gamma_light),
        "gamma_heavy": f32(L0_ffn_gamma_heavy),
        "q_tok": f32(L0_ffn_q_routing_token),
        "kv_tok": f32(L0_ffn_kv_routing_token),
        "wq_light": f32(L0_ffn_wq_light), "wkv_light": f32(L0_ffn_wkv_light),
        "wo_light": f32(L0_ffn_wo_light), "wq_heavy": f32(L0_ffn_wq_heavy),
        "wkv_heavy": f32(L0_ffn_wkv_heavy), "wo_heavy": f32(L0_ffn_wo_heavy),
        "wj": f32(L1_attn_wj_w), "wjv": f32(L1_attn_wjv_w),
        "wo": f32(L1_attn_wo_w), "wqkv": f32(L1_ffn_wqkv),
        "wo_l": f32(L1_ffn_wo), "qs_t": qs_t, "ks_t": ks_t,
        "lng": f32(L1_ffn_ln_g), "lnb": f32(L1_ffn_ln_b),
        "w1e": w1e, "w2e": w2e, "cq": cq, "sq": sq, "ck": ck, "sk": sk,
    }
    maps = [dict(common, x0=x0[b]) for b in range(NB)]
    res = run_bass_kernel_spmd(nc, maps, core_ids=list(range(NB)))
    return np.stack([res.results[b]["out4"] for b in range(NB)])
